# version 05
# baseline (speedup 1.0000x reference)
from contextlib import ExitStack

import numpy as np
import concourse.bass as bass
import concourse.mybir as mybir
from concourse.bass_utils import run_bass_kernel_spmd

F32 = mybir.dt.float32
BF16 = mybir.dt.bfloat16
AF = mybir.ActivationFunctionType
ALU = mybir.AluOpType
AX = mybir.AxisListType

NCORES = 8
NB = 2
SEQ = 8192
T = 2048
C = 1024
NCH = 8
NH = 16
DK = 64
TT = 512
NTT = T // TT
HID = 4096
NE = 8
MTT = 256
NMTT = T // MTT
EPS = 1e-5
GROUPS = [[0, 1, 2, 3], [4, 5, 6, 7]]
PAIRSWAP = [i + 1 if i % 2 == 0 else i - 1 for i in range(32)]

ENGS = ("tensor", "vector", "scalar", "gpsimd", "sync")
CARVED = {}

BUILD_PHASE = 3


class Plan:
    CHAINED = ("vector", "scalar")

    def __init__(self):
        self.ops = {e: [] for e in ENGS}
        self.cnt = {}
        self._lastwait = {e: {} for e in ENGS}
        self._lastop = {e: None for e in ENGS}

    def sem(self, name):
        if name not in self.cnt:
            self.cnt[name] = 0
        return name

    def emit(self, eng, fn, incs=()):
        incs = tuple(incs)
        if eng in self.CHAINED:
            prev = self._lastop[eng]
            if prev is not None:
                self.wait(eng, prev[0], prev[1])
            if not incs:
                incs = (("ch_" + eng, 1),)
        for s, a in incs:
            self.cnt[self.sem(s)] += a
        if eng in self.CHAINED:
            self._lastop[eng] = (incs[0][0], self.cnt[incs[0][0]])
        self.ops[eng].append(("op", fn, incs))

    def wait(self, eng, sem, val=None):
        self.sem(sem)
        v = self.cnt[sem] if val is None else val
        if v <= 0:
            return
        lw = self._lastwait[eng]
        if lw.get(sem, -1) >= v:
            return
        lw[sem] = v
        self.ops[eng].append(("wait", sem, v))

    def val(self, sem):
        return self.cnt[self.sem(sem)]


class Rot:
    def __init__(self, pl, name, depth, unit_p=1, unit_c=1):
        self.pl = pl
        self.depth = depth
        self.up = unit_p
        self.uc = unit_c
        self.sp = pl.sem(name + "_p")
        self.sc = pl.sem(name + "_c")
        self.np_ = 0
        self.nc_ = 0

    def begin_produce(self, eng):
        if self.np_ >= self.depth:
            self.pl.wait(eng, self.sc, (self.np_ - self.depth + 1) * self.uc)
        return self.np_ % self.depth

    def end_produce(self):
        self.np_ += 1
        return (self.sp, self.up)

    def begin_consume(self, eng):
        self.pl.wait(eng, self.sp, (self.nc_ + 1) * self.up)
        return self.nc_ % self.depth

    def end_consume(self):
        self.nc_ += 1
        return (self.sc, self.uc)

    def drain(self, eng):
        self.pl.wait(eng, self.sc, self.np_ * self.uc)


def build_program():
    nc = bass.Bass("TRN2", target_bir_lowering=False, detect_race_conditions=False)

    dx = nc.dram_tensor("x_fm", [C, T], F32, kind="ExternalInput").ap()
    dxh = nc.dram_tensor("x_halo", [C, 2], F32, kind="ExternalInput").ap()
    dwqkT = nc.dram_tensor("wqkT", [C, 2 * C], F32, kind="ExternalInput").ap()
    dbqk = nc.dram_tensor("bqk", [128, 16], F32, kind="ExternalInput").ap()
    dcos = nc.dram_tensor("ropecos", [128, NCH], F32, kind="ExternalInput").ap()
    dsin = nc.dram_tensor("ropesin", [128, NCH], F32, kind="ExternalInput").ap()
    dlepw = nc.dram_tensor("lepew", [128, NCH * 3], F32, kind="ExternalInput").ap()
    dlepb = nc.dram_tensor("lepeb", [128, NCH], F32, kind="ExternalInput").ap()
    dln1w = nc.dram_tensor("ln1w", [128, NCH], F32, kind="ExternalInput").ap()
    dln1b = nc.dram_tensor("ln1b", [128, NCH], F32, kind="ExternalInput").ap()
    dln2w = nc.dram_tensor("ln2w", [128, NCH], F32, kind="ExternalInput").ap()
    dln2b = nc.dram_tensor("ln2b", [128, NCH], F32, kind="ExternalInput").ap()
    dwqkv = nc.dram_tensor("wqkv", [C, 3 * C], F32, kind="ExternalInput").ap()
    dwproj = nc.dram_tensor("wproj", [C, C], F32, kind="ExternalInput").ap()
    dgatew = nc.dram_tensor("gatew", [C, NE], F32, kind="ExternalInput").ap()
    dgateb = nc.dram_tensor("gateb", [NE, 1], F32, kind="ExternalInput").ap()
    dw1 = nc.dram_tensor("w1", [NE, C, HID], F32, kind="ExternalInput").ap()
    dw2 = nc.dram_tensor("w2", [NE, HID, C], F32, kind="ExternalInput").ap()
    dident = nc.dram_tensor("ident", [128, 128], F32, kind="ExternalInput").ap()
    dhmask = nc.dram_tensor("hmask", [NCH, NH, 128], F32, kind="ExternalInput").ap()
    dcsts = nc.dram_tensor("csts", [128, 8], F32, kind="ExternalInput").ap()
    dout = nc.dram_tensor("out_fm", [C, T], F32, kind="ExternalOutput").ap()

    d_ckv_i = nc.dram_tensor("ckv_i", [64, C], F32).ap()
    d_ckv_o = nc.dram_tensor("ckv_o", [64, C], F32).ap()
    d_ckm_i = nc.dram_tensor("ckm_i", [128, NCH], F32).ap()
    d_ckm_o = nc.dram_tensor("ckm_o", [128, NCH], F32).ap()
    d_kg_i = nc.dram_tensor("kg_i", [C, T], BF16).ap()
    d_kg_o = nc.dram_tensor("kg_o", [4 * C, T], BF16).ap()
    d_vg_i = nc.dram_tensor("vg_i", [NH * T, DK + 1], BF16).ap()
    d_vg_o = nc.dram_tensor("vg_o", [4 * NH * T, DK + 1], BF16).ap()
    d_x2 = nc.dram_tensor("park_x2", [C, T], F32).ap()
    d_y0 = nc.dram_tensor("park_y0", [C, T], F32).ap()
    d_xn2 = nc.dram_tensor("park_xn2", [C, T], BF16).ap()

    remaining = nc.sbuf_bytes_remaining
    A0 = 0
    B0 = A0 + 65536
    C0 = B0 + 65536
    D0 = C0 + 33792
    E0 = D0 + 33344
    TOTAL = (remaining - 64) & ~31
    assert TOTAL >= E0 + 14400, (TOTAL, E0)
    arena_h = nc.alloc_sbuf_tensor("arena", [128, TOTAL], mybir.dt.uint8)
    arena_addr = nc.lookup_mloc(arena_h).addr

    def carve(name, shape, dtype, off):
        nbytes = int(np.prod(shape[1:])) * mybir.dt.size(dtype)
        assert off >= 0 and off + nbytes <= TOTAL, (name, off, nbytes, TOTAL)
        h = nc.virtual_sbuf_tensor(name, shape, dtype)
        nc._tensor(h.name, h.shape, h.dtype, type="SB")
        mloc = nc.lookup_mloc(h)
        mloc.allocated = True
        mloc.addr = arena_addr + off
        CARVED[name] = h.name
        return h

    X = carve("X", [128, NCH, T], F32, A0)
    RES = carve("RES", [128, NCH, T], F32, B0)
    XLN = carve("XLN", [128, NCH, T + 2], BF16, C0)

    eoff = [E0]

    def carve_e(name, shape, dtype):
        h = carve(name, shape, dtype, eoff[0])
        nbytes = int(np.prod(shape[1:])) * mybir.dt.size(dtype)
        eoff[0] = (eoff[0] + nbytes + 31) & ~31
        return h

    IDF = carve_e("IDF", [128, 128], F32)
    IDB = carve_e("IDB", [128, 128], BF16)
    ONESB = carve_e("ONESB", [128, 128], BF16)
    COS = carve_e("COS", [128, NCH], F32)
    SIN = carve_e("SIN", [128, NCH], F32)
    LEPW = carve_e("LEPW", [128, NCH * 3], F32)
    LEPB = carve_e("LEPB", [128, NCH], F32)
    BQK = carve_e("BQK", [128, 16], F32)
    LN1W = carve_e("LN1W", [128, NCH], F32)
    LN1B = carve_e("LN1B", [128, NCH], F32)
    LN2W = carve_e("LN2W", [128, NCH], F32)
    LN2B = carve_e("LN2B", [128, NCH], F32)
    GATEW = carve_e("GATEW", [128, NCH, NE], F32)
    GATEB = carve_e("GATEB", [NE, 1], F32)
    HMB = carve_e("HMB", [NH, NCH * 128], BF16)
    CSTS = carve_e("CSTS", [128, 8], F32)
    MUR = carve_e("MUR", [1, TT], F32)
    VARR = carve_e("VARR", [1, TT], F32)
    RSR = carve_e("RSR", [1, TT], F32)
    MURB = carve_e("MURB", [1, TT], BF16)
    RSRB = carve_e("RSRB", [1, TT], BF16)
    KVGB = carve_e("KVGB", [128, C // 2], BF16)
    KMG = carve_e("KMG", [128, NCH], F32)
    KMM = carve_e("KMM", [128, NCH, NH], BF16)
    assert eoff[0] <= TOTAL, (eoff[0], TOTAL)

    WQK_SB = [carve("WQK_SB%d" % i, [128, NCH, 128], BF16, D0 + i * 2048) for i in range(2)]
    SQT = [carve("SQT%d" % i, [128, TT], BF16, D0 + 4096 + i * 1024) for i in range(2)]
    XBT = [carve("XBT%d" % i, [128, TT], BF16, D0 + 6144 + i * 1024) for i in range(2)]
    TN1 = carve("TN1", [128, TT], F32, D0 + 8192)
    ELU = [carve("ELU%d" % i, [128, TT], BF16, D0 + 10240 + i * 1024) for i in range(2)]
    KCH = carve("KCH", [128, TT], BF16, D0 + 12288)
    SWP = carve("SWP", [128, TT], BF16, D0 + 13312)
    RT4 = carve("RT4", [128, TT], BF16, D0 + 14336)
    KRCH = carve("KRCH", [128, TT], BF16, D0 + 15360)
    KRT = carve("KRT", [128, 4, C], BF16, D0 + 16384)
    VT = carve("VT", [128, 4, C], BF16, D0 + 24576)
    KMEAN = carve("KMEAN", [128, NCH], F32, D0 + 32768)
    KMT = carve("KMT", [128, 1], F32, D0 + 32800)
    QF = carve("QF", [128, NCH, T], BF16, B0)
    KL = carve("KL", [128, NCH, T], BF16, B0 + 32768)
    AOUT = carve("AOUT", [128, NCH, T], BF16, B0 + 32768)
    VL = carve("VL", [128, NH, 16, DK + 1], BF16, C0)
    KAUG = [carve("KAUG%d" % i, [65, 4 * T], BF16, C0 + i * 16384) for i in range(2)]
    VH = [carve("VH%d" % i, [128, 64, DK + 1], BF16, D0 + i * 8352) for i in range(2)]
    QAUG = [carve("QAUG%d" % i, [65, T], BF16, D0 + 16704 + i * 4096) for i in range(2)]
    PT = [carve("PT%d" % i, [128, 512], BF16, D0 + 24896 + i * 1024) for i in range(2)]
    MROW = carve("MROW", [128, 1], F32, D0 + 26944)
    MTMP = carve("MTMP", [128, 1], F32, D0 + 26976)
    RL = carve("RL", [1, 512], F32, D0 + 27008)
    RLB = carve("RLB", [1, 512], BF16, D0 + 29088)
    OT16 = carve("OT16", [64, 512], BF16, D0 + 30144)
    XN1T = carve("XN1T", [128, NCH, TT], BF16, D0 + 16384)
    VCH = [carve("VCH%d" % i, [128, TT], BF16, D0 + 12288 + i * 1024) for i in range(2)]
    SROW = carve("SROW", [128, 2, 4 * T], BF16, A0)
    Y1T = carve("Y1T", [128, NCH, T], F32, A0)
    W1SB = carve("W1SB", [128, NCH, HID], BF16, A0)
    W2SB = carve("W2SB", [128, 32, C], BF16, B0)
    YACC = carve("YACC", [128, NCH, T], BF16, C0)
    HSB = carve("HSB", [128, 32, MTT], BF16, D0)
    XNT = carve("XNT", [128, NCH, MTT], BF16, D0 + 16384)
    MBSB = carve("MBSB", [128, MTT], BF16, D0 + 20480)
    T1M = carve("T1M", [128, MTT], F32, D0 + 21504)
    LG = carve("LG", [8, TT], F32, D0 + 22528)
    MASKTM = carve("MASKTM", [128, 16, NE], F32, D0 + 24576)
    MROWE = carve("MROWE", [1, MTT], BF16, D0 + 25088)
    XNF = carve("XNF", [128, TT], F32, D0 + 25600)
    XNB = carve("XNB", [128, TT], BF16, D0 + 27648)
    MX = carve("MX", [128, 1], F32, D0 + 28672)
    X2T = carve("X2T", [128, NCH, TT], F32, A0)
    OUTT = carve("OUTT", [128, NCH, TT], F32, A0 + 16384)
    HLO = carve("HLO", [128, 16], F32, D0 + 32832)
    HSC = carve("HSC", [128, 32], F32, D0 + 32896)
    KVS = carve("KVS", [64, C], F32, B0)
    QTI = carve("QTI", [128, NCH, TT], BF16, D0 + 16384)
    QRTI = carve("QRTI", [128, NCH, TT], BF16, D0 + 24576)
    ATB = carve("ATB", [128, TT], BF16, D0 + 12288)
    ZROW = carve("ZROW", [NH, TT], F32, D0 + 13312)
    ZROWB = carve("ZROWB", [NH, TT], BF16, D0 + 15360)

    PSA = nc.alloc_psum_tensor("PSA", [128, 512], F32)
    PSB = nc.alloc_psum_tensor("PSB", [128, 512], F32)
    PSC = nc.alloc_psum_tensor("PSC", [128, 512], F32)
    PSD = nc.alloc_psum_tensor("PSD", [128, 512], F32)
    PSE = nc.alloc_psum_tensor("PSE", [128, 1024], F32)
    PST = nc.alloc_psum_tensor("PST", [128, 2048], BF16)
    PSMM = [PSA, PSB]
    PSMM2 = [PSC, PSD]

    pl = Plan()
    r_mm = Rot(pl, "r_mm", 2)
    r_tp = Rot(pl, "r_tp", 2)
    r_wqk = Rot(pl, "r_wqk", 2, unit_p=16)
    r_sq = Rot(pl, "r_sq", 2)
    r_xb = Rot(pl, "r_xb", 2)
    r_elu = Rot(pl, "r_elu", 2)
    r_zb = Rot(pl, "r_zb", 1)
    r_mm2 = Rot(pl, "r_mm2", 2)
    r_vch = Rot(pl, "r_vch", 2)

    lns_prev = [None]

    def ln_stats(src_chunk, W, src_f32=True, src_ready=None):
        if src_ready is not None:
            pl.wait("scalar", src_ready)
            pl.wait("tensor", src_ready)

        r_mm.drain("tensor")
        r_mm2.drain("tensor")
        for c in range(NCH):
            if src_f32:
                slot = r_xb.begin_produce("scalar")
                pl.emit("scalar", lambda e, c=c, slot=slot: e.activation(
                    XBT[slot][:, 0:W], src_chunk(c), AF.Copy), [r_xb.end_produce()])
            slot = r_sq.begin_produce("scalar")
            pl.emit("scalar", lambda e, c=c, slot=slot: e.activation(
                SQT[slot][:, 0:W], src_chunk(c), AF.Square), [r_sq.end_produce()])
            if src_f32:
                sl = r_xb.begin_consume("tensor")
                incs_a = [r_xb.end_consume()]
                if c == NCH - 1:
                    incs_a.append(("lns_pe", 1))
                pl.emit("tensor", lambda e, sl=sl, c=c: e.matmul(
                    PSA[0:1, 0:W], ONESB[:, 0:1], XBT[sl][:, 0:W],
                    start=(c == 0), stop=(c == NCH - 1)), incs_a)
            else:
                pl.emit("tensor", lambda e, c=c: e.matmul(
                    PSA[0:1, 0:W], ONESB[:, 0:1], src_chunk(c),
                    start=(c == 0), stop=(c == NCH - 1)),
                    [("lns_pe", 1)] if c == NCH - 1 else [])
            sl = r_sq.begin_consume("tensor")
            incs = [r_sq.end_consume()]
            if c == NCH - 1:
                incs.append(("lns_pe", 1))
            pl.emit("tensor", lambda e, sl=sl, c=c: e.matmul(
                PSB[0:1, 0:W], ONESB[:, 0:1], SQT[sl][:, 0:W],
                start=(c == 0), stop=(c == NCH - 1)), incs)
        pl.wait("vector", "lns_pe")
        pl.emit("vector", lambda e: e.tensor_scalar_mul(MUR[:, 0:W], PSA[0:1, 0:W], 1.0 / C))
        pl.emit("vector", lambda e: e.tensor_scalar_mul(VARR[:, 0:W], PSB[0:1, 0:W], 1.0 / C))
        pl.emit("vector", lambda e: e.tensor_mul(RSR[:, 0:W], MUR[:, 0:W], MUR[:, 0:W]))
        pl.emit("vector", lambda e: e.scalar_tensor_tensor(
            VARR[:, 0:W], VARR[:, 0:W], float(EPS), RSR[:, 0:W], ALU.add, ALU.subtract),
                [("lns_dv", 1)])
        pl.wait("scalar", "lns_dv")
        pl.emit("scalar", lambda e: e.activation(VARR[:, 0:W], VARR[:, 0:W], AF.Sqrt))
        pl.emit("scalar", lambda e: e.activation(MURB[:, 0:W], MUR[:, 0:W], AF.Copy),
                [("lns_ac", 1)])
        pl.wait("vector", "lns_ac")
        pl.emit("vector", lambda e: e.reciprocal(RSR[:, 0:W], VARR[:, 0:W]), [("lns_dv2", 1)])
        pl.wait("scalar", "lns_dv2")
        pl.emit("scalar", lambda e: e.activation(RSRB[:, 0:W], RSR[:, 0:W], AF.Copy),
                [("lns_ac2", 1)])
        pl.wait("tensor", "lns_ac2")
        if lns_prev[0] is not None:
            pl.wait("tensor", lns_prev[0][0], lns_prev[0][1])
        pl.emit("tensor", lambda e: e.matmul(PSC[:, 0:W], ONESB[0:1, :], MURB[0:1, 0:W],
                                             start=True, stop=True))
        pl.emit("tensor", lambda e: e.matmul(PSD[:, 0:W], ONESB[0:1, :], RSRB[0:1, 0:W],
                                             start=True, stop=True), [("lns_bc", 1)])
        pl.wait("vector", "lns_bc")

    def lns_done(sem):
        lns_prev[0] = (sem, pl.val(sem))

    pl.emit("sync", lambda e: e.dma_start(
        X[:, :, :], dx.rearrange("(c p) t -> p c t", p=128)), [("s_x", 16)])
    pl.emit("sync", lambda e: e.dma_start(
        HLO[:, 0:16].rearrange("p (c k) -> p c k", k=2),
        dxh.rearrange("(c p) k -> p c k", p=128)), [("s_x", 16)])
    for sb, dr in [(IDF, dident), (COS, dcos), (SIN, dsin), (LEPW, dlepw), (LEPB, dlepb),
                   (BQK, dbqk), (LN1W, dln1w), (LN1B, dln1b), (LN2W, dln2w), (LN2B, dln2b),
                   (GATEB, dgateb), (CSTS, dcsts)]:
        pl.emit("sync", lambda e, sb=sb, dr=dr: e.dma_start(sb[:, :], dr[:, :]),
                [("s_const", 16)])
    pl.emit("sync", lambda e: e.dma_start(
        GATEW[:, :, :], dgatew.rearrange("(c p) e -> p c e", p=128)), [("s_const", 16)])
    pl.emit("gpsimd", lambda e: e.dma_start(IDB[:, :], dident[:, :]), [("s_constg", 16)])
    pl.emit("gpsimd", lambda e: e.dma_start(
        HMB[:, :].rearrange("h (c p) -> h c p", c=NCH),
        dhmask.rearrange("c h p -> h c p")), [("s_constg", 16)])
    pl.emit("gpsimd", lambda e: e.memset(ONESB[:, :], 1.0), [("s_mset", 1)])
    pl.emit("gpsimd", lambda e: e.memset(KMEAN[:, :], 0.0), [("s_mset", 1)])
    pl.emit("gpsimd", lambda e: e.memset(KMM[:, :, :], 0.0), [("s_mset", 1)])
    pl.emit("gpsimd", lambda e: e.memset(KVS[:, :], 0.0), [("s_mset", 1)])
    N_CONST = pl.val("s_const")

    for eng in ("tensor", "vector", "scalar"):
        pl.wait(eng, "s_const", N_CONST)
        pl.wait(eng, "s_constg", 32)
        pl.wait(eng, "s_mset", 4)
        pl.wait(eng, "s_x", 32)

    for c in range(NCH):
        pl.emit("scalar", lambda e, c=c: e.activation(
            SQT[0][:, 2 * c:2 * c + 2], HLO[:, 2 * c:2 * c + 2], AF.Copy))
        pl.emit("scalar", lambda e, c=c: e.activation(
            SQT[0][:, 16 + 2 * c:16 + 2 * c + 2], HLO[:, 2 * c:2 * c + 2], AF.Square),
            [("h_st", 1)] if c == NCH - 1 else [])
    pl.wait("tensor", "h_st")
    for c in range(NCH):
        pl.emit("tensor", lambda e, c=c: e.matmul(
            PSA[0:1, 0:2], ONESB[:, 0:1], SQT[0][:, 2 * c:2 * c + 2],
            start=(c == 0), stop=(c == NCH - 1)))
        pl.emit("tensor", lambda e, c=c: e.matmul(
            PSB[0:1, 0:2], ONESB[:, 0:1], SQT[0][:, 16 + 2 * c:16 + 2 * c + 2],
            start=(c == 0), stop=(c == NCH - 1)),
            [("h_mm", 1)] if c == NCH - 1 else [])
    pl.wait("vector", "h_mm")
    pl.emit("vector", lambda e: e.tensor_scalar_mul(MUR[:, 0:2], PSA[0:1, 0:2], 1.0 / C))
    pl.emit("vector", lambda e: e.tensor_scalar_mul(VARR[:, 0:2], PSB[0:1, 0:2], 1.0 / C))
    pl.emit("vector", lambda e: e.tensor_mul(RSR[:, 0:2], MUR[:, 0:2], MUR[:, 0:2]))
    pl.emit("vector", lambda e: e.scalar_tensor_tensor(
        VARR[:, 0:2], VARR[:, 0:2], float(EPS), RSR[:, 0:2], ALU.add, ALU.subtract),
        [("h_rw", 1)])
    pl.wait("scalar", "h_rw")
    pl.emit("scalar", lambda e: e.activation(VARR[:, 0:2], VARR[:, 0:2], AF.Sqrt))
    pl.emit("scalar", lambda e: e.activation(MURB[:, 0:2], MUR[:, 0:2], AF.Copy),
            [("h_sq", 1)])
    pl.wait("vector", "h_sq")
    pl.emit("vector", lambda e: e.reciprocal(RSR[:, 0:2], VARR[:, 0:2]), [("h_rc", 1)])
    pl.wait("scalar", "h_rc")
    pl.emit("scalar", lambda e: e.activation(RSRB[:, 0:2], RSR[:, 0:2], AF.Copy),
            [("h_sq2", 1)])
    pl.wait("tensor", "h_sq2")
    pl.emit("tensor", lambda e: e.matmul(PSC[:, 0:2], ONESB[0:1, :], MURB[0:1, 0:2],
                                         start=True, stop=True))
    pl.emit("tensor", lambda e: e.matmul(PSD[:, 0:2], ONESB[0:1, :], RSRB[0:1, 0:2],
                                         start=True, stop=True), [("h_bc", 1)])
    pl.wait("vector", "h_bc")
    for c in range(NCH):
        for k in range(2):
            col = 0 if k == 0 else T + 1
            pl.emit("vector", lambda e, c=c, k=k: e.tensor_sub(
                HSC[:, 4 * c + k:4 * c + k + 1], HLO[:, 2 * c + k:2 * c + k + 1],
                PSC[:, k:k + 1]))
            pl.emit("vector", lambda e, c=c, k=k, col=col: e.tensor_tensor(
                XLN[:, c, col:col + 1], HSC[:, 4 * c + k:4 * c + k + 1],
                PSD[:, k:k + 1], ALU.mult),
                [("h_done", 1)] if (c == NCH - 1 and k == 1) else [])
    lns_done("h_done")
    pl.wait("scalar", "h_mm")
    pl.wait("tensor", "h_rw")

    def emit_qk_chunk(j, t0, out_dve):
        wslot = r_wqk.begin_produce("gpsimd")
        pl.emit("gpsimd", lambda e, j=j, wslot=wslot: e.dma_start(
            WQK_SB[wslot][:, :, :],
            dwqkT[:, j * 128:(j + 1) * 128].rearrange("(c p) j -> p c j", p=128)),
            [r_wqk.end_produce()])
        wsl = r_wqk.begin_consume("tensor")
        pl.wait("tensor", "p1_xln")
        mslot = r_mm.begin_produce("tensor")
        for c in range(NCH):
            incs = []
            if c == NCH - 1:
                incs = [r_mm.end_produce(), r_wqk.end_consume()]
            pl.emit("tensor", lambda e, c=c, wsl=wsl, mslot=mslot, t0=t0: e.matmul(
                PSMM[mslot][:, 0:TT], WQK_SB[wsl][:, c, :],
                XLN[:, c, 1 + t0:1 + t0 + TT], start=(c == 0), stop=(c == NCH - 1)), incs)
        msl = r_mm.begin_consume("vector")
        ps = PSMM[msl]
        eslot = r_elu.begin_produce("vector")
        pl.emit("vector", lambda e, ps=ps, j=j, eslot=eslot: e.tensor_scalar(
            ELU[eslot][:, 0:TT], ps[:, 0:TT], BQK[:, j:j + 1], 0.0, ALU.add, ALU.min),
            [r_elu.end_produce()])
        el = r_elu.begin_consume("scalar")
        pl.emit("scalar", lambda e, el=el: e.activation(
            ELU[el][:, 0:TT], ELU[el][:, 0:TT], AF.Exp), [r_elu.end_consume()])
        pl.wait("vector", "r_elu_c")
        pl.emit("vector", lambda e, ps=ps, j=j: e.tensor_scalar(
            TN1[:, 0:TT], ps[:, 0:TT], BQK[:, j:j + 1], 0.0, ALU.add, ALU.max))
        out_dve(el, ps)

    for tt in range(NTT):
        t0 = tt * TT
        ln_stats(lambda c, t0=t0: X[:, c, t0:t0 + TT], TT)
        for c in range(NCH):
            pl.emit("vector", lambda e, c=c, t0=t0: e.tensor_sub(
                TN1[:, 0:TT], X[:, c, t0:t0 + TT], PSC[:, 0:TT]))
            pl.emit("vector", lambda e, c=c, t0=t0: e.tensor_tensor(
                XLN[:, c, 1 + t0:1 + t0 + TT], TN1[:, 0:TT], PSD[:, 0:TT], ALU.mult),
                [("p1_xln", 1)])
        lns_done("p1_xln")

        for jj in range(8):
            j = 8 + jj

            def k_out(el, ps, jj=jj, t0=t0):
                pl.emit("vector", lambda e, el=el: e.tensor_add(
                    KCH[:, 0:TT], TN1[:, 0:TT], ELU[el][:, 0:TT]), [r_mm.end_consume()])
                pl.emit("vector", lambda e: e.reduce_sum(KMT[:, :], KCH[:, 0:TT], axis=AX.X))
                pl.emit("vector", lambda e, jj=jj: e.tensor_add(
                    KMEAN[:, jj:jj + 1], KMEAN[:, jj:jj + 1], KMT[:, :]), [("p1_km", 1)])
                pl.emit("vector", lambda e: e.stream_shuffle(SWP[:, 0:TT], KCH[:, 0:TT],
                                                             PAIRSWAP))
                pl.emit("vector", lambda e, jj=jj: e.tensor_scalar_mul(
                    RT4[:, 0:TT], KCH[:, 0:TT], COS[:, jj:jj + 1]))
                pl.emit("vector", lambda e, jj=jj: e.scalar_tensor_tensor(
                    KRCH[:, 0:TT], SWP[:, 0:TT], SIN[:, jj:jj + 1], RT4[:, 0:TT],
                    ALU.mult, ALU.add), [("p1_kr", 1)])
                for s in range(4):
                    tsl = r_tp.begin_produce("tensor")
                    pl.wait("tensor", "p1_kr")
                    pl.emit("tensor", lambda e, s=s, tsl=tsl: e.transpose(
                        PST[:, tsl * 1024:tsl * 1024 + 128],
                        KRCH[:, s * 128:(s + 1) * 128], IDB[:, :]), [r_tp.end_produce()])
                    tc = r_tp.begin_consume("scalar")
                    pl.emit("scalar", lambda e, s=s, jj=jj, tc=tc: e.activation(
                        KRT[:, s, jj * 128:(jj + 1) * 128],
                        PST[:, tc * 1024:tc * 1024 + 128], AF.Copy),
                        [r_tp.end_consume()])
                    tsl = r_tp.begin_produce("tensor")
                    pl.emit("tensor", lambda e, s=s, jj=jj, t0=t0, tsl=tsl: e.transpose(
                        PST[:, tsl * 1024:tsl * 1024 + 128],
                        XLN[:, jj, 1 + t0 + s * 128:1 + t0 + (s + 1) * 128], IDB[:, :]),
                        [r_tp.end_produce()])
                    tc = r_tp.begin_consume("scalar")
                    pl.emit("scalar", lambda e, s=s, jj=jj, tc=tc: e.activation(
                        VT[:, s, jj * 128:(jj + 1) * 128],
                        PST[:, tc * 1024:tc * 1024 + 128], AF.Copy),
                        [r_tp.end_consume()])

            emit_qk_chunk(j, t0, k_out)

        pl.wait("tensor", "h_done")
        pl.wait("tensor", "r_tp_c")
        pl.wait("tensor", "p1_kva")
        for h in range(NH):
            for s in range(4):
                incs = [("p1_kv", 1)] if (s == 3 and h == NH - 1) else []
                pl.emit("tensor", lambda e, h=h, s=s: e.matmul(
                    PSE[0:64, h * 64:(h + 1) * 64],
                    KRT[:, s, h * 64:(h + 1) * 64], VT[:, s, h * 64:(h + 1) * 64],
                    start=(s == 0), stop=(s == 3)), incs)
        pl.wait("vector", "p1_kv")
        pl.wait("vector", "s_mset", 4)
        pl.emit("vector", lambda e: e.scalar_tensor_tensor(
            KVS[:, :], PSE[0:64, 0:1024], 1.0 / SEQ, KVS[:, :], ALU.mult, ALU.add),
            [("p1_kva", 1)])

    pl.wait("gpsimd", "p1_kva")
    pl.emit("gpsimd", lambda e: e.dma_start(d_ckv_i[:, :], KVS[:, :]), [("s_ccst", 16)])
    pl.wait("gpsimd", "p1_km", 32)
    pl.emit("gpsimd", lambda e: e.dma_start(d_ckm_i[:, :], KMEAN[:, :]), [("s_ccst", 16)])
    pl.wait("gpsimd", "s_ccst")
    pl.emit("gpsimd", lambda e: e.collective_compute(
        "AllReduce", ALU.add, GROUPS, [d_ckv_i[:, :]], [d_ckv_o[:, :]]), [("s_cc", 1)])
    pl.emit("gpsimd", lambda e: e.collective_compute(
        "AllReduce", ALU.add, GROUPS, [d_ckm_i[:, :]], [d_ckm_o[:, :]]), [("s_cc", 1)])
    pl.wait("gpsimd", "s_cc")
    ckv_he = d_ckv_o.rearrange("d (hh two e) -> d hh two e", two=2, e=64)
    pl.emit("gpsimd", lambda e: e.dma_start(
        KVGB[0:64, :].rearrange("p (hh e) -> p hh e", e=64), ckv_he[:, :, 0, :]),
        [("s_ccl", 16)])
    pl.emit("gpsimd", lambda e: e.dma_start(
        KVGB[64:128, :].rearrange("p (hh e) -> p hh e", e=64), ckv_he[:, :, 1, :]),
        [("s_ccl", 16)])
    pl.emit("gpsimd", lambda e: e.dma_start(KMG[:, :], d_ckm_o[:, :]), [("s_ccl", 16)])

    pl.wait("vector", "s_ccst", 16)
    for tt in range(NTT):
        t0 = tt * TT
        for c in range(NCH):
            pl.emit("vector", lambda e, c=c, t0=t0: e.tensor_scalar_mul(
                TN1[:, 0:TT], XLN[:, c, t0:t0 + TT], LEPW[:, 3 * c:3 * c + 1]))
            pl.emit("vector", lambda e, c=c, t0=t0: e.scalar_tensor_tensor(
                TN1[:, 0:TT], XLN[:, c, 1 + t0:1 + t0 + TT], LEPW[:, 3 * c + 1:3 * c + 2],
                TN1[:, 0:TT], ALU.mult, ALU.add))
            pl.emit("vector", lambda e, c=c, t0=t0: e.scalar_tensor_tensor(
                RES[:, c, t0:t0 + TT], XLN[:, c, 2 + t0:2 + t0 + TT],
                LEPW[:, 3 * c + 2:3 * c + 3], TN1[:, 0:TT], ALU.mult, ALU.add),
                [("p1_conv", 1)])

    pl.wait("scalar", "s_ccl", 48)
    for c in range(NCH):
        pl.emit("scalar", lambda e, c=c: e.activation(
            KMM[0:64, c, 2 * c:2 * c + 1], KMG[0:64, c:c + 1], AF.Copy, scale=CSTS[0:64, 0:1]))
        pl.emit("scalar", lambda e, c=c: e.activation(
            KMM[64:128, c, 2 * c + 1:2 * c + 2], KMG[64:128, c:c + 1], AF.Copy,
            scale=CSTS[64:128, 0:1]), [("p1_kmm", 1)] if c == NCH - 1 else [])

    for tt in range(NTT):
        t0 = tt * TT
        for j in range(8):
            def q_out(el, ps, j=j):
                pl.emit("vector", lambda e, el=el, j=j: e.tensor_add(
                    QTI[:, j, 0:TT], TN1[:, 0:TT], ELU[el][:, 0:TT]),
                    [("p2_q", 1)])
                pl.emit("vector", lambda e: e.engine_nop(), [r_mm.end_consume()])

            pl.wait("vector", "s_ccst", 16)
            emit_qk_chunk(j, t0, q_out)
        pl.wait("tensor", "p2_q")
        pl.wait("tensor", "p1_kmm")
        pl.wait("tensor", "p1_kva")
        pl.wait("tensor", "p2_zcons")
        for c in range(NCH):
            pl.emit("tensor", lambda e, c=c: e.matmul(
                PSE[0:NH, 0:TT], KMM[:, c, :], QTI[:, c, 0:TT],
                start=(c == 0), stop=(c == NCH - 1)),
                [("p2_zpe", 1)] if c == NCH - 1 else [])
        pl.wait("vector", "p2_zpe")
        pl.emit("vector", lambda e: e.tensor_scalar_add(ZROW[:, :], PSE[0:NH, 0:TT], 1e-6),
                [("p2_zcons", 1)])
        pl.emit("vector", lambda e: e.reciprocal(ZROW[:, :], ZROW[:, :]))
        pl.emit("vector", lambda e: e.tensor_copy(ZROWB[:, :], ZROW[:, :]), [("p2_z", 1)])
        pl.wait("tensor", "p2_z")
        pl.wait("tensor", "s_ccl", 48)
        pl.wait("vector", "p1_conv")
        for c in range(NCH):
            r_zb.begin_produce("tensor")
            pl.emit("tensor", lambda e, c=c: e.matmul(
                PSE[:, 512:512 + TT], HMB[:, c * 128:(c + 1) * 128], ZROWB[:, 0:TT],
                start=True, stop=True), [r_zb.end_produce()])
            pl.emit("vector", lambda e, c=c: e.stream_shuffle(
                SWP[:, 0:TT], QTI[:, c, 0:TT], PAIRSWAP))
            pl.emit("vector", lambda e, c=c: e.tensor_scalar_mul(
                RT4[:, 0:TT], QTI[:, c, 0:TT], COS[:, c:c + 1]))
            pl.emit("vector", lambda e, c=c: e.scalar_tensor_tensor(
                TN1[:, 0:TT], SWP[:, 0:TT], SIN[:, c:c + 1], RT4[:, 0:TT],
                ALU.mult, ALU.add))
            r_zb.begin_consume("vector")
            pl.emit("vector", lambda e, c=c: e.tensor_tensor(
                QRTI[:, c, 0:TT], TN1[:, 0:TT], PSE[:, 512:512 + TT], ALU.mult),
                [r_zb.end_consume()])
            pl.wait("tensor", "r_zb_c")
            mslot = r_mm.begin_produce("tensor")
            ps = PSMM[mslot]
            pl.emit("tensor", lambda e, c=c, ps=ps: e.matmul(
                ps[0:64, 0:TT], KVGB[0:64, c * 64:(c + 1) * 64],
                QRTI[0:64, c, 0:TT], start=True, stop=True))
            pl.emit("tensor", lambda e, c=c, ps=ps: e.matmul(
                ps[64:128, 0:TT], KVGB[64:128, c * 64:(c + 1) * 64],
                QRTI[64:128, c, 0:TT], start=True, stop=True), [r_mm.end_produce()])
            msl = r_mm.begin_consume("vector")
            pl.emit("vector", lambda e, c=c, t0=t0, msl=msl: e.scalar_tensor_tensor(
                RES[:, c, t0:t0 + TT], PSMM[msl][:, 0:TT], LEPB[:, c:c + 1],
                RES[:, c, t0:t0 + TT], ALU.add, ALU.add),
                [r_mm.end_consume()])
        pl.emit("vector", lambda e: e.engine_nop(), [("p2_res", 1)])
        ln_stats(lambda c, t0=t0: RES[:, c, t0:t0 + TT], TT, src_ready="p2_res")
        for c in range(NCH):
            pl.emit("vector", lambda e, c=c, t0=t0: e.tensor_sub(
                TN1[:, 0:TT], RES[:, c, t0:t0 + TT], PSC[:, 0:TT]))
            pl.emit("vector", lambda e: e.tensor_tensor(
                TN1[:, 0:TT], TN1[:, 0:TT], PSD[:, 0:TT], ALU.mult), [("p2_pn", 1)])
            pl.emit("vector", lambda e, c=c, t0=t0: e.tensor_add(
                X[:, c, t0:t0 + TT], X[:, c, t0:t0 + TT], TN1[:, 0:TT]), [("p2_x2", 1)])
        lns_done("p2_pn")
        pl.wait("sync", "p2_x2")
        pl.emit("sync", lambda e, t0=t0: e.dma_start(
            d_x2[:, t0:t0 + TT].rearrange("(c p) t -> p c t", p=128),
            X[:, :, t0:t0 + TT]), [("s_x2p", 16)])
        if BUILD_PHASE >= 2:
            ln_stats(lambda c, t0=t0: X[:, c, t0:t0 + TT], TT, src_ready="p2_x2")
            pl.wait("vector", "s_x2p")
            for c in range(NCH):
                pl.emit("vector", lambda e, c=c, t0=t0: e.tensor_sub(
                    TN1[:, 0:TT], X[:, c, t0:t0 + TT], PSC[:, 0:TT]))
                pl.emit("vector", lambda e, c=c, t0=t0: e.tensor_tensor(
                    X[:, c, t0:t0 + TT], TN1[:, 0:TT], PSD[:, 0:TT], ALU.mult),
                    [("p2_y0", 1)])
            lns_done("p2_y0")
            pl.wait("sync", "p2_y0")
            pl.emit("sync", lambda e, t0=t0: e.dma_start(
                d_y0[:, t0:t0 + TT].rearrange("(c p) t -> p c t", p=128),
                X[:, :, t0:t0 + TT]), [("s_y0p", 16)])

    if BUILD_PHASE == 1:
        pl.wait("sync", "p2_x2")
        pl.emit("sync", lambda e: e.dma_start(
            dout.rearrange("(c p) t -> p c t", p=128), X[:, :, :]), [("s_out", 16)])
        pl.wait("sync", "s_out")


    if BUILD_PHASE >= 2:
        for tt in range(NTT):
            t0 = tt * TT
            ln_stats(lambda c, t0=t0: X[:, c, t0:t0 + TT], TT, src_ready="p2_y0")
            for c in range(NCH):
                pl.wait("vector", "p2_xn1")
                pl.emit("vector", lambda e, c=c, t0=t0: e.tensor_sub(
                    TN1[:, 0:TT], X[:, c, t0:t0 + TT], PSC[:, 0:TT]))
                pl.emit("vector", lambda e: e.tensor_tensor(
                    TN1[:, 0:TT], TN1[:, 0:TT], PSD[:, 0:TT], ALU.mult), [("p2_ln1", 1)])
                pl.wait("scalar", "p2_ln1")
                pl.emit("scalar", lambda e, c=c: e.activation(
                    XN1T[:, c, 0:TT], TN1[:, 0:TT], AF.Identity,
                    scale=LN1W[:, c:c + 1], bias=LN1B[:, c:c + 1]), [("p2_xn1", 1)])
            lns_done("p2_ln1")
            for jc in range(24):
                wslot = r_wqk.begin_produce("gpsimd")
                pl.emit("gpsimd", lambda e, jc=jc, wslot=wslot: e.dma_start(
                    WQK_SB[wslot][:, :, :],
                    dwqkv[:, jc * 128:(jc + 1) * 128].rearrange("(c p) j -> p c j", p=128)),
                    [r_wqk.end_produce()])
                wsl = r_wqk.begin_consume("tensor")
                pl.wait("tensor", "p2_xn1")
                mslot = r_mm.begin_produce("tensor")
                for c in range(NCH):
                    incs = []
                    if c == NCH - 1:
                        incs = [r_mm.end_produce(), r_wqk.end_consume()]
                    pl.emit("tensor", lambda e, c=c, wsl=wsl, mslot=mslot: e.matmul(
                        PSMM[mslot][:, 0:TT], WQK_SB[wsl][:, c, :],
                        XN1T[:, c, 0:TT], start=(c == 0), stop=(c == NCH - 1)), incs)
                msl = r_mm.begin_consume("scalar")
                ps = PSMM[msl]
                if jc < 8:
                    pl.emit("scalar", lambda e, jc=jc, t0=t0, ps=ps: e.activation(
                        QF[:, jc, t0:t0 + TT], ps[:, 0:TT], AF.Copy,
                        scale=CSTS[:, 2:3]), [r_mm.end_consume()])
                elif jc < 16:
                    pl.emit("scalar", lambda e, jc=jc, t0=t0, ps=ps: e.activation(
                        KL[:, jc - 8, t0:t0 + TT], ps[:, 0:TT], AF.Copy),
                        [r_mm.end_consume()])
                else:
                    vslot = r_vch.begin_produce("scalar")
                    pl.emit("scalar", lambda e, ps=ps, vslot=vslot: e.activation(
                        VCH[vslot][:, 0:TT], ps[:, 0:TT], AF.Copy),
                        [r_mm.end_consume()])
                    r_vch.np_ += 1
                    vch_ready = pl.val("r_mm_c")
                    vsl = r_vch.nc_ % 2
                    r_vch.nc_ += 1
                    pl.wait("tensor", "r_mm_c", vch_ready)
                    h0 = 2 * (jc - 16)
                    for s in range(4):
                        ts_ = tt * 4 + s
                        tsl = r_tp.begin_produce("tensor")
                        incs = [r_tp.end_produce()]
                        if s == 3:
                            incs.append(("r_vch_c", 1))
                        pl.emit("tensor", lambda e, s=s, vsl=vsl, tsl=tsl: e.transpose(
                            PST[:, tsl * 1024:tsl * 1024 + 128],
                            VCH[vsl][:, s * 128:(s + 1) * 128], IDB[:, :]), incs)
                        tc = r_tp.begin_consume("scalar")
                        pl.emit("scalar", lambda e, h0=h0, ts_=ts_, tc=tc: e.activation(
                            VL[:, h0, ts_, 0:64],
                            PST[:, tc * 1024:tc * 1024 + 64], AF.Copy))
                        pl.emit("scalar", lambda e, h0=h0, ts_=ts_, tc=tc: e.activation(
                            VL[:, h0 + 1, ts_, 0:64],
                            PST[:, tc * 1024 + 64:tc * 1024 + 128], AF.Copy),
                            [r_tp.end_consume()])
        QF_DONE = pl.val("r_mm_c")
        VL_TPC = pl.val("r_tp_c")

        pl.wait("gpsimd", "p2_pn")
        pl.wait("gpsimd", "r_tp_c", VL_TPC)
        for h in range(NH):
            pl.emit("gpsimd", lambda e, h=h: e.memset(VL[:, h, :, 64:65], 1.0),
                    [("s_mset", 1)])
        N_MSET_VL = pl.val("s_mset")

        pl.wait("sync", "r_mm_c", QF_DONE)
        pl.emit("sync", lambda e: e.dma_start(
            d_kg_i.rearrange("(c p) t -> p c t", p=128), KL[:, :, :]), [("s_kg", 16)])
        pl.wait("sync", "r_tp_c", VL_TPC)
        pl.wait("sync", "s_mset", N_MSET_VL)
        pl.emit("sync", lambda e: e.dma_start(
            d_vg_i.rearrange("(h ts p) e -> p h ts e", p=128, ts=16), VL[:, :, :, :]),
            [("s_vg", 16)])
        pl.wait("gpsimd", "s_vg")
        pl.emit("gpsimd", lambda e: e.memset(KAUG[0][64:65, :], 1.0), [("s_mset", 1)])
        pl.emit("gpsimd", lambda e: e.memset(KAUG[1][64:65, :], 1.0), [("s_mset", 1)])
        N_MSET2 = pl.val("s_mset")
        pl.wait("gpsimd", "s_kg")
        pl.wait("gpsimd", "s_vg")
        pl.emit("gpsimd", lambda e: e.collective_compute(
            "AllGather", ALU.bypass, GROUPS, [d_kg_i[:, :]], [d_kg_o[:, :]]),
            [("s_cc2", 1)])
        pl.emit("gpsimd", lambda e: e.collective_compute(
            "AllGather", ALU.bypass, GROUPS, [d_vg_i[:, :]], [d_vg_o[:, :]]),
            [("s_cc2", 1)])

        st_mark = {}
        av_mark = {}
        hd_mark = {}

        def emit_load(h):
            buf = h % 2
            pl.wait("sync", "s_cc2", 2)
            if h >= 2:
                pl.wait("sync", "r_mm2_p", st_mark[h - 2])
                pl.wait("sync", "b_av", av_mark[h - 2])
            for r in range(4):
                pl.emit("sync", lambda e, h=h, r=r, buf=buf: e.dma_start(
                    KAUG[buf][0:64, r * T:(r + 1) * T],
                    d_kg_o[r * C + 64 * h: r * C + 64 * h + 64, :]), [("s_hd", 16)])
            for r in range(4):
                pl.emit("sync", lambda e, h=h, r=r, buf=buf: e.dma_start(
                    VH[buf][:, r * 16:(r + 1) * 16, :],
                    d_vg_o[(r * 16 + h) * T:(r * 16 + h + 1) * T, :]
                    .rearrange("(j p) e -> p j e", p=128)), [("s_hd", 16)])
            hd_mark[h] = pl.val("s_hd")

        qtg = [0]

        def gen_A(h):
            buf = h % 2
            cQ = h // 2
            rb = (h % 2) * 64
            pl.wait("vector", "r_mm_c", QF_DONE)
            if h >= 2:
                pl.wait("vector", "r_mm2_p", st_mark[h - 2])
            pl.emit("vector", lambda e, buf=buf, cQ=cQ, rb=rb: e.tensor_copy(
                QAUG[buf][0:64, :], QF[rb:rb + 64, cQ, :]), [("a_qb", 1)])
            pl.wait("tensor", "a_qb")
            pl.wait("tensor", "s_hd", hd_mark[h] - 64)
            pl.wait("tensor", "s_mset", N_MSET2)
            if h == 0:
                pl.wait("scalar", "s_y0p")
            for qt in range(16):
                g = qtg[0]
                slot = g % 2
                for kc in range(16):
                    mslot = r_mm.begin_produce("tensor")
                    pl.emit("tensor", lambda e, buf=buf, qt=qt, kc=kc, mslot=mslot: e.matmul(
                        PSMM[mslot][:, 0:TT], QAUG[buf][0:64, qt * 128:(qt + 1) * 128],
                        KAUG[buf][0:64, kc * TT:(kc + 1) * TT],
                        start=True, stop=True), [r_mm.end_produce()])
                    ev_eng = "vector" if kc % 2 == 0 else "scalar"
                    msl = r_mm.begin_consume(ev_eng)
                    if g >= 2 and kc <= 1:
                        pl.wait(ev_eng, "a_rm", g - 1)
                    if ev_eng == "vector":
                        pl.emit("vector", lambda e, msl=msl, slot=slot, kc=kc: e.tensor_copy(
                            SROW[:, slot, kc * TT:(kc + 1) * TT],
                            PSMM[msl][:, 0:TT]), [r_mm.end_consume()])
                    else:
                        pl.emit("scalar", lambda e, msl=msl, slot=slot, kc=kc: e.activation(
                            SROW[:, slot, kc * TT:(kc + 1) * TT],
                            PSMM[msl][:, 0:TT], AF.Copy), [r_mm.end_consume()])
                    yield
                pl.wait("vector", "r_mm_c")
                pl.wait("vector", "a_mt")
                pl.emit("vector", lambda e, slot=slot: e.reduce_max(
                    MROW[:, :], SROW[:, slot, :], axis=AX.X), [("a_rm", 1)])
                pl.wait("tensor", "a_rm")
                pl.wait("tensor", "a_qrow")
                pl.wait("tensor", "b_fin")
                pl.emit("tensor", lambda e: e.matmul(
                    PSE[0:1, 512:640], MROW[:, 0:1], IDF[:, :],
                    start=True, stop=True), [("a_mt", 1)])
                pl.wait("scalar", "a_mt")
                pl.emit("scalar", lambda e, buf=buf, qt=qt: e.activation(
                    QAUG[buf][64:65, qt * 128:(qt + 1) * 128], PSE[0:1, 512:640],
                    AF.Copy, scale=CSTS[0:1, 1:2]), [("a_qrow", 1)])
                qtg[0] += 1

        def gen_B(h):
            buf = h % 2
            cQ = h // 2
            rb = (h % 2) * 64
            pl.wait("tensor", "a_qrow", 16 * (h + 1))
            pl.wait("tensor", "s_hd", hd_mark[h])
            for qt4 in range(4):
                pl.wait("tensor", "b_fin")
                for kc in range(64):
                    m2 = r_mm2.begin_produce("tensor")
                    pl.emit("tensor", lambda e, buf=buf, qt4=qt4, kc=kc, m2=m2: e.matmul(
                        PSMM2[m2][:, 0:TT], KAUG[buf][:, kc * 128:(kc + 1) * 128],
                        QAUG[buf][:, qt4 * TT:(qt4 + 1) * TT],
                        start=True, stop=True), [r_mm2.end_produce()])
                    m2c = r_mm2.begin_consume("scalar")
                    if pl.val("b_av") >= 1:
                        pl.wait("scalar", "b_av", pl.val("b_av") - 1)
                    pslot = pl.val("r_mm2_c") % 2
                    pl.emit("scalar", lambda e, m2c=m2c, pslot=pslot: e.activation(
                        PT[pslot][:, 0:TT], PSMM2[m2c][:, 0:TT], AF.Exp),
                        [r_mm2.end_consume()])
                    pl.wait("tensor", "r_mm2_c")
                    pl.emit("tensor", lambda e, buf=buf, kc=kc, pslot=pslot: e.matmul(
                        PSE[0:65, 0:TT], VH[buf][:, kc, :], PT[pslot][:, 0:TT],
                        start=(kc == 0), stop=(kc == 63)), [("b_av", 1)])
                    yield
                pl.wait("vector", "b_av")
                pl.emit("vector", lambda e: e.reciprocal(RL[:, :], PSE[64:65, 0:TT]),
                        [("b_rl", 1)])
                pl.wait("scalar", "b_rl")
                pl.emit("scalar", lambda e: e.activation(RLB[:, :], RL[:, :], AF.Copy),
                        [("b_rlb", 1)])
                pl.emit("scalar", lambda e: e.activation(OT16[:, :], PSE[0:64, 0:TT],
                                                         AF.Copy), [("b_ot", 1)])
                pl.wait("tensor", "b_rlb")
                pl.wait("tensor", "a_qrow")
                pl.emit("tensor", lambda e: e.matmul(
                    PSE[0:64, 512:1024], ONESB[0:1, 0:64], RLB[0:1, :],
                    start=True, stop=True), [("b_lb", 1)])
                pl.wait("vector", "b_lb")
                pl.wait("vector", "b_ot")
                pl.emit("vector", lambda e, rb=rb, cQ=cQ, qt4=qt4: e.tensor_tensor(
                    AOUT[rb:rb + 64, cQ, qt4 * TT:(qt4 + 1) * TT], OT16[:, :],
                    PSE[0:64, 512:1024], ALU.mult), [("b_fin", 1)])
            st_mark[h] = pl.val("r_mm2_p")
            av_mark[h] = pl.val("b_av")

        pl.wait("tensor", "p2_ln1")

        def drive(ga, gb):
            da = ga is None
            db = gb is None
            while not (da and db):
                if not da:
                    try:
                        next(ga)
                    except StopIteration:
                        da = True
                if not db:
                    try:
                        next(gb)
                    except StopIteration:
                        db = True

        for h in range(NH):
            emit_load(h)
            drive(gen_A(h), gen_B(h - 1) if h >= 1 else None)
        drive(None, gen_B(NH - 1))

        AOUT_DONE = pl.val("b_fin")
        A_RM_ALL = pl.val("a_rm")
        pl.wait("gpsimd", "b_av", av_mark[NH - 1])
        pl.wait("sync", "a_rm", A_RM_ALL)
        pl.emit("sync", lambda e: e.dma_start(
            Y1T[:, :, :], d_y0.rearrange("(c p) t -> p c t", p=128)), [("s_y0r", 16)])
        for tt in range(NTT):
            t0 = tt * TT
            for oc in range(NCH):
                wslot = r_wqk.begin_produce("gpsimd")
                pl.emit("gpsimd", lambda e, oc=oc, wslot=wslot: e.dma_start(
                    WQK_SB[wslot][:, :, :],
                    dwproj[:, oc * 128:(oc + 1) * 128].rearrange("(c p) j -> p c j", p=128)),
                    [r_wqk.end_produce()])
                wsl = r_wqk.begin_consume("tensor")
                pl.wait("tensor", "b_fin", AOUT_DONE)
                mslot = r_mm.begin_produce("tensor")
                for c in range(NCH):
                    incs = []
                    if c == NCH - 1:
                        incs = [r_mm.end_produce(), r_wqk.end_consume()]
                    pl.emit("tensor", lambda e, c=c, wsl=wsl, mslot=mslot, t0=t0: e.matmul(
                        PSMM[mslot][:, 0:TT], WQK_SB[wsl][:, c, :],
                        AOUT[:, c, t0:t0 + TT], start=(c == 0), stop=(c == NCH - 1)), incs)
                msl = r_mm.begin_consume("vector")
                pl.wait("vector", "s_y0r", 16)
                pl.emit("vector", lambda e, oc=oc, t0=t0, msl=msl: e.tensor_add(
                    Y1T[:, oc, t0:t0 + TT], Y1T[:, oc, t0:t0 + TT], PSMM[msl][:, 0:TT]),
                    [r_mm.end_consume()])
        Y1_DONE = pl.val("r_mm_c")

    if BUILD_PHASE == 2:
        pl.wait("sync", "r_mm_c", Y1_DONE)
        pl.emit("sync", lambda e: e.dma_start(
            dout.rearrange("(c p) t -> p c t", p=128), Y1T[:, :, :]), [("s_out", 16)])
        pl.wait("sync", "s_out")


    if BUILD_PHASE >= 3:
        for tt in range(NTT):
            t0 = tt * TT
            pl.wait("tensor", "p3_msk")
            ln_stats(lambda c, t0=t0: Y1T[:, c, t0:t0 + TT], TT, src_ready="r_mm_c")
            for c in range(NCH):
                pl.wait("vector", "p3_xnf")
                pl.emit("vector", lambda e, c=c, t0=t0: e.tensor_sub(
                    TN1[:, 0:TT], Y1T[:, c, t0:t0 + TT], PSC[:, 0:TT]))
                pl.emit("vector", lambda e: e.tensor_tensor(
                    TN1[:, 0:TT], TN1[:, 0:TT], PSD[:, 0:TT], ALU.mult), [("p3_ln2", 1)])
                pl.wait("scalar", "p3_ln2")
                pl.wait("scalar", "p3_gmm")
                pl.emit("scalar", lambda e, c=c: e.activation(
                    XNF[:, 0:TT], TN1[:, 0:TT], AF.Identity,
                    scale=LN2W[:, c:c + 1], bias=LN2B[:, c:c + 1]), [("p3_xnf", 1)])
                pl.wait("scalar", "s_xnp")
                pl.emit("scalar", lambda e: e.activation(
                    XNB[:, 0:TT], XNF[:, 0:TT], AF.Copy), [("p3_xnb", 1)])
                pl.wait("sync", "p3_xnb")
                pl.emit("sync", lambda e, c=c, t0=t0: e.dma_start(
                    d_xn2[c * 128:(c + 1) * 128, t0:t0 + TT], XNB[:, 0:TT]),
                    [("s_xnp", 16)])
                pl.wait("tensor", "p3_xnf")
                pl.wait("tensor", "p3_lg")
                pl.emit("tensor", lambda e, c=c: e.matmul(
                    PSE[0:NE, 0:TT], GATEW[:, c, :], XNF[:, 0:TT],
                    start=(c == 0), stop=(c == NCH - 1)), [("p3_gmm", 1)])
            lns_done("p3_ln2")
            pl.wait("scalar", "p3_gmm")
            pl.emit("scalar", lambda e: e.activation(
                LG[:, 0:TT], PSE[0:NE, 0:TT], AF.Identity, bias=GATEB[:, 0:1]),
                [("p3_lg", 1)])
            for s in range(4):
                ts_ = tt * 4 + s
                pl.wait("tensor", "p3_lg")
                pl.wait("tensor", "p3_msk")
                pl.emit("tensor", lambda e, s=s: e.transpose(
                    PSB[:, 0:8], LG[0:8, s * 128:(s + 1) * 128], IDF[0:8, 0:8]),
                    [("p3_gt", 1)])
                pl.wait("vector", "p3_gt")
                pl.emit("vector", lambda e: e.reduce_max(MX[:, :], PSB[:, 0:8], axis=AX.X))
                pl.emit("vector", lambda e, ts_=ts_: e.tensor_scalar(
                    MASKTM[:, ts_, :], PSB[:, 0:8], MX[:, 0:1], None, ALU.is_ge),
                    [("p3_msk", 1)])
            pl.wait("scalar", "b_fin", AOUT_DONE)
            for c in range(NCH):
                pl.emit("scalar", lambda e, c=c, t0=t0: e.activation(
                    YACC[:, c, t0:t0 + TT], Y1T[:, c, t0:t0 + TT], AF.Copy),
                    [("p3_yi", 1)])

        m1t_idx = [0]
        m2t_idx = [0]
        for e_ in range(NE):
            if e_ == 0:
                pl.wait("gpsimd", "p3_yi", 32)
            pl.wait("gpsimd", "p3_m1d", e_)
            pl.emit("gpsimd", lambda e, e_=e_: e.dma_start(
                W1SB[:, :, :], dw1[e_, :, :].rearrange("(c p) j -> p c j", p=128)),
                [("s_w1", 16)])
            if e_ == 0:
                pl.wait("gpsimd", "r_mm_c", Y1_DONE)
            pl.wait("gpsimd", "p3_m2d", e_)
            pl.emit("gpsimd", lambda e, e_=e_: e.dma_start(
                W2SB[:, :, :], dw2[e_, :, :].rearrange("(j p) o -> p j o", p=128)),
                [("s_w2", 16)])
            for mt in range(NMTT):
                blk = e_ * NMTT + mt
                if blk > 0:
                    pl.wait("sync", "p3_m1t", blk)
                pl.wait("sync", "s_xnp")
                pl.emit("sync", lambda e, mt=mt: e.dma_start(
                    XNT[:, :, :],
                    d_xn2[:, mt * MTT:(mt + 1) * MTT].rearrange("(c p) t -> p c t", p=128)),
                    [("s_xnl", 16)])
                XNT_RDY = pl.val("s_xnl")
                pl.wait("tensor", "s_w1", 16 * (e_ + 1))
                pl.wait("tensor", "s_xnl", XNT_RDY)
                pl.wait("scalar", "p3_m2t", blk)
                for j in range(32):
                    mslot = r_mm.begin_produce("tensor")
                    for c in range(NCH):
                        incs = []
                        if c == NCH - 1:
                            incs = [r_mm.end_produce()]
                            if j == 31:
                                incs.append(("p3_m1t", 1))
                                if mt == NMTT - 1:
                                    incs.append(("p3_m1d", 1))
                        pl.emit("tensor", lambda e, c=c, j=j, mslot=mslot: e.matmul(
                            PSMM[mslot][:, 0:MTT], W1SB[:, c, j * 128:(j + 1) * 128],
                            XNT[:, c, 0:MTT], start=(c == 0), stop=(c == NCH - 1)), incs)
                    msl = r_mm.begin_consume("scalar")
                    pl.emit("scalar", lambda e, j=j, msl=msl: e.activation(
                        HSB[:, j, 0:MTT], PSMM[msl][:, 0:MTT], AF.Gelu_apprx_tanh),
                        [r_mm.end_consume()])
                HSB_RDY = pl.val("r_mm_c")
                pl.wait("tensor", "p3_msk", 4 * NTT)
                for half in range(2):
                    ts_ = mt * 2 + half
                    pl.wait("tensor", "p3_mr")
                    pl.emit("tensor", lambda e, ts_=ts_, e_=e_: e.matmul(
                        PSE[0:1, 512:640], MASKTM[:, ts_, e_:e_ + 1], IDF[:, :],
                        start=True, stop=True), [("p3_mt", 1)])
                    pl.wait("scalar", "p3_mt")
                    pl.emit("scalar", lambda e, half=half: e.activation(
                        MROWE[0:1, half * 128:(half + 1) * 128], PSE[0:1, 512:640],
                        AF.Copy), [("p3_mr", 1)])
                pl.wait("tensor", "p3_mr")
                pl.wait("tensor", "p3_mbs")
                pl.emit("tensor", lambda e: e.matmul(
                    PSE[:, 0:MTT], ONESB[0:1, :], MROWE[0:1, 0:MTT],
                    start=True, stop=True), [("p3_mb", 1)])
                pl.wait("scalar", "p3_mb")
                pl.wait("scalar", "p3_moe")
                pl.emit("scalar", lambda e: e.activation(
                    MBSB[:, 0:MTT], PSE[:, 0:MTT], AF.Copy), [("p3_mbs", 1)])
                pl.wait("tensor", "s_w2", 16 * (e_ + 1))
                pl.wait("tensor", "r_mm_c", HSB_RDY)
                for o in range(NCH):
                    m2 = r_mm2.begin_produce("tensor")
                    for j in range(32):
                        incs = []
                        if j == 31:
                            incs = [r_mm2.end_produce()]
                            if o == NCH - 1:
                                incs.append(("p3_m2t", 1))
                                if mt == NMTT - 1:
                                    incs.append(("p3_m2d", 1))
                        pl.emit("tensor", lambda e, o=o, j=j, m2=m2: e.matmul(
                            PSMM2[m2][:, 0:MTT], W2SB[:, j, o * 128:(o + 1) * 128],
                            HSB[:, j, 0:MTT], start=(j == 0), stop=(j == 31)), incs)
                    m2c = r_mm2.begin_consume("vector")
                    pl.wait("vector", "p3_mbs")
                    pl.emit("vector", lambda e, m2c=m2c: e.tensor_tensor(
                        T1M[:, 0:MTT], PSMM2[m2c][:, 0:MTT], MBSB[:, 0:MTT], ALU.mult),
                        [r_mm2.end_consume()])
                    pl.emit("vector", lambda e, o=o, mt=mt: e.tensor_add(
                        YACC[:, o, mt * MTT:(mt + 1) * MTT],
                        YACC[:, o, mt * MTT:(mt + 1) * MTT], T1M[:, 0:MTT]),
                        [("p3_moe", 1)])

        for tt in range(NTT):
            t0 = tt * TT
            pl.wait("sync", "p3_m1d", NE)
            pl.wait("sync", "p3_out")
            pl.emit("sync", lambda e, t0=t0: e.dma_start(
                X2T[:, :, :],
                d_x2[:, t0:t0 + TT].rearrange("(c p) t -> p c t", p=128)),
                [("s_x2r", 16)])
            ln_stats(lambda c, t0=t0: YACC[:, c, t0:t0 + TT], TT, src_f32=False,
                     src_ready="p3_moe")
            pl.wait("vector", "s_x2r")
            pl.wait("vector", "s_out")
            for c in range(NCH):
                pl.emit("vector", lambda e, c=c, t0=t0: e.tensor_sub(
                    TN1[:, 0:TT], YACC[:, c, t0:t0 + TT], PSC[:, 0:TT]))
                incs = [("p3_fn", 1)] if c == NCH - 1 else []
                pl.emit("vector", lambda e: e.tensor_tensor(
                    TN1[:, 0:TT], TN1[:, 0:TT], PSD[:, 0:TT], ALU.mult), incs)
                pl.emit("vector", lambda e, c=c: e.tensor_add(
                    OUTT[:, c, 0:TT], X2T[:, c, 0:TT], TN1[:, 0:TT]), [("p3_out", 1)])
            lns_done("p3_fn")
            pl.wait("sync", "p3_out")
            pl.emit("sync", lambda e, t0=t0: e.dma_start(
                dout[:, t0:t0 + TT].rearrange("(c p) t -> p c t", p=128),
                OUTT[:, :, :]), [("s_out", 16)])
        pl.wait("sync", "s_out")

    emit(nc, pl)
    return nc


def emit(nc, pl):
    with ExitStack() as st:
        sems = {n: st.enter_context(nc.semaphore(n)) for n in pl.cnt}
        nums = sorted(s.num for s in sems.values())
        assert nums[-1] - nums[0] + 1 == len(nums), "sem numbers not contiguous"
        clear_range = range(nums[0], nums[-1] + 1)
        with nc.Block() as block0:
            @block0.gpsimd
            def _(g):
                g.dma_reset(clear_range)
                g.sem_clear(clear_range)
        nc.all_engine_barrier()
        with nc.Block() as block:
            for eng_name, deco in (("tensor", block.tensor), ("vector", block.vector),
                                   ("scalar", block.scalar), ("gpsimd", block.gpsimd),
                                   ("sync", block.sync)):
                def runner(e, eng_name=eng_name):
                    for item in pl.ops[eng_name]:
                        if item[0] == "wait":
                            e.wait_ge(sems[item[1]], item[2])
                        else:
                            ins = item[1](e)
                            incs = item[2]
                            if eng_name != "tensor":
                                assert len(incs) <= 1, (eng_name, incs)
                            if len(incs) > 1:
                                ins.then_inc(sems[incs[0][0]], incs[0][1])
                                for s, a in incs[1:]:
                                    e.nop().then_inc(sems[s], a)
                            else:
                                for s, a in incs:
                                    ins.then_inc(sems[s], a)
                deco(runner)



def _stage(inputs):
    x = np.asarray(inputs["L0_attn_x"])
    W_qk = np.asarray(inputs["L0_attn_W_qk"])
    b_qk = np.asarray(inputs["L0_attn_b_qk"])
    lepe_w = np.asarray(inputs["L0_attn_lepe_w"])
    lepe_b = np.asarray(inputs["L0_attn_lepe_b"])
    ln1w = np.asarray(inputs["L0_ffn_ln1_w"]); ln1b = np.asarray(inputs["L0_ffn_ln1_b"])
    ln2w = np.asarray(inputs["L0_ffn_ln2_w"]); ln2b = np.asarray(inputs["L0_ffn_ln2_b"])
    wqkv = np.asarray(inputs["L0_ffn_w_qkv"]); wproj = np.asarray(inputs["L0_ffn_w_proj"])
    gatew = np.asarray(inputs["L0_ffn_gate_w"]); gateb = np.asarray(inputs["L0_ffn_gate_b"])
    w1 = np.asarray(inputs["L0_ffn_w1"]); w2 = np.asarray(inputs["L0_ffn_w2"])

    k_max = C // 2
    theta = 1.0 / (10000.0 ** (np.arange(k_max, dtype=np.float64) / k_max))
    cosv = np.repeat(np.cos(theta), 2).astype(np.float32)
    sinv = np.zeros(C, dtype=np.float32)
    sinv[0::2] = -np.sin(theta)
    sinv[1::2] = np.sin(theta)

    def colview(v):
        return np.ascontiguousarray(np.asarray(v, dtype=np.float32).reshape(NCH, 128).T)

    hmask = np.zeros((NCH, NH, 128), dtype=np.float32)
    for c in range(NCH):
        hmask[c, 2 * c, 0:64] = 1.0
        hmask[c, 2 * c + 1, 64:128] = 1.0

    shared = {
        "wqkT": np.ascontiguousarray(W_qk.T),
        "bqk": np.ascontiguousarray(b_qk.reshape(16, 128).T),
        "ropecos": colview(cosv), "ropesin": colview(sinv),
        "lepew": np.ascontiguousarray(
            lepe_w[:, 0, :].reshape(NCH, 128, 3).transpose(1, 0, 2).reshape(128, NCH * 3)),
        "lepeb": colview(lepe_b),
        "ln1w": colview(ln1w), "ln1b": colview(ln1b),
        "ln2w": colview(ln2w), "ln2b": colview(ln2b),
        "wqkv": np.ascontiguousarray(wqkv), "wproj": np.ascontiguousarray(wproj),
        "gatew": np.ascontiguousarray(gatew),
        "gateb": np.ascontiguousarray(np.asarray(gateb).reshape(NE, 1)),
        "w1": np.ascontiguousarray(w1), "w2": np.ascontiguousarray(w2),
        "ident": np.eye(128, dtype=np.float32),
        "csts": np.broadcast_to(np.array(
            [1.0 / SEQ, -1.0, 0.125, 0.5, EPS, 0.0, 0.0, 0.0], dtype=np.float32),
            (128, 8)).copy(),
        "hmask": hmask,
    }
    in_maps = []
    for core in range(NCORES):
        b = core // 4
        t0 = (core % 4) * T
        m = dict(shared)
        m["x_fm"] = np.ascontiguousarray(x[b, t0:t0 + T].T)
        halo = np.zeros((C, 2), dtype=np.float32)
        if t0 > 0:
            halo[:, 0] = x[b, t0 - 1]
        if t0 + T < SEQ:
            halo[:, 1] = x[b, t0 + T]
        m["x_halo"] = halo
        in_maps.append(m)
    return in_maps


def kernel(**inputs):
    in_maps = _stage(inputs)
    nc = build_program()
    res = run_bass_kernel_spmd(nc, in_maps, core_ids=list(range(NCORES)))
    out = np.empty((NB, SEQ, C), dtype=np.float32)
    for core in range(NCORES):
        b = core // 4
        t0 = (core % 4) * T
        out[b, t0:t0 + T] = res.results[core]["out_fm"].T
    return out


# version 06
# speedup vs baseline: 1.7900x; 1.7900x over previous
from contextlib import ExitStack

import numpy as np
import concourse.bass as bass
import concourse.mybir as mybir
from concourse.bass_utils import run_bass_kernel_spmd

F32 = mybir.dt.float32
BF16 = mybir.dt.bfloat16
AF = mybir.ActivationFunctionType
ALU = mybir.AluOpType
AX = mybir.AxisListType

NCORES = 8
NB = 2
SEQ = 8192
T = 2048
C = 1024
NCH = 8
NH = 16
DK = 64
TT = 512
NTT = T // TT
HID = 4096
NE = 8
MTT = 256
NMTT = T // MTT
EPS = 1e-5
GROUPS = [[0, 1, 2, 3], [4, 5, 6, 7]]
PAIRSWAP = [i + 1 if i % 2 == 0 else i - 1 for i in range(32)]

ENGS = ("tensor", "vector", "scalar", "gpsimd", "sync")
CARVED = {}

BUILD_PHASE = 3


class Plan:
    CHAINED = ("vector", "scalar")

    def __init__(self):
        self.ops = {e: [] for e in ENGS}
        self.cnt = {}
        self._lastwait = {e: {} for e in ENGS}
        self._lastop = {e: None for e in ENGS}

    def sem(self, name):
        if name not in self.cnt:
            self.cnt[name] = 0
        return name

    def emit(self, eng, fn, incs=()):
        incs = tuple(incs)
        if eng in self.CHAINED:
            prev = self._lastop[eng]
            if prev is not None:
                self.wait(eng, prev[0], prev[1])
            if not incs:
                incs = (("ch_" + eng, 1),)
        for s, a in incs:
            self.cnt[self.sem(s)] += a
        if eng in self.CHAINED:
            self._lastop[eng] = (incs[0][0], self.cnt[incs[0][0]])
        self.ops[eng].append(("op", fn, incs))

    def wait(self, eng, sem, val=None):
        self.sem(sem)
        v = self.cnt[sem] if val is None else val
        if v <= 0:
            return
        lw = self._lastwait[eng]
        if lw.get(sem, -1) >= v:
            return
        lw[sem] = v
        self.ops[eng].append(("wait", sem, v))

    def val(self, sem):
        return self.cnt[self.sem(sem)]


class Rot:
    def __init__(self, pl, name, depth, unit_p=1, unit_c=1):
        self.pl = pl
        self.depth = depth
        self.up = unit_p
        self.uc = unit_c
        self.sp = pl.sem(name + "_p")
        self.sc = pl.sem(name + "_c")
        self.np_ = 0
        self.nc_ = 0

    def begin_produce(self, eng):
        if self.np_ >= self.depth:
            self.pl.wait(eng, self.sc, (self.np_ - self.depth + 1) * self.uc)
        return self.np_ % self.depth

    def end_produce(self):
        self.np_ += 1
        return (self.sp, self.up)

    def begin_consume(self, eng):
        self.pl.wait(eng, self.sp, (self.nc_ + 1) * self.up)
        return self.nc_ % self.depth

    def end_consume(self):
        self.nc_ += 1
        return (self.sc, self.uc)

    def drain(self, eng):
        self.pl.wait(eng, self.sc, self.np_ * self.uc)


def build_program():
    nc = bass.Bass("TRN2", target_bir_lowering=False, detect_race_conditions=False)

    dx = nc.dram_tensor("x_fm", [C, T], F32, kind="ExternalInput").ap()
    dxh = nc.dram_tensor("x_halo", [C, 2], F32, kind="ExternalInput").ap()
    dwqkT = nc.dram_tensor("wqkT", [C, 2 * C], F32, kind="ExternalInput").ap()
    dbqk = nc.dram_tensor("bqk", [128, 16], F32, kind="ExternalInput").ap()
    dcos = nc.dram_tensor("ropecos", [128, NCH], F32, kind="ExternalInput").ap()
    dsin = nc.dram_tensor("ropesin", [128, NCH], F32, kind="ExternalInput").ap()
    dlepw = nc.dram_tensor("lepew", [128, NCH * 3], F32, kind="ExternalInput").ap()
    dlepb = nc.dram_tensor("lepeb", [128, NCH], F32, kind="ExternalInput").ap()
    dln1w = nc.dram_tensor("ln1w", [128, NCH], F32, kind="ExternalInput").ap()
    dln1b = nc.dram_tensor("ln1b", [128, NCH], F32, kind="ExternalInput").ap()
    dln2w = nc.dram_tensor("ln2w", [128, NCH], F32, kind="ExternalInput").ap()
    dln2b = nc.dram_tensor("ln2b", [128, NCH], F32, kind="ExternalInput").ap()
    dwqkv = nc.dram_tensor("wqkv", [C, 3 * C], F32, kind="ExternalInput").ap()
    dwproj = nc.dram_tensor("wproj", [C, C], F32, kind="ExternalInput").ap()
    dgatew = nc.dram_tensor("gatew", [C, NE], F32, kind="ExternalInput").ap()
    dgateb = nc.dram_tensor("gateb", [NE, 1], F32, kind="ExternalInput").ap()
    dw1 = nc.dram_tensor("w1", [NE, C, HID], F32, kind="ExternalInput").ap()
    dw2 = nc.dram_tensor("w2", [NE, HID, C], F32, kind="ExternalInput").ap()
    dident = nc.dram_tensor("ident", [128, 128], F32, kind="ExternalInput").ap()
    dhmask = nc.dram_tensor("hmask", [NCH, NH, 128], F32, kind="ExternalInput").ap()
    dcsts = nc.dram_tensor("csts", [128, 8], F32, kind="ExternalInput").ap()
    dout = nc.dram_tensor("out_fm", [C, T], F32, kind="ExternalOutput").ap()

    d_ckv_i = nc.dram_tensor("ckv_i", [64, C], F32).ap()
    d_ckv_o = nc.dram_tensor("ckv_o", [64, C], F32).ap()
    d_ckm_i = nc.dram_tensor("ckm_i", [128, NCH], F32).ap()
    d_ckm_o = nc.dram_tensor("ckm_o", [128, NCH], F32).ap()
    d_kg_i = nc.dram_tensor("kg_i", [C, T], BF16).ap()
    d_kg_o = nc.dram_tensor("kg_o", [4 * C, T], BF16).ap()
    d_vg_i = nc.dram_tensor("vg_i", [NH * T, DK + 1], BF16).ap()
    d_vg_o = nc.dram_tensor("vg_o", [4 * NH * T, DK + 1], BF16).ap()
    d_x2 = nc.dram_tensor("park_x2", [C, T], F32).ap()
    d_y0 = nc.dram_tensor("park_y0", [C, T], F32).ap()
    d_xn2 = nc.dram_tensor("park_xn2", [C, T], BF16).ap()

    remaining = nc.sbuf_bytes_remaining
    A0 = 0
    B0 = A0 + 65536
    C0 = B0 + 65536
    D0 = C0 + 33792
    E0 = D0 + 33344
    TOTAL = (remaining - 64) & ~31
    assert TOTAL >= E0 + 14400, (TOTAL, E0)
    arena_h = nc.alloc_sbuf_tensor("arena", [128, TOTAL], mybir.dt.uint8)
    arena_addr = nc.lookup_mloc(arena_h).addr

    def carve(name, shape, dtype, off):
        nbytes = int(np.prod(shape[1:])) * mybir.dt.size(dtype)
        assert off >= 0 and off + nbytes <= TOTAL, (name, off, nbytes, TOTAL)
        h = nc.virtual_sbuf_tensor(name, shape, dtype)
        nc._tensor(h.name, h.shape, h.dtype, type="SB")
        mloc = nc.lookup_mloc(h)
        mloc.allocated = True
        mloc.addr = arena_addr + off
        CARVED[name] = h.name
        return h

    X = carve("X", [128, NCH, T], F32, A0)
    RES = carve("RES", [128, NCH, T], F32, B0)
    XLN = carve("XLN", [128, NCH, T + 2], BF16, C0)

    eoff = [E0]

    def carve_e(name, shape, dtype):
        h = carve(name, shape, dtype, eoff[0])
        nbytes = int(np.prod(shape[1:])) * mybir.dt.size(dtype)
        eoff[0] = (eoff[0] + nbytes + 31) & ~31
        return h

    IDF = carve_e("IDF", [128, 128], F32)
    IDB = carve_e("IDB", [128, 128], BF16)
    ONESB = carve_e("ONESB", [128, 128], BF16)
    COS = carve_e("COS", [128, NCH], F32)
    SIN = carve_e("SIN", [128, NCH], F32)
    LEPW = carve_e("LEPW", [128, NCH * 3], F32)
    LEPB = carve_e("LEPB", [128, NCH], F32)
    BQK = carve_e("BQK", [128, 16], F32)
    LN1W = carve_e("LN1W", [128, NCH], F32)
    LN1B = carve_e("LN1B", [128, NCH], F32)
    LN2W = carve_e("LN2W", [128, NCH], F32)
    LN2B = carve_e("LN2B", [128, NCH], F32)
    GATEW = carve_e("GATEW", [128, NCH, NE], F32)
    GATEB = carve_e("GATEB", [NE, 1], F32)
    HMB = carve_e("HMB", [NH, NCH * 128], BF16)
    CSTS = carve_e("CSTS", [128, 8], F32)
    MUR = carve_e("MUR", [1, TT], F32)
    VARR = carve_e("VARR", [1, TT], F32)
    RSR = carve_e("RSR", [1, TT], F32)
    MURB = carve_e("MURB", [1, TT], BF16)
    RSRB = carve_e("RSRB", [1, TT], BF16)
    KVGB = carve_e("KVGB", [128, C // 2], BF16)
    KMG = carve_e("KMG", [128, NCH], F32)
    KMM = carve_e("KMM", [128, NCH, NH], BF16)
    assert eoff[0] <= TOTAL, (eoff[0], TOTAL)

    WQK_SB = [carve("WQK_SB%d" % i, [128, NCH, 128], BF16, D0 + i * 2048) for i in range(2)]
    SQT = [carve("SQT%d" % i, [128, TT], BF16, D0 + 4096 + i * 1024) for i in range(2)]
    XBT = [carve("XBT%d" % i, [128, TT], BF16, D0 + 6144 + i * 1024) for i in range(2)]
    TN1 = carve("TN1", [128, TT], F32, D0 + 8192)
    ELU = [carve("ELU%d" % i, [128, TT], BF16, D0 + 10240 + i * 1024) for i in range(2)]
    KCH = carve("KCH", [128, TT], BF16, D0 + 12288)
    SWP = carve("SWP", [128, TT], BF16, D0 + 13312)
    RT4 = carve("RT4", [128, TT], BF16, D0 + 14336)
    KRCH = carve("KRCH", [128, TT], BF16, D0 + 15360)
    KRT = carve("KRT", [128, 4, C], BF16, D0 + 16384)
    VT = carve("VT", [128, 4, C], BF16, D0 + 24576)
    KMEAN = carve("KMEAN", [128, NCH], F32, D0 + 32768)
    KMT = carve("KMT", [128, 1], F32, D0 + 32800)
    QF = carve("QF", [128, NCH, T], BF16, B0)
    KL = carve("KL", [128, NCH, T], BF16, B0 + 32768)
    AOUT = carve("AOUT", [128, NCH, T], BF16, B0 + 32768)
    VL = carve("VL", [128, NH, 16, DK + 1], BF16, C0)
    KAUG = [carve("KAUG%d" % i, [65, 4 * T], BF16, C0 + i * 16384) for i in range(2)]
    VH = [carve("VH%d" % i, [128, 64, DK + 1], BF16, D0 + i * 8352) for i in range(2)]
    QAUG = [carve("QAUG%d" % i, [65, T], BF16, D0 + 16704 + i * 4096) for i in range(2)]
    PT = [carve("PT%d" % i, [128, 512], BF16, D0 + 24896 + i * 1024) for i in range(2)]
    MROW = carve("MROW", [128, 1], F32, D0 + 26944)
    MTMP = carve("MTMP", [128, 1], F32, D0 + 26976)
    RL = carve("RL", [1, 512], F32, D0 + 27008)
    RLB = carve("RLB", [1, 512], BF16, D0 + 29088)
    OT16 = carve("OT16", [64, 512], BF16, D0 + 30144)
    XN1T = carve("XN1T", [128, NCH, TT], BF16, D0 + 16384)
    VCH = [carve("VCH%d" % i, [128, TT], BF16, D0 + 12288 + i * 1024) for i in range(2)]
    SROW = carve("SROW", [128, 2, 4 * T], BF16, A0)
    Y1T = carve("Y1T", [128, NCH, T], F32, A0)
    W1SB = carve("W1SB", [128, NCH, HID], BF16, A0)
    W2SB = carve("W2SB", [128, 32, C], BF16, B0)
    YACC = carve("YACC", [128, NCH, T], BF16, C0)
    HSB = carve("HSB", [128, 32, MTT], BF16, D0)
    XNT = carve("XNT", [128, NCH, MTT], BF16, D0 + 16384)
    MBSB = carve("MBSB", [128, MTT], BF16, D0 + 20480)
    T1M = carve("T1M", [128, MTT], F32, D0 + 21504)
    LG = carve("LG", [8, TT], F32, D0 + 22528)
    MASKTM = carve("MASKTM", [128, 16, NE], F32, D0 + 24576)
    MROWE = carve("MROWE", [1, MTT], BF16, D0 + 25088)
    XNF = carve("XNF", [128, TT], F32, D0 + 25600)
    XNB = carve("XNB", [128, TT], BF16, D0 + 27648)
    MX = carve("MX", [128, 1], F32, D0 + 28672)
    X2T = carve("X2T", [128, NCH, TT], F32, A0)
    OUTT = carve("OUTT", [128, NCH, TT], F32, A0 + 16384)
    HLO = carve("HLO", [128, 16], F32, D0 + 32832)
    HSC = carve("HSC", [128, 32], F32, D0 + 32896)
    KVS = carve("KVS", [64, C], F32, B0)
    QTI = carve("QTI", [128, NCH, TT], BF16, D0 + 16384)
    QRTI = carve("QRTI", [128, NCH, TT], BF16, D0 + 24576)
    ATB = carve("ATB", [128, TT], BF16, D0 + 12288)
    ZROW = carve("ZROW", [NH, TT], F32, D0 + 13312)
    ZROWB = carve("ZROWB", [NH, TT], BF16, D0 + 15360)

    PSA = nc.alloc_psum_tensor("PSA", [128, 512], F32)
    PSB = nc.alloc_psum_tensor("PSB", [128, 512], F32)
    PSC = nc.alloc_psum_tensor("PSC", [128, 512], F32)
    PSD = nc.alloc_psum_tensor("PSD", [128, 512], F32)
    PSE = nc.alloc_psum_tensor("PSE", [128, 1024], F32)
    PST = nc.alloc_psum_tensor("PST", [128, 2048], BF16)
    PSMM = [PSA, PSB]
    PSMM2 = [PSC, PSD]

    pl = Plan()
    r_mm = Rot(pl, "r_mm", 2)
    r_tp = Rot(pl, "r_tp", 2)
    r_wqk = Rot(pl, "r_wqk", 2, unit_p=16)
    r_sq = Rot(pl, "r_sq", 2)
    r_xb = Rot(pl, "r_xb", 2)
    r_elu = Rot(pl, "r_elu", 2)
    r_zb = Rot(pl, "r_zb", 1)
    r_mm2 = Rot(pl, "r_mm2", 2)
    r_vch = Rot(pl, "r_vch", 2)

    lns_prev = [None]

    def ln_stats(src_chunk, W, src_f32=True, src_ready=None):
        if src_ready is not None:
            pl.wait("scalar", src_ready)
            pl.wait("tensor", src_ready)

        r_mm.drain("tensor")
        r_mm2.drain("tensor")
        for c in range(NCH):
            if src_f32:
                slot = r_xb.begin_produce("scalar")
                pl.emit("scalar", lambda e, c=c, slot=slot: e.activation(
                    XBT[slot][:, 0:W], src_chunk(c), AF.Copy), [r_xb.end_produce()])
            slot = r_sq.begin_produce("scalar")
            pl.emit("scalar", lambda e, c=c, slot=slot: e.activation(
                SQT[slot][:, 0:W], src_chunk(c), AF.Square), [r_sq.end_produce()])
            if src_f32:
                sl = r_xb.begin_consume("tensor")
                incs_a = [r_xb.end_consume()]
                if c == NCH - 1:
                    incs_a.append(("lns_pe", 1))
                pl.emit("tensor", lambda e, sl=sl, c=c: e.matmul(
                    PSA[0:1, 0:W], ONESB[:, 0:1], XBT[sl][:, 0:W],
                    start=(c == 0), stop=(c == NCH - 1)), incs_a)
            else:
                pl.emit("tensor", lambda e, c=c: e.matmul(
                    PSA[0:1, 0:W], ONESB[:, 0:1], src_chunk(c),
                    start=(c == 0), stop=(c == NCH - 1)),
                    [("lns_pe", 1)] if c == NCH - 1 else [])
            sl = r_sq.begin_consume("tensor")
            incs = [r_sq.end_consume()]
            if c == NCH - 1:
                incs.append(("lns_pe", 1))
            pl.emit("tensor", lambda e, sl=sl, c=c: e.matmul(
                PSB[0:1, 0:W], ONESB[:, 0:1], SQT[sl][:, 0:W],
                start=(c == 0), stop=(c == NCH - 1)), incs)
        pl.wait("vector", "lns_pe")
        pl.emit("vector", lambda e: e.tensor_scalar_mul(MUR[:, 0:W], PSA[0:1, 0:W], 1.0 / C))
        pl.emit("vector", lambda e: e.tensor_scalar_mul(VARR[:, 0:W], PSB[0:1, 0:W], 1.0 / C))
        pl.emit("vector", lambda e: e.tensor_mul(RSR[:, 0:W], MUR[:, 0:W], MUR[:, 0:W]))
        pl.emit("vector", lambda e: e.scalar_tensor_tensor(
            VARR[:, 0:W], VARR[:, 0:W], float(EPS), RSR[:, 0:W], ALU.add, ALU.subtract),
                [("lns_dv", 1)])
        pl.wait("scalar", "lns_dv")
        pl.emit("scalar", lambda e: e.activation(VARR[:, 0:W], VARR[:, 0:W], AF.Sqrt))
        pl.emit("scalar", lambda e: e.activation(MURB[:, 0:W], MUR[:, 0:W], AF.Copy),
                [("lns_ac", 1)])
        pl.wait("vector", "lns_ac")
        pl.emit("vector", lambda e: e.reciprocal(RSR[:, 0:W], VARR[:, 0:W]), [("lns_dv2", 1)])
        pl.wait("scalar", "lns_dv2")
        pl.emit("scalar", lambda e: e.activation(RSRB[:, 0:W], RSR[:, 0:W], AF.Copy),
                [("lns_ac2", 1)])
        pl.wait("tensor", "lns_ac2")
        if lns_prev[0] is not None:
            pl.wait("tensor", lns_prev[0][0], lns_prev[0][1])
        pl.emit("tensor", lambda e: e.matmul(PSC[:, 0:W], ONESB[0:1, :], MURB[0:1, 0:W],
                                             start=True, stop=True))
        pl.emit("tensor", lambda e: e.matmul(PSD[:, 0:W], ONESB[0:1, :], RSRB[0:1, 0:W],
                                             start=True, stop=True), [("lns_bc", 1)])
        pl.wait("vector", "lns_bc")

    def lns_done(sem):
        lns_prev[0] = (sem, pl.val(sem))

    pl.emit("sync", lambda e: e.dma_start(
        X[:, :, :], dx.rearrange("(c p) t -> p c t", p=128)), [("s_x", 16)])
    pl.emit("sync", lambda e: e.dma_start(
        HLO[:, 0:16].rearrange("p (c k) -> p c k", k=2),
        dxh.rearrange("(c p) k -> p c k", p=128)), [("s_x", 16)])
    for sb, dr in [(IDF, dident), (COS, dcos), (SIN, dsin), (LEPW, dlepw), (LEPB, dlepb),
                   (BQK, dbqk), (LN1W, dln1w), (LN1B, dln1b), (LN2W, dln2w), (LN2B, dln2b),
                   (GATEB, dgateb), (CSTS, dcsts)]:
        pl.emit("sync", lambda e, sb=sb, dr=dr: e.dma_start(sb[:, :], dr[:, :]),
                [("s_const", 16)])
    pl.emit("sync", lambda e: e.dma_start(
        GATEW[:, :, :], dgatew.rearrange("(c p) e -> p c e", p=128)), [("s_const", 16)])
    pl.emit("gpsimd", lambda e: e.dma_start(IDB[:, :], dident[:, :]), [("s_constg", 16)])
    pl.emit("gpsimd", lambda e: e.dma_start(
        HMB[:, :].rearrange("h (c p) -> h c p", c=NCH),
        dhmask.rearrange("c h p -> h c p")), [("s_constg", 16)])
    pl.emit("gpsimd", lambda e: e.memset(ONESB[:, :], 1.0), [("s_mset", 1)])
    pl.emit("gpsimd", lambda e: e.memset(KMEAN[:, :], 0.0), [("s_mset", 1)])
    pl.emit("gpsimd", lambda e: e.memset(KMM[:, :, :], 0.0), [("s_mset", 1)])
    pl.emit("gpsimd", lambda e: e.memset(KVS[:, :], 0.0), [("s_mset", 1)])
    N_CONST = pl.val("s_const")

    for eng in ("tensor", "vector", "scalar"):
        pl.wait(eng, "s_const", N_CONST)
        pl.wait(eng, "s_constg", 32)
        pl.wait(eng, "s_mset", 4)
        pl.wait(eng, "s_x", 32)

    for c in range(NCH):
        pl.emit("scalar", lambda e, c=c: e.activation(
            SQT[0][:, 2 * c:2 * c + 2], HLO[:, 2 * c:2 * c + 2], AF.Copy))
        pl.emit("scalar", lambda e, c=c: e.activation(
            SQT[0][:, 16 + 2 * c:16 + 2 * c + 2], HLO[:, 2 * c:2 * c + 2], AF.Square),
            [("h_st", 1)] if c == NCH - 1 else [])
    pl.wait("tensor", "h_st")
    for c in range(NCH):
        pl.emit("tensor", lambda e, c=c: e.matmul(
            PSA[0:1, 0:2], ONESB[:, 0:1], SQT[0][:, 2 * c:2 * c + 2],
            start=(c == 0), stop=(c == NCH - 1)))
        pl.emit("tensor", lambda e, c=c: e.matmul(
            PSB[0:1, 0:2], ONESB[:, 0:1], SQT[0][:, 16 + 2 * c:16 + 2 * c + 2],
            start=(c == 0), stop=(c == NCH - 1)),
            [("h_mm", 1)] if c == NCH - 1 else [])
    pl.wait("vector", "h_mm")
    pl.emit("vector", lambda e: e.tensor_scalar_mul(MUR[:, 0:2], PSA[0:1, 0:2], 1.0 / C))
    pl.emit("vector", lambda e: e.tensor_scalar_mul(VARR[:, 0:2], PSB[0:1, 0:2], 1.0 / C))
    pl.emit("vector", lambda e: e.tensor_mul(RSR[:, 0:2], MUR[:, 0:2], MUR[:, 0:2]))
    pl.emit("vector", lambda e: e.scalar_tensor_tensor(
        VARR[:, 0:2], VARR[:, 0:2], float(EPS), RSR[:, 0:2], ALU.add, ALU.subtract),
        [("h_rw", 1)])
    pl.wait("scalar", "h_rw")
    pl.emit("scalar", lambda e: e.activation(VARR[:, 0:2], VARR[:, 0:2], AF.Sqrt))
    pl.emit("scalar", lambda e: e.activation(MURB[:, 0:2], MUR[:, 0:2], AF.Copy),
            [("h_sq", 1)])
    pl.wait("vector", "h_sq")
    pl.emit("vector", lambda e: e.reciprocal(RSR[:, 0:2], VARR[:, 0:2]), [("h_rc", 1)])
    pl.wait("scalar", "h_rc")
    pl.emit("scalar", lambda e: e.activation(RSRB[:, 0:2], RSR[:, 0:2], AF.Copy),
            [("h_sq2", 1)])
    pl.wait("tensor", "h_sq2")
    pl.emit("tensor", lambda e: e.matmul(PSC[:, 0:2], ONESB[0:1, :], MURB[0:1, 0:2],
                                         start=True, stop=True))
    pl.emit("tensor", lambda e: e.matmul(PSD[:, 0:2], ONESB[0:1, :], RSRB[0:1, 0:2],
                                         start=True, stop=True), [("h_bc", 1)])
    pl.wait("vector", "h_bc")
    for c in range(NCH):
        for k in range(2):
            col = 0 if k == 0 else T + 1
            pl.emit("vector", lambda e, c=c, k=k: e.tensor_sub(
                HSC[:, 4 * c + k:4 * c + k + 1], HLO[:, 2 * c + k:2 * c + k + 1],
                PSC[:, k:k + 1]))
            pl.emit("vector", lambda e, c=c, k=k, col=col: e.tensor_tensor(
                XLN[:, c, col:col + 1], HSC[:, 4 * c + k:4 * c + k + 1],
                PSD[:, k:k + 1], ALU.mult),
                [("h_done", 1)] if (c == NCH - 1 and k == 1) else [])
    lns_done("h_done")
    pl.wait("scalar", "h_mm")
    pl.wait("tensor", "h_rw")

    def emit_qk_chunk(j, t0, out_dve):
        wslot = r_wqk.begin_produce("gpsimd")
        pl.emit("gpsimd", lambda e, j=j, wslot=wslot: e.dma_start(
            WQK_SB[wslot][:, :, :],
            dwqkT[:, j * 128:(j + 1) * 128].rearrange("(c p) j -> p c j", p=128)),
            [r_wqk.end_produce()])
        wsl = r_wqk.begin_consume("tensor")
        pl.wait("tensor", "p1_xln")
        mslot = r_mm.begin_produce("tensor")
        for c in range(NCH):
            incs = []
            if c == NCH - 1:
                incs = [r_mm.end_produce(), r_wqk.end_consume()]
            pl.emit("tensor", lambda e, c=c, wsl=wsl, mslot=mslot, t0=t0: e.matmul(
                PSMM[mslot][:, 0:TT], WQK_SB[wsl][:, c, :],
                XLN[:, c, 1 + t0:1 + t0 + TT], start=(c == 0), stop=(c == NCH - 1)), incs)
        msl = r_mm.begin_consume("vector")
        ps = PSMM[msl]
        eslot = r_elu.begin_produce("vector")
        pl.emit("vector", lambda e, ps=ps, j=j, eslot=eslot: e.tensor_scalar(
            ELU[eslot][:, 0:TT], ps[:, 0:TT], BQK[:, j:j + 1], 0.0, ALU.add, ALU.min),
            [r_elu.end_produce()])
        el = r_elu.begin_consume("scalar")
        pl.emit("scalar", lambda e, el=el: e.activation(
            ELU[el][:, 0:TT], ELU[el][:, 0:TT], AF.Exp), [r_elu.end_consume()])
        pl.wait("vector", "r_elu_c")
        pl.emit("vector", lambda e, ps=ps, j=j: e.tensor_scalar(
            TN1[:, 0:TT], ps[:, 0:TT], BQK[:, j:j + 1], 0.0, ALU.add, ALU.max))
        out_dve(el, ps)

    for tt in range(NTT):
        t0 = tt * TT
        ln_stats(lambda c, t0=t0: X[:, c, t0:t0 + TT], TT)
        for c in range(NCH):
            pl.emit("vector", lambda e, c=c, t0=t0: e.tensor_sub(
                TN1[:, 0:TT], X[:, c, t0:t0 + TT], PSC[:, 0:TT]))
            pl.emit("vector", lambda e, c=c, t0=t0: e.tensor_tensor(
                XLN[:, c, 1 + t0:1 + t0 + TT], TN1[:, 0:TT], PSD[:, 0:TT], ALU.mult),
                [("p1_xln", 1)])
        lns_done("p1_xln")

        for jj in range(8):
            j = 8 + jj

            def k_out(el, ps, jj=jj, t0=t0):
                pl.emit("vector", lambda e, el=el: e.tensor_add(
                    KCH[:, 0:TT], TN1[:, 0:TT], ELU[el][:, 0:TT]), [r_mm.end_consume()])
                pl.emit("vector", lambda e: e.reduce_sum(KMT[:, :], KCH[:, 0:TT], axis=AX.X))
                pl.emit("vector", lambda e, jj=jj: e.tensor_add(
                    KMEAN[:, jj:jj + 1], KMEAN[:, jj:jj + 1], KMT[:, :]), [("p1_km", 1)])
                pl.emit("vector", lambda e: e.stream_shuffle(SWP[:, 0:TT], KCH[:, 0:TT],
                                                             PAIRSWAP))
                pl.emit("vector", lambda e, jj=jj: e.tensor_scalar_mul(
                    RT4[:, 0:TT], KCH[:, 0:TT], COS[:, jj:jj + 1]))
                pl.emit("vector", lambda e, jj=jj: e.scalar_tensor_tensor(
                    KRCH[:, 0:TT], SWP[:, 0:TT], SIN[:, jj:jj + 1], RT4[:, 0:TT],
                    ALU.mult, ALU.add), [("p1_kr", 1)])
                for s in range(4):
                    tsl = r_tp.begin_produce("tensor")
                    pl.wait("tensor", "p1_kr")
                    pl.emit("tensor", lambda e, s=s, tsl=tsl: e.transpose(
                        PST[:, tsl * 1024:tsl * 1024 + 128],
                        KRCH[:, s * 128:(s + 1) * 128], IDB[:, :]), [r_tp.end_produce()])
                    tc = r_tp.begin_consume("scalar")
                    pl.emit("scalar", lambda e, s=s, jj=jj, tc=tc: e.activation(
                        KRT[:, s, jj * 128:(jj + 1) * 128],
                        PST[:, tc * 1024:tc * 1024 + 128], AF.Copy),
                        [r_tp.end_consume()])
                    tsl = r_tp.begin_produce("tensor")
                    pl.emit("tensor", lambda e, s=s, jj=jj, t0=t0, tsl=tsl: e.transpose(
                        PST[:, tsl * 1024:tsl * 1024 + 128],
                        XLN[:, jj, 1 + t0 + s * 128:1 + t0 + (s + 1) * 128], IDB[:, :]),
                        [r_tp.end_produce()])
                    tc = r_tp.begin_consume("scalar")
                    pl.emit("scalar", lambda e, s=s, jj=jj, tc=tc: e.activation(
                        VT[:, s, jj * 128:(jj + 1) * 128],
                        PST[:, tc * 1024:tc * 1024 + 128], AF.Copy),
                        [r_tp.end_consume()])

            emit_qk_chunk(j, t0, k_out)

        pl.wait("tensor", "h_done")
        pl.wait("tensor", "r_tp_c")
        pl.wait("tensor", "p1_kva")
        for h in range(NH):
            for s in range(4):
                incs = [("p1_kv", 1)] if (s == 3 and h == NH - 1) else []
                pl.emit("tensor", lambda e, h=h, s=s: e.matmul(
                    PSE[0:64, h * 64:(h + 1) * 64],
                    KRT[:, s, h * 64:(h + 1) * 64], VT[:, s, h * 64:(h + 1) * 64],
                    start=(s == 0), stop=(s == 3)), incs)
        pl.wait("vector", "p1_kv")
        pl.wait("vector", "s_mset", 4)
        pl.emit("vector", lambda e: e.scalar_tensor_tensor(
            KVS[:, :], PSE[0:64, 0:1024], 1.0 / SEQ, KVS[:, :], ALU.mult, ALU.add),
            [("p1_kva", 1)])

    pl.wait("gpsimd", "p1_kva")
    pl.emit("gpsimd", lambda e: e.dma_start(d_ckv_i[:, :], KVS[:, :]), [("s_ccst", 16)])
    pl.wait("gpsimd", "p1_km", 32)
    pl.emit("gpsimd", lambda e: e.dma_start(d_ckm_i[:, :], KMEAN[:, :]), [("s_ccst", 16)])
    pl.wait("gpsimd", "s_ccst")
    pl.emit("gpsimd", lambda e: e.collective_compute(
        "AllReduce", ALU.add, GROUPS, [d_ckv_i[:, :]], [d_ckv_o[:, :]]), [("s_cc", 1)])
    pl.emit("gpsimd", lambda e: e.collective_compute(
        "AllReduce", ALU.add, GROUPS, [d_ckm_i[:, :]], [d_ckm_o[:, :]]), [("s_cc", 1)])
    pl.wait("gpsimd", "s_cc")
    ckv_he = d_ckv_o.rearrange("d (hh two e) -> d hh two e", two=2, e=64)
    pl.emit("gpsimd", lambda e: e.dma_start(
        KVGB[0:64, :].rearrange("p (hh e) -> p hh e", e=64), ckv_he[:, :, 0, :]),
        [("s_ccl", 16)])
    pl.emit("gpsimd", lambda e: e.dma_start(
        KVGB[64:128, :].rearrange("p (hh e) -> p hh e", e=64), ckv_he[:, :, 1, :]),
        [("s_ccl", 16)])
    pl.emit("gpsimd", lambda e: e.dma_start(KMG[:, :], d_ckm_o[:, :]), [("s_ccl", 16)])

    pl.wait("vector", "s_ccst", 16)
    for tt in range(NTT):
        t0 = tt * TT
        for c in range(NCH):
            pl.emit("vector", lambda e, c=c, t0=t0: e.tensor_scalar_mul(
                TN1[:, 0:TT], XLN[:, c, t0:t0 + TT], LEPW[:, 3 * c:3 * c + 1]))
            pl.emit("vector", lambda e, c=c, t0=t0: e.scalar_tensor_tensor(
                TN1[:, 0:TT], XLN[:, c, 1 + t0:1 + t0 + TT], LEPW[:, 3 * c + 1:3 * c + 2],
                TN1[:, 0:TT], ALU.mult, ALU.add))
            pl.emit("vector", lambda e, c=c, t0=t0: e.scalar_tensor_tensor(
                RES[:, c, t0:t0 + TT], XLN[:, c, 2 + t0:2 + t0 + TT],
                LEPW[:, 3 * c + 2:3 * c + 3], TN1[:, 0:TT], ALU.mult, ALU.add),
                [("p1_conv", 1)])

    pl.wait("scalar", "s_ccl", 48)
    for c in range(NCH):
        pl.emit("scalar", lambda e, c=c: e.activation(
            KMM[0:64, c, 2 * c:2 * c + 1], KMG[0:64, c:c + 1], AF.Copy, scale=CSTS[0:64, 0:1]))
        pl.emit("scalar", lambda e, c=c: e.activation(
            KMM[64:128, c, 2 * c + 1:2 * c + 2], KMG[64:128, c:c + 1], AF.Copy,
            scale=CSTS[64:128, 0:1]), [("p1_kmm", 1)] if c == NCH - 1 else [])

    for tt in range(NTT):
        t0 = tt * TT
        for j in range(8):
            def q_out(el, ps, j=j):
                pl.emit("vector", lambda e, el=el, j=j: e.tensor_add(
                    QTI[:, j, 0:TT], TN1[:, 0:TT], ELU[el][:, 0:TT]),
                    [("p2_q", 1)])
                pl.emit("vector", lambda e: e.engine_nop(), [r_mm.end_consume()])

            pl.wait("vector", "s_ccst", 16)
            emit_qk_chunk(j, t0, q_out)
        pl.wait("tensor", "p2_q")
        pl.wait("tensor", "p1_kmm")
        pl.wait("tensor", "p1_kva")
        pl.wait("tensor", "p2_zcons")
        for c in range(NCH):
            pl.emit("tensor", lambda e, c=c: e.matmul(
                PSE[0:NH, 0:TT], KMM[:, c, :], QTI[:, c, 0:TT],
                start=(c == 0), stop=(c == NCH - 1)),
                [("p2_zpe", 1)] if c == NCH - 1 else [])
        pl.wait("vector", "p2_zpe")
        pl.emit("vector", lambda e: e.tensor_scalar_add(ZROW[:, :], PSE[0:NH, 0:TT], 1e-6),
                [("p2_zcons", 1)])
        pl.emit("vector", lambda e: e.reciprocal(ZROW[:, :], ZROW[:, :]))
        pl.emit("vector", lambda e: e.tensor_copy(ZROWB[:, :], ZROW[:, :]), [("p2_z", 1)])
        pl.wait("tensor", "p2_z")
        pl.wait("tensor", "s_ccl", 48)
        pl.wait("vector", "p1_conv")
        for c in range(NCH):
            r_zb.begin_produce("tensor")
            pl.emit("tensor", lambda e, c=c: e.matmul(
                PSE[:, 512:512 + TT], HMB[:, c * 128:(c + 1) * 128], ZROWB[:, 0:TT],
                start=True, stop=True), [r_zb.end_produce()])
            pl.emit("vector", lambda e, c=c: e.stream_shuffle(
                SWP[:, 0:TT], QTI[:, c, 0:TT], PAIRSWAP))
            pl.emit("vector", lambda e, c=c: e.tensor_scalar_mul(
                RT4[:, 0:TT], QTI[:, c, 0:TT], COS[:, c:c + 1]))
            pl.emit("vector", lambda e, c=c: e.scalar_tensor_tensor(
                TN1[:, 0:TT], SWP[:, 0:TT], SIN[:, c:c + 1], RT4[:, 0:TT],
                ALU.mult, ALU.add))
            r_zb.begin_consume("vector")
            pl.emit("vector", lambda e, c=c: e.tensor_tensor(
                QRTI[:, c, 0:TT], TN1[:, 0:TT], PSE[:, 512:512 + TT], ALU.mult),
                [r_zb.end_consume()])
            pl.wait("tensor", "r_zb_c")
            mslot = r_mm.begin_produce("tensor")
            ps = PSMM[mslot]
            pl.emit("tensor", lambda e, c=c, ps=ps: e.matmul(
                ps[0:64, 0:TT], KVGB[0:64, c * 64:(c + 1) * 64],
                QRTI[0:64, c, 0:TT], start=True, stop=True))
            pl.emit("tensor", lambda e, c=c, ps=ps: e.matmul(
                ps[64:128, 0:TT], KVGB[64:128, c * 64:(c + 1) * 64],
                QRTI[64:128, c, 0:TT], start=True, stop=True), [r_mm.end_produce()])
            msl = r_mm.begin_consume("vector")
            pl.emit("vector", lambda e, c=c, t0=t0, msl=msl: e.scalar_tensor_tensor(
                RES[:, c, t0:t0 + TT], PSMM[msl][:, 0:TT], LEPB[:, c:c + 1],
                RES[:, c, t0:t0 + TT], ALU.add, ALU.add),
                [r_mm.end_consume()])
        pl.emit("vector", lambda e: e.engine_nop(), [("p2_res", 1)])
        ln_stats(lambda c, t0=t0: RES[:, c, t0:t0 + TT], TT, src_ready="p2_res")
        for c in range(NCH):
            pl.emit("vector", lambda e, c=c, t0=t0: e.tensor_sub(
                TN1[:, 0:TT], RES[:, c, t0:t0 + TT], PSC[:, 0:TT]))
            pl.emit("vector", lambda e: e.tensor_tensor(
                TN1[:, 0:TT], TN1[:, 0:TT], PSD[:, 0:TT], ALU.mult), [("p2_pn", 1)])
            pl.emit("vector", lambda e, c=c, t0=t0: e.tensor_add(
                X[:, c, t0:t0 + TT], X[:, c, t0:t0 + TT], TN1[:, 0:TT]), [("p2_x2", 1)])
        lns_done("p2_pn")
        pl.wait("sync", "p2_x2")
        pl.emit("sync", lambda e, t0=t0: e.dma_start(
            d_x2[:, t0:t0 + TT].rearrange("(c p) t -> p c t", p=128),
            X[:, :, t0:t0 + TT]), [("s_x2p", 16)])
        if BUILD_PHASE >= 2:
            ln_stats(lambda c, t0=t0: X[:, c, t0:t0 + TT], TT, src_ready="p2_x2")
            pl.wait("vector", "s_x2p")
            for c in range(NCH):
                pl.emit("vector", lambda e, c=c, t0=t0: e.tensor_sub(
                    TN1[:, 0:TT], X[:, c, t0:t0 + TT], PSC[:, 0:TT]))
                pl.emit("vector", lambda e, c=c, t0=t0: e.tensor_tensor(
                    X[:, c, t0:t0 + TT], TN1[:, 0:TT], PSD[:, 0:TT], ALU.mult),
                    [("p2_y0", 1)])
            lns_done("p2_y0")

    if BUILD_PHASE == 1:
        pl.wait("sync", "p2_x2")
        pl.emit("sync", lambda e: e.dma_start(
            dout.rearrange("(c p) t -> p c t", p=128), X[:, :, :]), [("s_out", 16)])
        pl.wait("sync", "s_out")


    if BUILD_PHASE >= 2:
        for tt in range(NTT):
            t0 = tt * TT
            ln_stats(lambda c, t0=t0: X[:, c, t0:t0 + TT], TT, src_ready="p2_y0")
            for c in range(NCH):
                pl.wait("vector", "p2_xn1")
                pl.emit("vector", lambda e, c=c, t0=t0: e.tensor_sub(
                    TN1[:, 0:TT], X[:, c, t0:t0 + TT], PSC[:, 0:TT]))
                pl.emit("vector", lambda e: e.tensor_tensor(
                    TN1[:, 0:TT], TN1[:, 0:TT], PSD[:, 0:TT], ALU.mult), [("p2_ln1", 1)])
                pl.wait("scalar", "p2_ln1")
                pl.emit("scalar", lambda e, c=c: e.activation(
                    XN1T[:, c, 0:TT], TN1[:, 0:TT], AF.Identity,
                    scale=LN1W[:, c:c + 1], bias=LN1B[:, c:c + 1]), [("p2_xn1", 1)])
            lns_done("p2_ln1")
            for jc in range(24):
                wslot = r_wqk.begin_produce("gpsimd")
                pl.emit("gpsimd", lambda e, jc=jc, wslot=wslot: e.dma_start(
                    WQK_SB[wslot][:, :, :],
                    dwqkv[:, jc * 128:(jc + 1) * 128].rearrange("(c p) j -> p c j", p=128)),
                    [r_wqk.end_produce()])
                wsl = r_wqk.begin_consume("tensor")
                pl.wait("tensor", "p2_xn1")
                mslot = r_mm.begin_produce("tensor")
                for c in range(NCH):
                    incs = []
                    if c == NCH - 1:
                        incs = [r_mm.end_produce(), r_wqk.end_consume()]
                    pl.emit("tensor", lambda e, c=c, wsl=wsl, mslot=mslot: e.matmul(
                        PSMM[mslot][:, 0:TT], WQK_SB[wsl][:, c, :],
                        XN1T[:, c, 0:TT], start=(c == 0), stop=(c == NCH - 1)), incs)
                msl = r_mm.begin_consume("scalar")
                ps = PSMM[msl]
                if jc < 8:
                    pl.emit("scalar", lambda e, jc=jc, t0=t0, ps=ps: e.activation(
                        QF[:, jc, t0:t0 + TT], ps[:, 0:TT], AF.Copy,
                        scale=CSTS[:, 2:3]), [r_mm.end_consume()])
                elif jc < 16:
                    pl.emit("scalar", lambda e, jc=jc, t0=t0, ps=ps: e.activation(
                        KL[:, jc - 8, t0:t0 + TT], ps[:, 0:TT], AF.Copy),
                        [r_mm.end_consume()])
                else:
                    vslot = r_vch.begin_produce("scalar")
                    pl.emit("scalar", lambda e, ps=ps, vslot=vslot: e.activation(
                        VCH[vslot][:, 0:TT], ps[:, 0:TT], AF.Copy),
                        [r_mm.end_consume()])
                    r_vch.np_ += 1
                    vch_ready = pl.val("r_mm_c")
                    vsl = r_vch.nc_ % 2
                    r_vch.nc_ += 1
                    pl.wait("tensor", "r_mm_c", vch_ready)
                    h0 = 2 * (jc - 16)
                    for s in range(4):
                        ts_ = tt * 4 + s
                        tsl = r_tp.begin_produce("tensor")
                        incs = [r_tp.end_produce()]
                        if s == 3:
                            incs.append(("r_vch_c", 1))
                        pl.emit("tensor", lambda e, s=s, vsl=vsl, tsl=tsl: e.transpose(
                            PST[:, tsl * 1024:tsl * 1024 + 128],
                            VCH[vsl][:, s * 128:(s + 1) * 128], IDB[:, :]), incs)
                        tc = r_tp.begin_consume("scalar")
                        pl.emit("scalar", lambda e, h0=h0, ts_=ts_, tc=tc: e.activation(
                            VL[:, h0, ts_, 0:64],
                            PST[:, tc * 1024:tc * 1024 + 64], AF.Copy))
                        pl.emit("scalar", lambda e, h0=h0, ts_=ts_, tc=tc: e.activation(
                            VL[:, h0 + 1, ts_, 0:64],
                            PST[:, tc * 1024 + 64:tc * 1024 + 128], AF.Copy),
                            [r_tp.end_consume()])
        QF_DONE = pl.val("r_mm_c")
        VL_TPC = pl.val("r_tp_c")

        pl.wait("gpsimd", "p2_pn")
        pl.wait("gpsimd", "r_tp_c", VL_TPC)
        for h in range(NH):
            pl.emit("gpsimd", lambda e, h=h: e.memset(VL[:, h, :, 64:65], 1.0),
                    [("s_mset", 1)])
        N_MSET_VL = pl.val("s_mset")

        pl.wait("sync", "r_mm_c", QF_DONE)
        pl.emit("sync", lambda e: e.dma_start(
            d_kg_i.rearrange("(c p) t -> p c t", p=128), KL[:, :, :]), [("s_kg", 16)])
        pl.wait("sync", "r_tp_c", VL_TPC)
        pl.wait("sync", "s_mset", N_MSET_VL)
        pl.emit("sync", lambda e: e.dma_start(
            d_vg_i.rearrange("(h ts p) e -> p h ts e", p=128, ts=16), VL[:, :, :, :]),
            [("s_vg", 16)])
        pl.wait("gpsimd", "s_vg")
        pl.emit("gpsimd", lambda e: e.memset(KAUG[0][64:65, :], 1.0), [("s_mset", 1)])
        pl.emit("gpsimd", lambda e: e.memset(KAUG[1][64:65, :], 1.0), [("s_mset", 1)])
        pl.emit("gpsimd", lambda e: e.memset(QAUG[0][64:65, :], 0.0), [("s_mset", 1)])
        pl.emit("gpsimd", lambda e: e.memset(QAUG[1][64:65, :], 0.0), [("s_mset", 1)])
        N_MSET2 = pl.val("s_mset")
        pl.wait("gpsimd", "s_kg")
        pl.wait("gpsimd", "s_vg")
        pl.emit("gpsimd", lambda e: e.collective_compute(
            "AllGather", ALU.bypass, GROUPS, [d_kg_i[:, :]], [d_kg_o[:, :]]),
            [("s_cc2", 1)])
        pl.emit("gpsimd", lambda e: e.collective_compute(
            "AllGather", ALU.bypass, GROUPS, [d_vg_i[:, :]], [d_vg_o[:, :]]),
            [("s_cc2", 1)])

        st_mark = {}
        av_mark = {}
        hd_mark = {}

        def emit_load(h):
            buf = h % 2
            cQ = h // 2
            rb = (h % 2) * 64
            pl.wait("sync", "s_cc2", 2)
            if h >= 2:
                pl.wait("sync", "r_mm2_p", st_mark[h - 2])
                pl.wait("sync", "b_av", av_mark[h - 2])
            for r in range(4):
                pl.emit("sync", lambda e, h=h, r=r, buf=buf: e.dma_start(
                    KAUG[buf][0:64, r * T:(r + 1) * T],
                    d_kg_o[r * C + 64 * h: r * C + 64 * h + 64, :]), [("s_hd", 16)])
            for r in range(4):
                pl.emit("sync", lambda e, h=h, r=r, buf=buf: e.dma_start(
                    VH[buf][:, r * 16:(r + 1) * 16, :],
                    d_vg_o[(r * 16 + h) * T:(r * 16 + h + 1) * T, :]
                    .rearrange("(j p) e -> p j e", p=128)), [("s_hd", 16)])
            hd_mark[h] = pl.val("s_hd")
            pl.wait("vector", "r_mm_c", QF_DONE)
            if h >= 2:
                pl.wait("vector", "r_mm2_p", st_mark[h - 2])
            pl.emit("vector", lambda e, buf=buf, cQ=cQ, rb=rb: e.tensor_copy(
                QAUG[buf][0:64, :], QF[rb:rb + 64, cQ, :]), [("a_qb", 1)])

        def emit_B(h):
            buf = h % 2
            cQ = h // 2
            rb = (h % 2) * 64
            pl.wait("tensor", "a_qb", h + 1)
            pl.wait("tensor", "s_mset", N_MSET2)
            pl.wait("tensor", "s_hd", hd_mark[h])
            for qt4 in range(4):
                pl.wait("tensor", "b_fin")
                pend = None
                for kc in range(65):
                    if kc < 64:
                        m2 = r_mm2.begin_produce("tensor")
                        pl.emit("tensor", lambda e, buf=buf, qt4=qt4, kc=kc, m2=m2: e.matmul(
                            PSMM2[m2][:, 0:TT], KAUG[buf][:, kc * 128:(kc + 1) * 128],
                            QAUG[buf][:, qt4 * TT:(qt4 + 1) * TT],
                            start=True, stop=True), [r_mm2.end_produce()])
                        m2c = r_mm2.begin_consume("scalar")
                        pl.wait("scalar", "b_av", pl.val("b_av"))
                        pslot = pl.val("r_mm2_c") % 2
                        pl.emit("scalar", lambda e, m2c=m2c, pslot=pslot: e.activation(
                            PT[pslot][:, 0:TT], PSMM2[m2c][:, 0:TT], AF.Exp),
                            [r_mm2.end_consume()])
                        nxt = (kc, pslot, pl.val("r_mm2_c"))
                    if pend is not None:
                        kcp, pslotp, expv = pend
                        pl.wait("tensor", "r_mm2_c", expv)
                        pl.emit("tensor", lambda e, buf=buf, kcp=kcp, pslotp=pslotp: e.matmul(
                            PSE[0:65, 0:TT], VH[buf][:, kcp, :], PT[pslotp][:, 0:TT],
                            start=(kcp == 0), stop=(kcp == 63)), [("b_av", 1)])
                    if kc < 64:
                        pend = nxt
                pl.wait("vector", "b_av")
                pl.emit("vector", lambda e: e.reciprocal(RL[:, :], PSE[64:65, 0:TT]),
                        [("b_rl", 1)])
                pl.wait("scalar", "b_rl")
                pl.emit("scalar", lambda e: e.activation(RLB[:, :], RL[:, :], AF.Copy),
                        [("b_rlb", 1)])
                pl.emit("scalar", lambda e: e.activation(OT16[:, :], PSE[0:64, 0:TT],
                                                         AF.Copy), [("b_ot", 1)])
                pl.wait("tensor", "b_rlb")
                pl.emit("tensor", lambda e: e.matmul(
                    PSE[0:64, 512:1024], ONESB[0:1, 0:64], RLB[0:1, :],
                    start=True, stop=True), [("b_lb", 1)])
                pl.wait("vector", "b_lb")
                pl.wait("vector", "b_ot")
                pl.emit("vector", lambda e, rb=rb, cQ=cQ, qt4=qt4: e.tensor_tensor(
                    AOUT[rb:rb + 64, cQ, qt4 * TT:(qt4 + 1) * TT], OT16[:, :],
                    PSE[0:64, 512:1024], ALU.mult), [("b_fin", 1)])
            st_mark[h] = pl.val("r_mm2_p")
            av_mark[h] = pl.val("b_av")

        pl.wait("tensor", "p2_ln1")

        for h in range(NH):
            emit_load(h)
            if h >= 1:
                emit_B(h - 1)
        emit_B(NH - 1)

        AOUT_DONE = pl.val("b_fin")
        pl.wait("gpsimd", "b_av", av_mark[NH - 1])
        for tt in range(NTT):
            t0 = tt * TT
            for oc in range(NCH):
                wslot = r_wqk.begin_produce("gpsimd")
                pl.emit("gpsimd", lambda e, oc=oc, wslot=wslot: e.dma_start(
                    WQK_SB[wslot][:, :, :],
                    dwproj[:, oc * 128:(oc + 1) * 128].rearrange("(c p) j -> p c j", p=128)),
                    [r_wqk.end_produce()])
                wsl = r_wqk.begin_consume("tensor")
                pl.wait("tensor", "b_fin", AOUT_DONE)
                mslot = r_mm.begin_produce("tensor")
                for c in range(NCH):
                    incs = []
                    if c == NCH - 1:
                        incs = [r_mm.end_produce(), r_wqk.end_consume()]
                    pl.emit("tensor", lambda e, c=c, wsl=wsl, mslot=mslot, t0=t0: e.matmul(
                        PSMM[mslot][:, 0:TT], WQK_SB[wsl][:, c, :],
                        AOUT[:, c, t0:t0 + TT], start=(c == 0), stop=(c == NCH - 1)), incs)
                msl = r_mm.begin_consume("vector")
                pl.emit("vector", lambda e, oc=oc, t0=t0, msl=msl: e.tensor_add(
                    Y1T[:, oc, t0:t0 + TT], Y1T[:, oc, t0:t0 + TT], PSMM[msl][:, 0:TT]),
                    [r_mm.end_consume()])
        Y1_DONE = pl.val("r_mm_c")

    if BUILD_PHASE == 2:
        pl.wait("sync", "r_mm_c", Y1_DONE)
        pl.emit("sync", lambda e: e.dma_start(
            dout.rearrange("(c p) t -> p c t", p=128), Y1T[:, :, :]), [("s_out", 16)])
        pl.wait("sync", "s_out")


    if BUILD_PHASE >= 3:
        for tt in range(NTT):
            t0 = tt * TT
            pl.wait("tensor", "p3_msk")
            ln_stats(lambda c, t0=t0: Y1T[:, c, t0:t0 + TT], TT, src_ready="r_mm_c")
            for c in range(NCH):
                pl.wait("vector", "p3_xnf")
                pl.emit("vector", lambda e, c=c, t0=t0: e.tensor_sub(
                    TN1[:, 0:TT], Y1T[:, c, t0:t0 + TT], PSC[:, 0:TT]))
                pl.emit("vector", lambda e: e.tensor_tensor(
                    TN1[:, 0:TT], TN1[:, 0:TT], PSD[:, 0:TT], ALU.mult), [("p3_ln2", 1)])
                pl.wait("scalar", "p3_ln2")
                pl.wait("scalar", "p3_gmm")
                pl.emit("scalar", lambda e, c=c: e.activation(
                    XNF[:, 0:TT], TN1[:, 0:TT], AF.Identity,
                    scale=LN2W[:, c:c + 1], bias=LN2B[:, c:c + 1]), [("p3_xnf", 1)])
                pl.wait("scalar", "s_xnp")
                pl.emit("scalar", lambda e: e.activation(
                    XNB[:, 0:TT], XNF[:, 0:TT], AF.Copy), [("p3_xnb", 1)])
                pl.wait("sync", "p3_xnb")
                pl.emit("sync", lambda e, c=c, t0=t0: e.dma_start(
                    d_xn2[c * 128:(c + 1) * 128, t0:t0 + TT], XNB[:, 0:TT]),
                    [("s_xnp", 16)])
                pl.wait("tensor", "p3_xnf")
                pl.wait("tensor", "p3_lg")
                pl.emit("tensor", lambda e, c=c: e.matmul(
                    PSE[0:NE, 0:TT], GATEW[:, c, :], XNF[:, 0:TT],
                    start=(c == 0), stop=(c == NCH - 1)), [("p3_gmm", 1)])
            lns_done("p3_ln2")
            pl.wait("scalar", "p3_gmm")
            pl.emit("scalar", lambda e: e.activation(
                LG[:, 0:TT], PSE[0:NE, 0:TT], AF.Identity, bias=GATEB[:, 0:1]),
                [("p3_lg", 1)])
            for s in range(4):
                ts_ = tt * 4 + s
                pl.wait("tensor", "p3_lg")
                pl.wait("tensor", "p3_msk")
                pl.emit("tensor", lambda e, s=s: e.transpose(
                    PSB[:, 0:8], LG[0:8, s * 128:(s + 1) * 128], IDF[0:8, 0:8]),
                    [("p3_gt", 1)])
                pl.wait("vector", "p3_gt")
                pl.emit("vector", lambda e: e.reduce_max(MX[:, :], PSB[:, 0:8], axis=AX.X))
                pl.emit("vector", lambda e, ts_=ts_: e.tensor_scalar(
                    MASKTM[:, ts_, :], PSB[:, 0:8], MX[:, 0:1], None, ALU.is_ge),
                    [("p3_msk", 1)])
            pl.wait("scalar", "b_fin", AOUT_DONE)
            for c in range(NCH):
                pl.emit("scalar", lambda e, c=c, t0=t0: e.activation(
                    YACC[:, c, t0:t0 + TT], Y1T[:, c, t0:t0 + TT], AF.Copy),
                    [("p3_yi", 1)])

        m1t_idx = [0]
        m2t_idx = [0]
        for e_ in range(NE):
            if e_ == 0:
                pl.wait("gpsimd", "p3_yi", 32)
            pl.wait("gpsimd", "p3_m1d", e_)
            pl.emit("gpsimd", lambda e, e_=e_: e.dma_start(
                W1SB[:, :, :], dw1[e_, :, :].rearrange("(c p) j -> p c j", p=128)),
                [("s_w1", 16)])
            if e_ == 0:
                pl.wait("gpsimd", "r_mm_c", Y1_DONE)
            pl.wait("gpsimd", "p3_m2d", e_)
            pl.emit("gpsimd", lambda e, e_=e_: e.dma_start(
                W2SB[:, :, :], dw2[e_, :, :].rearrange("(j p) o -> p j o", p=128)),
                [("s_w2", 16)])
            for mt in range(NMTT):
                blk = e_ * NMTT + mt
                if blk > 0:
                    pl.wait("sync", "p3_m1t", blk)
                pl.wait("sync", "s_xnp")
                pl.emit("sync", lambda e, mt=mt: e.dma_start(
                    XNT[:, :, :],
                    d_xn2[:, mt * MTT:(mt + 1) * MTT].rearrange("(c p) t -> p c t", p=128)),
                    [("s_xnl", 16)])
                XNT_RDY = pl.val("s_xnl")
                pl.wait("tensor", "s_w1", 16 * (e_ + 1))
                pl.wait("tensor", "s_xnl", XNT_RDY)
                pl.wait("scalar", "p3_m2t", blk)
                for j in range(32):
                    mslot = r_mm.begin_produce("tensor")
                    for c in range(NCH):
                        incs = []
                        if c == NCH - 1:
                            incs = [r_mm.end_produce()]
                            if j == 31:
                                incs.append(("p3_m1t", 1))
                                if mt == NMTT - 1:
                                    incs.append(("p3_m1d", 1))
                        pl.emit("tensor", lambda e, c=c, j=j, mslot=mslot: e.matmul(
                            PSMM[mslot][:, 0:MTT], W1SB[:, c, j * 128:(j + 1) * 128],
                            XNT[:, c, 0:MTT], start=(c == 0), stop=(c == NCH - 1)), incs)
                    msl = r_mm.begin_consume("scalar")
                    pl.emit("scalar", lambda e, j=j, msl=msl: e.activation(
                        HSB[:, j, 0:MTT], PSMM[msl][:, 0:MTT], AF.Gelu_apprx_tanh),
                        [r_mm.end_consume()])
                HSB_RDY = pl.val("r_mm_c")
                pl.wait("tensor", "p3_msk", 4 * NTT)
                for half in range(2):
                    ts_ = mt * 2 + half
                    pl.wait("tensor", "p3_mr")
                    pl.emit("tensor", lambda e, ts_=ts_, e_=e_: e.matmul(
                        PSE[0:1, 512:640], MASKTM[:, ts_, e_:e_ + 1], IDF[:, :],
                        start=True, stop=True), [("p3_mt", 1)])
                    pl.wait("scalar", "p3_mt")
                    pl.emit("scalar", lambda e, half=half: e.activation(
                        MROWE[0:1, half * 128:(half + 1) * 128], PSE[0:1, 512:640],
                        AF.Copy), [("p3_mr", 1)])
                pl.wait("tensor", "p3_mr")
                pl.wait("tensor", "p3_mbs")
                pl.emit("tensor", lambda e: e.matmul(
                    PSE[:, 0:MTT], ONESB[0:1, :], MROWE[0:1, 0:MTT],
                    start=True, stop=True), [("p3_mb", 1)])
                pl.wait("scalar", "p3_mb")
                pl.wait("scalar", "p3_moe")
                pl.emit("scalar", lambda e: e.activation(
                    MBSB[:, 0:MTT], PSE[:, 0:MTT], AF.Copy), [("p3_mbs", 1)])
                pl.wait("tensor", "s_w2", 16 * (e_ + 1))
                pl.wait("tensor", "r_mm_c", HSB_RDY)
                for o in range(NCH):
                    m2 = r_mm2.begin_produce("tensor")
                    for j in range(32):
                        incs = []
                        if j == 31:
                            incs = [r_mm2.end_produce()]
                            if o == NCH - 1:
                                incs.append(("p3_m2t", 1))
                                if mt == NMTT - 1:
                                    incs.append(("p3_m2d", 1))
                        pl.emit("tensor", lambda e, o=o, j=j, m2=m2: e.matmul(
                            PSMM2[m2][:, 0:MTT], W2SB[:, j, o * 128:(o + 1) * 128],
                            HSB[:, j, 0:MTT], start=(j == 0), stop=(j == 31)), incs)
                    m2c = r_mm2.begin_consume("vector")
                    pl.wait("vector", "p3_mbs")
                    pl.emit("vector", lambda e, m2c=m2c: e.tensor_tensor(
                        T1M[:, 0:MTT], PSMM2[m2c][:, 0:MTT], MBSB[:, 0:MTT], ALU.mult),
                        [r_mm2.end_consume()])
                    pl.emit("vector", lambda e, o=o, mt=mt: e.tensor_add(
                        YACC[:, o, mt * MTT:(mt + 1) * MTT],
                        YACC[:, o, mt * MTT:(mt + 1) * MTT], T1M[:, 0:MTT]),
                        [("p3_moe", 1)])

        for tt in range(NTT):
            t0 = tt * TT
            pl.wait("sync", "p3_m1d", NE)
            pl.wait("sync", "p3_out")
            pl.emit("sync", lambda e, t0=t0: e.dma_start(
                X2T[:, :, :],
                d_x2[:, t0:t0 + TT].rearrange("(c p) t -> p c t", p=128)),
                [("s_x2r", 16)])
            ln_stats(lambda c, t0=t0: YACC[:, c, t0:t0 + TT], TT, src_f32=False,
                     src_ready="p3_moe")
            pl.wait("vector", "s_x2r")
            pl.wait("vector", "s_out")
            for c in range(NCH):
                pl.emit("vector", lambda e, c=c, t0=t0: e.tensor_sub(
                    TN1[:, 0:TT], YACC[:, c, t0:t0 + TT], PSC[:, 0:TT]))
                incs = [("p3_fn", 1)] if c == NCH - 1 else []
                pl.emit("vector", lambda e: e.tensor_tensor(
                    TN1[:, 0:TT], TN1[:, 0:TT], PSD[:, 0:TT], ALU.mult), incs)
                pl.emit("vector", lambda e, c=c: e.tensor_add(
                    OUTT[:, c, 0:TT], X2T[:, c, 0:TT], TN1[:, 0:TT]), [("p3_out", 1)])
            lns_done("p3_fn")
            pl.wait("sync", "p3_out")
            pl.emit("sync", lambda e, t0=t0: e.dma_start(
                dout[:, t0:t0 + TT].rearrange("(c p) t -> p c t", p=128),
                OUTT[:, :, :]), [("s_out", 16)])
        pl.wait("sync", "s_out")

    emit(nc, pl)
    return nc


def emit(nc, pl):
    with ExitStack() as st:
        sems = {n: st.enter_context(nc.semaphore(n)) for n in pl.cnt}
        nums = sorted(s.num for s in sems.values())
        assert nums[-1] - nums[0] + 1 == len(nums), "sem numbers not contiguous"
        clear_range = range(nums[0], nums[-1] + 1)
        with nc.Block() as block0:
            @block0.gpsimd
            def _(g):
                g.dma_reset(clear_range)
                g.sem_clear(clear_range)
        nc.all_engine_barrier()
        with nc.Block() as block:
            for eng_name, deco in (("tensor", block.tensor), ("vector", block.vector),
                                   ("scalar", block.scalar), ("gpsimd", block.gpsimd),
                                   ("sync", block.sync)):
                def runner(e, eng_name=eng_name):
                    for item in pl.ops[eng_name]:
                        if item[0] == "wait":
                            e.wait_ge(sems[item[1]], item[2])
                        else:
                            ins = item[1](e)
                            incs = item[2]
                            if eng_name != "tensor":
                                assert len(incs) <= 1, (eng_name, incs)
                            if len(incs) > 1:
                                ins.then_inc(sems[incs[0][0]], incs[0][1])
                                for s, a in incs[1:]:
                                    e.nop().then_inc(sems[s], a)
                            else:
                                for s, a in incs:
                                    ins.then_inc(sems[s], a)
                deco(runner)



def _stage(inputs):
    x = np.asarray(inputs["L0_attn_x"])
    W_qk = np.asarray(inputs["L0_attn_W_qk"])
    b_qk = np.asarray(inputs["L0_attn_b_qk"])
    lepe_w = np.asarray(inputs["L0_attn_lepe_w"])
    lepe_b = np.asarray(inputs["L0_attn_lepe_b"])
    ln1w = np.asarray(inputs["L0_ffn_ln1_w"]); ln1b = np.asarray(inputs["L0_ffn_ln1_b"])
    ln2w = np.asarray(inputs["L0_ffn_ln2_w"]); ln2b = np.asarray(inputs["L0_ffn_ln2_b"])
    wqkv = np.asarray(inputs["L0_ffn_w_qkv"]); wproj = np.asarray(inputs["L0_ffn_w_proj"])
    gatew = np.asarray(inputs["L0_ffn_gate_w"]); gateb = np.asarray(inputs["L0_ffn_gate_b"])
    w1 = np.asarray(inputs["L0_ffn_w1"]); w2 = np.asarray(inputs["L0_ffn_w2"])

    k_max = C // 2
    theta = 1.0 / (10000.0 ** (np.arange(k_max, dtype=np.float64) / k_max))
    cosv = np.repeat(np.cos(theta), 2).astype(np.float32)
    sinv = np.zeros(C, dtype=np.float32)
    sinv[0::2] = -np.sin(theta)
    sinv[1::2] = np.sin(theta)

    def colview(v):
        return np.ascontiguousarray(np.asarray(v, dtype=np.float32).reshape(NCH, 128).T)

    hmask = np.zeros((NCH, NH, 128), dtype=np.float32)
    for c in range(NCH):
        hmask[c, 2 * c, 0:64] = 1.0
        hmask[c, 2 * c + 1, 64:128] = 1.0

    shared = {
        "wqkT": np.ascontiguousarray(W_qk.T),
        "bqk": np.ascontiguousarray(b_qk.reshape(16, 128).T),
        "ropecos": colview(cosv), "ropesin": colview(sinv),
        "lepew": np.ascontiguousarray(
            lepe_w[:, 0, :].reshape(NCH, 128, 3).transpose(1, 0, 2).reshape(128, NCH * 3)),
        "lepeb": colview(lepe_b),
        "ln1w": colview(ln1w), "ln1b": colview(ln1b),
        "ln2w": colview(ln2w), "ln2b": colview(ln2b),
        "wqkv": np.ascontiguousarray(wqkv), "wproj": np.ascontiguousarray(wproj),
        "gatew": np.ascontiguousarray(gatew),
        "gateb": np.ascontiguousarray(np.asarray(gateb).reshape(NE, 1)),
        "w1": np.ascontiguousarray(w1), "w2": np.ascontiguousarray(w2),
        "ident": np.eye(128, dtype=np.float32),
        "csts": np.broadcast_to(np.array(
            [1.0 / SEQ, -1.0, 0.125, 0.5, EPS, 0.0, 0.0, 0.0], dtype=np.float32),
            (128, 8)).copy(),
        "hmask": hmask,
    }
    in_maps = []
    for core in range(NCORES):
        b = core // 4
        t0 = (core % 4) * T
        m = dict(shared)
        m["x_fm"] = np.ascontiguousarray(x[b, t0:t0 + T].T)
        halo = np.zeros((C, 2), dtype=np.float32)
        if t0 > 0:
            halo[:, 0] = x[b, t0 - 1]
        if t0 + T < SEQ:
            halo[:, 1] = x[b, t0 + T]
        m["x_halo"] = halo
        in_maps.append(m)
    return in_maps


def kernel(**inputs):
    in_maps = _stage(inputs)
    nc = build_program()
    res = run_bass_kernel_spmd(nc, in_maps, core_ids=list(range(NCORES)))
    out = np.empty((NB, SEQ, C), dtype=np.float32)
    for core in range(NCORES):
        b = core // 4
        t0 = (core % 4) * T
        out[b, t0:t0 + T] = res.results[core]["out_fm"].T
    return out


# version 07
# speedup vs baseline: 2.0303x; 1.1343x over previous
from contextlib import ExitStack

import numpy as np
import concourse.bass as bass
import concourse.mybir as mybir
from concourse.bass_utils import run_bass_kernel_spmd

F32 = mybir.dt.float32
BF16 = mybir.dt.bfloat16
AF = mybir.ActivationFunctionType
ALU = mybir.AluOpType
AX = mybir.AxisListType

NCORES = 8
NB = 2
SEQ = 8192
T = 2048
C = 1024
NCH = 8
NH = 16
DK = 64
TT = 512
NTT = T // TT
HID = 4096
NE = 8
MTT = 256
NMTT = T // MTT
EPS = 1e-5
GROUPS = [[0, 1, 2, 3], [4, 5, 6, 7]]
PAIRSWAP = [i + 1 if i % 2 == 0 else i - 1 for i in range(32)]

ENGS = ("tensor", "vector", "scalar", "gpsimd", "sync")
CARVED = {}

BUILD_PHASE = 3


class Plan:
    CHAINED = ("vector", "scalar")

    def __init__(self):
        self.ops = {e: [] for e in ENGS}
        self.cnt = {}
        self._lastwait = {e: {} for e in ENGS}
        self._lastop = {e: None for e in ENGS}

    def sem(self, name):
        if name not in self.cnt:
            self.cnt[name] = 0
        return name

    def emit(self, eng, fn, incs=()):
        incs = tuple(incs)
        if eng in self.CHAINED:
            prev = self._lastop[eng]
            if prev is not None:
                self.wait(eng, prev[0], prev[1])
            if not incs:
                incs = (("ch_" + eng, 1),)
        for s, a in incs:
            self.cnt[self.sem(s)] += a
        if eng in self.CHAINED:
            self._lastop[eng] = (incs[0][0], self.cnt[incs[0][0]])
        self.ops[eng].append(("op", fn, incs))

    def wait(self, eng, sem, val=None):
        self.sem(sem)
        v = self.cnt[sem] if val is None else val
        if v <= 0:
            return
        lw = self._lastwait[eng]
        if lw.get(sem, -1) >= v:
            return
        lw[sem] = v
        self.ops[eng].append(("wait", sem, v))

    def val(self, sem):
        return self.cnt[self.sem(sem)]


class Rot:
    def __init__(self, pl, name, depth, unit_p=1, unit_c=1):
        self.pl = pl
        self.depth = depth
        self.up = unit_p
        self.uc = unit_c
        self.sp = pl.sem(name + "_p")
        self.sc = pl.sem(name + "_c")
        self.np_ = 0
        self.nc_ = 0

    def begin_produce(self, eng):
        if self.np_ >= self.depth:
            self.pl.wait(eng, self.sc, (self.np_ - self.depth + 1) * self.uc)
        return self.np_ % self.depth

    def end_produce(self):
        self.np_ += 1
        return (self.sp, self.up)

    def begin_consume(self, eng):
        self.pl.wait(eng, self.sp, (self.nc_ + 1) * self.up)
        return self.nc_ % self.depth

    def end_consume(self):
        self.nc_ += 1
        return (self.sc, self.uc)

    def drain(self, eng):
        self.pl.wait(eng, self.sc, self.np_ * self.uc)


def build_program():
    nc = bass.Bass("TRN2", target_bir_lowering=False, detect_race_conditions=False)

    dx = nc.dram_tensor("x_fm", [C, T], F32, kind="ExternalInput").ap()
    dxh = nc.dram_tensor("x_halo", [C, 2], F32, kind="ExternalInput").ap()
    dwqkT = nc.dram_tensor("wqkT", [C, 2 * C], F32, kind="ExternalInput").ap()
    dbqk = nc.dram_tensor("bqk", [128, 16], F32, kind="ExternalInput").ap()
    dcos = nc.dram_tensor("ropecos", [128, NCH], F32, kind="ExternalInput").ap()
    dsin = nc.dram_tensor("ropesin", [128, NCH], F32, kind="ExternalInput").ap()
    dlepw = nc.dram_tensor("lepew", [128, NCH * 3], F32, kind="ExternalInput").ap()
    dlepb = nc.dram_tensor("lepeb", [128, NCH], F32, kind="ExternalInput").ap()
    dln1w = nc.dram_tensor("ln1w", [128, NCH], F32, kind="ExternalInput").ap()
    dln1b = nc.dram_tensor("ln1b", [128, NCH], F32, kind="ExternalInput").ap()
    dln2w = nc.dram_tensor("ln2w", [128, NCH], F32, kind="ExternalInput").ap()
    dln2b = nc.dram_tensor("ln2b", [128, NCH], F32, kind="ExternalInput").ap()
    dwqkv = nc.dram_tensor("wqkv", [C, 3 * C], F32, kind="ExternalInput").ap()
    dwproj = nc.dram_tensor("wproj", [C, C], F32, kind="ExternalInput").ap()
    dgatew = nc.dram_tensor("gatew", [C, NE], F32, kind="ExternalInput").ap()
    dgateb = nc.dram_tensor("gateb", [NE, 1], F32, kind="ExternalInput").ap()
    dw1 = nc.dram_tensor("w1", [NE, C, HID], F32, kind="ExternalInput").ap()
    dw2 = nc.dram_tensor("w2", [NE, HID, C], F32, kind="ExternalInput").ap()
    dident = nc.dram_tensor("ident", [128, 128], F32, kind="ExternalInput").ap()
    dhmask = nc.dram_tensor("hmask", [NCH, NH, 128], F32, kind="ExternalInput").ap()
    dcsts = nc.dram_tensor("csts", [128, 8], F32, kind="ExternalInput").ap()
    dout = nc.dram_tensor("out_fm", [C, T], F32, kind="ExternalOutput").ap()

    d_ckv_i = nc.dram_tensor("ckv_i", [64, C], F32).ap()
    d_ckv_o = nc.dram_tensor("ckv_o", [64, C], F32).ap()
    d_ckm_i = nc.dram_tensor("ckm_i", [128, NCH], F32).ap()
    d_ckm_o = nc.dram_tensor("ckm_o", [128, NCH], F32).ap()
    d_kg_i = nc.dram_tensor("kg_i", [C, T], BF16).ap()
    d_vg_i = nc.dram_tensor("vg_i", [NH * T, DK + 1], BF16).ap()
    d_kg_oh = [nc.dram_tensor("kg_o%d" % i, [4 * (C // 2), T], BF16).ap()
               for i in range(2)]
    d_vg_oh = [nc.dram_tensor("vg_o%d" % i, [4 * (NH // 2) * T, DK + 1], BF16).ap()
               for i in range(2)]
    d_x2 = nc.dram_tensor("park_x2", [C, T], F32).ap()
    d_y0 = nc.dram_tensor("park_y0", [C, T], F32).ap()
    d_xn2 = nc.dram_tensor("park_xn2", [C, T], BF16).ap()

    remaining = nc.sbuf_bytes_remaining
    A0 = 0
    B0 = A0 + 65536
    C0 = B0 + 65536
    D0 = C0 + 33792
    E0 = D0 + 33344
    TOTAL = (remaining - 64) & ~31
    assert TOTAL >= E0 + 14400, (TOTAL, E0)
    arena_h = nc.alloc_sbuf_tensor("arena", [128, TOTAL], mybir.dt.uint8)
    arena_addr = nc.lookup_mloc(arena_h).addr

    def carve(name, shape, dtype, off):
        nbytes = int(np.prod(shape[1:])) * mybir.dt.size(dtype)
        assert off >= 0 and off + nbytes <= TOTAL, (name, off, nbytes, TOTAL)
        h = nc.virtual_sbuf_tensor(name, shape, dtype)
        nc._tensor(h.name, h.shape, h.dtype, type="SB")
        mloc = nc.lookup_mloc(h)
        mloc.allocated = True
        mloc.addr = arena_addr + off
        CARVED[name] = h.name
        return h

    X = carve("X", [128, NCH, T], F32, A0)
    RES = carve("RES", [128, NCH, T], F32, B0)
    XLN = carve("XLN", [128, NCH, T + 2], BF16, C0)

    eoff = [E0]

    def carve_e(name, shape, dtype):
        h = carve(name, shape, dtype, eoff[0])
        nbytes = int(np.prod(shape[1:])) * mybir.dt.size(dtype)
        eoff[0] = (eoff[0] + nbytes + 31) & ~31
        return h

    IDF = carve_e("IDF", [128, 128], F32)
    IDB = carve_e("IDB", [128, 128], BF16)
    ONESB = carve_e("ONESB", [128, 128], BF16)
    COS = carve_e("COS", [128, NCH], F32)
    SIN = carve_e("SIN", [128, NCH], F32)
    LEPW = carve_e("LEPW", [128, NCH * 3], F32)
    LEPB = carve_e("LEPB", [128, NCH], F32)
    BQK = carve_e("BQK", [128, 16], F32)
    LN1W = carve_e("LN1W", [128, NCH], F32)
    LN1B = carve_e("LN1B", [128, NCH], F32)
    LN2W = carve_e("LN2W", [128, NCH], F32)
    LN2B = carve_e("LN2B", [128, NCH], F32)
    GATEW = carve_e("GATEW", [128, NCH, NE], F32)
    GATEB = carve_e("GATEB", [NE, 1], F32)
    HMB = carve_e("HMB", [NH, NCH * 128], BF16)
    CSTS = carve_e("CSTS", [128, 8], F32)
    MUR = carve_e("MUR", [1, TT], F32)
    VARR = carve_e("VARR", [1, TT], F32)
    RSR = carve_e("RSR", [1, TT], F32)
    MURB = carve_e("MURB", [1, TT], BF16)
    RSRB = carve_e("RSRB", [1, TT], BF16)
    KVGB = carve_e("KVGB", [128, C // 2], BF16)
    KMG = carve_e("KMG", [128, NCH], F32)
    KMM = carve_e("KMM", [128, NCH, NH], BF16)
    assert eoff[0] <= TOTAL, (eoff[0], TOTAL)

    WQK_SB = [carve("WQK_SB%d" % i, [128, NCH, 128], BF16, D0 + i * 2048) for i in range(2)]
    SQT = [carve("SQT%d" % i, [128, TT], BF16, D0 + 4096 + i * 1024) for i in range(2)]
    XBT = [carve("XBT%d" % i, [128, TT], BF16, D0 + 6144 + i * 1024) for i in range(2)]
    TN1 = carve("TN1", [128, TT], F32, D0 + 8192)
    ELU = [carve("ELU%d" % i, [128, TT], BF16, D0 + 10240 + i * 1024) for i in range(2)]
    KCH = carve("KCH", [128, TT], BF16, D0 + 12288)
    SWP = carve("SWP", [128, TT], BF16, D0 + 13312)
    RT4 = carve("RT4", [128, TT], BF16, D0 + 14336)
    KRCH = carve("KRCH", [128, TT], BF16, D0 + 15360)
    KRT = carve("KRT", [128, 4, C], BF16, D0 + 16384)
    VT = carve("VT", [128, 4, C], BF16, D0 + 24576)
    KMEAN = carve("KMEAN", [128, NCH], F32, D0 + 32768)
    KMT = carve("KMT", [128, 1], F32, D0 + 32800)
    QF = carve("QF", [128, NCH, T], BF16, B0)
    KL = carve("KL", [128, NCH, T], BF16, B0 + 32768)
    AOUT = carve("AOUT", [128, NCH, T], BF16, B0 + 32768)
    VL = carve("VL", [128, NH, 16, DK + 1], BF16, C0)
    KAUG = [carve("KAUG%d" % i, [65, 4 * T], BF16, C0 + i * 16384) for i in range(2)]
    VH = [carve("VH%d" % i, [128, 64, DK + 1], BF16, D0 + i * 8352) for i in range(2)]
    QAUG = [carve("QAUG%d" % i, [65, T], BF16, D0 + 16704 + i * 4096) for i in range(2)]
    PT = [carve("PT%d" % i, [128, 512], BF16, D0 + 24896 + i * 1024) for i in range(2)]
    MROW = carve("MROW", [128, 1], F32, D0 + 26944)
    MTMP = carve("MTMP", [128, 1], F32, D0 + 26976)
    RL = carve("RL", [1, 512], F32, D0 + 27008)
    RLB = carve("RLB", [1, 512], BF16, D0 + 29088)
    OT16 = carve("OT16", [64, 512], BF16, D0 + 30144)
    XN1T = carve("XN1T", [128, NCH, TT], BF16, D0 + 16384)
    VCH = [carve("VCH%d" % i, [128, TT], BF16, D0 + 12288 + i * 1024) for i in range(2)]
    SROW = carve("SROW", [128, 2, 4 * T], BF16, A0)
    Y1T = carve("Y1T", [128, NCH, T], F32, A0)
    W1SB = carve("W1SB", [128, NCH, HID], BF16, A0)
    W2SB = carve("W2SB", [128, 32, C], BF16, B0)
    YACC = carve("YACC", [128, NCH, T], BF16, C0)
    HSB = carve("HSB", [128, 32, MTT], BF16, D0)
    XNT = [carve("XNTa", [128, NCH, MTT], BF16, D0 + 16384),
           carve("XNTb", [128, NCH, MTT], BF16, D0 + 28704)]
    MBSB = carve("MBSB", [128, MTT], BF16, D0 + 20480)
    T1M = carve("T1M", [128, MTT], F32, D0 + 21504)
    LG = carve("LG", [8, TT], F32, D0 + 22528)
    MASKTM = carve("MASKTM", [128, 16, NE], F32, D0 + 24576)
    MROWE = carve("MROWE", [1, MTT], BF16, D0 + 25088)
    XNF = carve("XNF", [128, TT], F32, D0 + 25600)
    XNB = carve("XNB", [128, TT], BF16, D0 + 27648)
    MX = carve("MX", [128, 1], F32, D0 + 28672)
    X2T = carve("X2T", [128, NCH, TT], F32, A0)
    OUTT = carve("OUTT", [128, NCH, TT], F32, A0 + 16384)
    HLO = carve("HLO", [128, 16], F32, D0 + 32832)
    HSC = carve("HSC", [128, 32], F32, D0 + 32896)
    KVS = carve("KVS", [64, C], F32, B0)
    QTI = carve("QTI", [128, NCH, TT], BF16, D0 + 16384)
    QRTI = carve("QRTI", [128, NCH, TT], BF16, D0 + 24576)
    ATB = carve("ATB", [128, TT], BF16, D0 + 12288)
    ZROW = carve("ZROW", [NH, TT], F32, D0 + 13312)
    ZROWB = carve("ZROWB", [NH, TT], BF16, D0 + 15360)

    PSA = nc.alloc_psum_tensor("PSA", [128, 512], F32)
    PSB = nc.alloc_psum_tensor("PSB", [128, 512], F32)
    PSC = nc.alloc_psum_tensor("PSC", [128, 512], F32)
    PSD = nc.alloc_psum_tensor("PSD", [128, 512], F32)
    PSE = nc.alloc_psum_tensor("PSE", [128, 1024], F32)
    PST = nc.alloc_psum_tensor("PST", [128, 2048], BF16)
    PSMM = [PSA, PSB]
    PSMM2 = [PSC, PSD]

    pl = Plan()
    r_mm = Rot(pl, "r_mm", 2)
    r_tp = Rot(pl, "r_tp", 2)
    r_wqk = Rot(pl, "r_wqk", 2, unit_p=16)
    r_sq = Rot(pl, "r_sq", 2)
    r_xb = Rot(pl, "r_xb", 2)
    r_elu = Rot(pl, "r_elu", 2)
    r_zb = Rot(pl, "r_zb", 1)
    r_mm2 = Rot(pl, "r_mm2", 2)
    r_vch = Rot(pl, "r_vch", 2)

    lns_prev = [None]

    def ln_stats(src_chunk, W, src_f32=True, src_ready=None):
        if src_ready is not None:
            pl.wait("scalar", src_ready)
            pl.wait("tensor", src_ready)

        r_mm.drain("tensor")
        r_mm2.drain("tensor")
        for c in range(NCH):
            if src_f32:
                slot = r_xb.begin_produce("scalar")
                pl.emit("scalar", lambda e, c=c, slot=slot: e.activation(
                    XBT[slot][:, 0:W], src_chunk(c), AF.Copy), [r_xb.end_produce()])
            slot = r_sq.begin_produce("scalar")
            pl.emit("scalar", lambda e, c=c, slot=slot: e.activation(
                SQT[slot][:, 0:W], src_chunk(c), AF.Square), [r_sq.end_produce()])
            if src_f32:
                sl = r_xb.begin_consume("tensor")
                incs_a = [r_xb.end_consume()]
                if c == NCH - 1:
                    incs_a.append(("lns_pe", 1))
                pl.emit("tensor", lambda e, sl=sl, c=c: e.matmul(
                    PSA[0:1, 0:W], ONESB[:, 0:1], XBT[sl][:, 0:W],
                    start=(c == 0), stop=(c == NCH - 1)), incs_a)
            else:
                pl.emit("tensor", lambda e, c=c: e.matmul(
                    PSA[0:1, 0:W], ONESB[:, 0:1], src_chunk(c),
                    start=(c == 0), stop=(c == NCH - 1)),
                    [("lns_pe", 1)] if c == NCH - 1 else [])
            sl = r_sq.begin_consume("tensor")
            incs = [r_sq.end_consume()]
            if c == NCH - 1:
                incs.append(("lns_pe", 1))
            pl.emit("tensor", lambda e, sl=sl, c=c: e.matmul(
                PSB[0:1, 0:W], ONESB[:, 0:1], SQT[sl][:, 0:W],
                start=(c == 0), stop=(c == NCH - 1)), incs)
        pl.wait("vector", "lns_pe")
        pl.emit("vector", lambda e: e.tensor_scalar_mul(MUR[:, 0:W], PSA[0:1, 0:W], 1.0 / C))
        pl.emit("vector", lambda e: e.tensor_scalar_mul(VARR[:, 0:W], PSB[0:1, 0:W], 1.0 / C))
        pl.emit("vector", lambda e: e.tensor_mul(RSR[:, 0:W], MUR[:, 0:W], MUR[:, 0:W]))
        pl.emit("vector", lambda e: e.scalar_tensor_tensor(
            VARR[:, 0:W], VARR[:, 0:W], float(EPS), RSR[:, 0:W], ALU.add, ALU.subtract),
                [("lns_dv", 1)])
        pl.wait("scalar", "lns_dv")
        pl.emit("scalar", lambda e: e.activation(VARR[:, 0:W], VARR[:, 0:W], AF.Sqrt))
        pl.emit("scalar", lambda e: e.activation(MURB[:, 0:W], MUR[:, 0:W], AF.Copy),
                [("lns_ac", 1)])
        pl.wait("vector", "lns_ac")
        pl.emit("vector", lambda e: e.reciprocal(RSR[:, 0:W], VARR[:, 0:W]), [("lns_dv2", 1)])
        pl.wait("scalar", "lns_dv2")
        pl.emit("scalar", lambda e: e.activation(RSRB[:, 0:W], RSR[:, 0:W], AF.Copy),
                [("lns_ac2", 1)])
        pl.wait("tensor", "lns_ac2")
        if lns_prev[0] is not None:
            pl.wait("tensor", lns_prev[0][0], lns_prev[0][1])
        pl.emit("tensor", lambda e: e.matmul(PSC[:, 0:W], ONESB[0:1, :], MURB[0:1, 0:W],
                                             start=True, stop=True))
        pl.emit("tensor", lambda e: e.matmul(PSD[:, 0:W], ONESB[0:1, :], RSRB[0:1, 0:W],
                                             start=True, stop=True), [("lns_bc", 1)])
        pl.wait("vector", "lns_bc")

    def lns_done(sem):
        lns_prev[0] = (sem, pl.val(sem))

    pl.emit("sync", lambda e: e.dma_start(
        X[:, :, :], dx.rearrange("(c p) t -> p c t", p=128)), [("s_x", 16)])
    pl.emit("sync", lambda e: e.dma_start(
        HLO[:, 0:16].rearrange("p (c k) -> p c k", k=2),
        dxh.rearrange("(c p) k -> p c k", p=128)), [("s_x", 16)])
    for sb, dr in [(IDF, dident), (COS, dcos), (SIN, dsin), (LEPW, dlepw), (LEPB, dlepb),
                   (BQK, dbqk), (LN1W, dln1w), (LN1B, dln1b), (LN2W, dln2w), (LN2B, dln2b),
                   (GATEB, dgateb), (CSTS, dcsts)]:
        pl.emit("sync", lambda e, sb=sb, dr=dr: e.dma_start(sb[:, :], dr[:, :]),
                [("s_const", 16)])
    pl.emit("sync", lambda e: e.dma_start(
        GATEW[:, :, :], dgatew.rearrange("(c p) e -> p c e", p=128)), [("s_const", 16)])
    pl.emit("gpsimd", lambda e: e.dma_start(IDB[:, :], dident[:, :]), [("s_constg", 16)])
    pl.emit("gpsimd", lambda e: e.dma_start(
        HMB[:, :].rearrange("h (c p) -> h c p", c=NCH),
        dhmask.rearrange("c h p -> h c p")), [("s_constg", 16)])
    pl.emit("gpsimd", lambda e: e.memset(ONESB[:, :], 1.0), [("s_mset", 1)])
    pl.emit("gpsimd", lambda e: e.memset(KMEAN[:, :], 0.0), [("s_mset", 1)])
    pl.emit("gpsimd", lambda e: e.memset(KMM[:, :, :], 0.0), [("s_mset", 1)])
    pl.emit("gpsimd", lambda e: e.memset(KVS[:, :], 0.0), [("s_mset", 1)])
    N_CONST = pl.val("s_const")

    for eng in ("tensor", "vector", "scalar"):
        pl.wait(eng, "s_const", N_CONST)
        pl.wait(eng, "s_constg", 32)
        pl.wait(eng, "s_mset", 4)
        pl.wait(eng, "s_x", 32)

    for c in range(NCH):
        pl.emit("scalar", lambda e, c=c: e.activation(
            SQT[0][:, 2 * c:2 * c + 2], HLO[:, 2 * c:2 * c + 2], AF.Copy))
        pl.emit("scalar", lambda e, c=c: e.activation(
            SQT[0][:, 16 + 2 * c:16 + 2 * c + 2], HLO[:, 2 * c:2 * c + 2], AF.Square),
            [("h_st", 1)] if c == NCH - 1 else [])
    pl.wait("tensor", "h_st")
    for c in range(NCH):
        pl.emit("tensor", lambda e, c=c: e.matmul(
            PSA[0:1, 0:2], ONESB[:, 0:1], SQT[0][:, 2 * c:2 * c + 2],
            start=(c == 0), stop=(c == NCH - 1)))
        pl.emit("tensor", lambda e, c=c: e.matmul(
            PSB[0:1, 0:2], ONESB[:, 0:1], SQT[0][:, 16 + 2 * c:16 + 2 * c + 2],
            start=(c == 0), stop=(c == NCH - 1)),
            [("h_mm", 1)] if c == NCH - 1 else [])
    pl.wait("vector", "h_mm")
    pl.emit("vector", lambda e: e.tensor_scalar_mul(MUR[:, 0:2], PSA[0:1, 0:2], 1.0 / C))
    pl.emit("vector", lambda e: e.tensor_scalar_mul(VARR[:, 0:2], PSB[0:1, 0:2], 1.0 / C))
    pl.emit("vector", lambda e: e.tensor_mul(RSR[:, 0:2], MUR[:, 0:2], MUR[:, 0:2]))
    pl.emit("vector", lambda e: e.scalar_tensor_tensor(
        VARR[:, 0:2], VARR[:, 0:2], float(EPS), RSR[:, 0:2], ALU.add, ALU.subtract),
        [("h_rw", 1)])
    pl.wait("scalar", "h_rw")
    pl.emit("scalar", lambda e: e.activation(VARR[:, 0:2], VARR[:, 0:2], AF.Sqrt))
    pl.emit("scalar", lambda e: e.activation(MURB[:, 0:2], MUR[:, 0:2], AF.Copy),
            [("h_sq", 1)])
    pl.wait("vector", "h_sq")
    pl.emit("vector", lambda e: e.reciprocal(RSR[:, 0:2], VARR[:, 0:2]), [("h_rc", 1)])
    pl.wait("scalar", "h_rc")
    pl.emit("scalar", lambda e: e.activation(RSRB[:, 0:2], RSR[:, 0:2], AF.Copy),
            [("h_sq2", 1)])
    pl.wait("tensor", "h_sq2")
    pl.emit("tensor", lambda e: e.matmul(PSC[:, 0:2], ONESB[0:1, :], MURB[0:1, 0:2],
                                         start=True, stop=True))
    pl.emit("tensor", lambda e: e.matmul(PSD[:, 0:2], ONESB[0:1, :], RSRB[0:1, 0:2],
                                         start=True, stop=True), [("h_bc", 1)])
    pl.wait("vector", "h_bc")
    for c in range(NCH):
        for k in range(2):
            col = 0 if k == 0 else T + 1
            pl.emit("vector", lambda e, c=c, k=k: e.tensor_sub(
                HSC[:, 4 * c + k:4 * c + k + 1], HLO[:, 2 * c + k:2 * c + k + 1],
                PSC[:, k:k + 1]))
            pl.emit("vector", lambda e, c=c, k=k, col=col: e.tensor_tensor(
                XLN[:, c, col:col + 1], HSC[:, 4 * c + k:4 * c + k + 1],
                PSD[:, k:k + 1], ALU.mult),
                [("h_done", 1)] if (c == NCH - 1 and k == 1) else [])
    lns_done("h_done")
    pl.wait("scalar", "h_mm")
    pl.wait("tensor", "h_rw")

    def emit_qk_chunk(j, t0, out_dve):
        wslot = r_wqk.begin_produce("gpsimd")
        pl.emit("gpsimd", lambda e, j=j, wslot=wslot: e.dma_start(
            WQK_SB[wslot][:, :, :],
            dwqkT[:, j * 128:(j + 1) * 128].rearrange("(c p) j -> p c j", p=128)),
            [r_wqk.end_produce()])
        wsl = r_wqk.begin_consume("tensor")
        pl.wait("tensor", "p1_xln")
        mslot = r_mm.begin_produce("tensor")
        for c in range(NCH):
            incs = []
            if c == NCH - 1:
                incs = [r_mm.end_produce(), r_wqk.end_consume()]
            pl.emit("tensor", lambda e, c=c, wsl=wsl, mslot=mslot, t0=t0: e.matmul(
                PSMM[mslot][:, 0:TT], WQK_SB[wsl][:, c, :],
                XLN[:, c, 1 + t0:1 + t0 + TT], start=(c == 0), stop=(c == NCH - 1)), incs)
        msl = r_mm.begin_consume("vector")
        ps = PSMM[msl]
        eslot = r_elu.begin_produce("vector")
        pl.emit("vector", lambda e, ps=ps, j=j, eslot=eslot: e.tensor_scalar(
            ELU[eslot][:, 0:TT], ps[:, 0:TT], BQK[:, j:j + 1], 0.0, ALU.add, ALU.min),
            [r_elu.end_produce()])
        el = r_elu.begin_consume("scalar")
        pl.emit("scalar", lambda e, el=el: e.activation(
            ELU[el][:, 0:TT], ELU[el][:, 0:TT], AF.Exp), [r_elu.end_consume()])
        pl.wait("vector", "r_elu_c")
        pl.emit("vector", lambda e, ps=ps, j=j: e.tensor_scalar(
            TN1[:, 0:TT], ps[:, 0:TT], BQK[:, j:j + 1], 0.0, ALU.add, ALU.max))
        out_dve(el, ps)

    for tt in range(NTT):
        t0 = tt * TT
        ln_stats(lambda c, t0=t0: X[:, c, t0:t0 + TT], TT)
        for c in range(NCH):
            pl.emit("vector", lambda e, c=c, t0=t0: e.tensor_sub(
                TN1[:, 0:TT], X[:, c, t0:t0 + TT], PSC[:, 0:TT]))
            pl.emit("vector", lambda e, c=c, t0=t0: e.tensor_tensor(
                XLN[:, c, 1 + t0:1 + t0 + TT], TN1[:, 0:TT], PSD[:, 0:TT], ALU.mult),
                [("p1_xln", 1)])
        lns_done("p1_xln")

        for jj in range(8):
            j = 8 + jj

            def k_out(el, ps, jj=jj, t0=t0):
                pl.emit("vector", lambda e, el=el: e.tensor_add(
                    KCH[:, 0:TT], TN1[:, 0:TT], ELU[el][:, 0:TT]), [r_mm.end_consume()])
                pl.emit("vector", lambda e: e.reduce_sum(KMT[:, :], KCH[:, 0:TT], axis=AX.X))
                pl.emit("vector", lambda e, jj=jj: e.tensor_add(
                    KMEAN[:, jj:jj + 1], KMEAN[:, jj:jj + 1], KMT[:, :]), [("p1_km", 1)])
                pl.emit("vector", lambda e: e.stream_shuffle(SWP[:, 0:TT], KCH[:, 0:TT],
                                                             PAIRSWAP))
                pl.emit("vector", lambda e, jj=jj: e.tensor_scalar_mul(
                    RT4[:, 0:TT], KCH[:, 0:TT], COS[:, jj:jj + 1]))
                pl.emit("vector", lambda e, jj=jj: e.scalar_tensor_tensor(
                    KRCH[:, 0:TT], SWP[:, 0:TT], SIN[:, jj:jj + 1], RT4[:, 0:TT],
                    ALU.mult, ALU.add), [("p1_kr", 1)])
                for s in range(4):
                    tsl = r_tp.begin_produce("tensor")
                    pl.wait("tensor", "p1_kr")
                    pl.emit("tensor", lambda e, s=s, tsl=tsl: e.transpose(
                        PST[:, tsl * 1024:tsl * 1024 + 128],
                        KRCH[:, s * 128:(s + 1) * 128], IDB[:, :]), [r_tp.end_produce()])
                    tc = r_tp.begin_consume("scalar")
                    pl.emit("scalar", lambda e, s=s, jj=jj, tc=tc: e.activation(
                        KRT[:, s, jj * 128:(jj + 1) * 128],
                        PST[:, tc * 1024:tc * 1024 + 128], AF.Copy),
                        [r_tp.end_consume()])
                    tsl = r_tp.begin_produce("tensor")
                    pl.emit("tensor", lambda e, s=s, jj=jj, t0=t0, tsl=tsl: e.transpose(
                        PST[:, tsl * 1024:tsl * 1024 + 128],
                        XLN[:, jj, 1 + t0 + s * 128:1 + t0 + (s + 1) * 128], IDB[:, :]),
                        [r_tp.end_produce()])
                    tc = r_tp.begin_consume("scalar")
                    pl.emit("scalar", lambda e, s=s, jj=jj, tc=tc: e.activation(
                        VT[:, s, jj * 128:(jj + 1) * 128],
                        PST[:, tc * 1024:tc * 1024 + 128], AF.Copy),
                        [r_tp.end_consume()])

            emit_qk_chunk(j, t0, k_out)

        pl.wait("tensor", "h_done")
        pl.wait("tensor", "r_tp_c")
        pl.wait("tensor", "p1_kva")
        for h in range(NH):
            for s in range(4):
                incs = [("p1_kv", 1)] if (s == 3 and h == NH - 1) else []
                pl.emit("tensor", lambda e, h=h, s=s: e.matmul(
                    PSE[0:64, h * 64:(h + 1) * 64],
                    KRT[:, s, h * 64:(h + 1) * 64], VT[:, s, h * 64:(h + 1) * 64],
                    start=(s == 0), stop=(s == 3)), incs)
        pl.wait("vector", "p1_kv")
        pl.wait("vector", "s_mset", 4)
        pl.emit("vector", lambda e: e.scalar_tensor_tensor(
            KVS[:, :], PSE[0:64, 0:1024], 1.0 / SEQ, KVS[:, :], ALU.mult, ALU.add),
            [("p1_kva", 1)])

    pl.wait("gpsimd", "p1_kva")
    pl.emit("gpsimd", lambda e: e.dma_start(d_ckv_i[:, :], KVS[:, :]), [("s_ccst", 16)])
    pl.wait("gpsimd", "p1_km", 32)
    pl.emit("gpsimd", lambda e: e.dma_start(d_ckm_i[:, :], KMEAN[:, :]), [("s_ccst", 16)])
    pl.wait("gpsimd", "s_ccst")
    pl.emit("gpsimd", lambda e: e.collective_compute(
        "AllReduce", ALU.add, GROUPS, [d_ckv_i[:, :]], [d_ckv_o[:, :]]), [("s_cc", 1)])
    pl.emit("gpsimd", lambda e: e.collective_compute(
        "AllReduce", ALU.add, GROUPS, [d_ckm_i[:, :]], [d_ckm_o[:, :]]), [("s_cc", 1)])
    pl.wait("gpsimd", "s_cc")
    ckv_he = d_ckv_o.rearrange("d (hh two e) -> d hh two e", two=2, e=64)
    pl.emit("gpsimd", lambda e: e.dma_start(
        KVGB[0:64, :].rearrange("p (hh e) -> p hh e", e=64), ckv_he[:, :, 0, :]),
        [("s_ccl", 16)])
    pl.emit("gpsimd", lambda e: e.dma_start(
        KVGB[64:128, :].rearrange("p (hh e) -> p hh e", e=64), ckv_he[:, :, 1, :]),
        [("s_ccl", 16)])
    pl.emit("gpsimd", lambda e: e.dma_start(KMG[:, :], d_ckm_o[:, :]), [("s_ccl", 16)])

    pl.wait("vector", "s_ccst", 16)
    for tt in range(NTT):
        t0 = tt * TT
        for c in range(NCH):
            pl.emit("vector", lambda e, c=c, t0=t0: e.tensor_scalar_mul(
                TN1[:, 0:TT], XLN[:, c, t0:t0 + TT], LEPW[:, 3 * c:3 * c + 1]))
            pl.emit("vector", lambda e, c=c, t0=t0: e.scalar_tensor_tensor(
                TN1[:, 0:TT], XLN[:, c, 1 + t0:1 + t0 + TT], LEPW[:, 3 * c + 1:3 * c + 2],
                TN1[:, 0:TT], ALU.mult, ALU.add))
            pl.emit("vector", lambda e, c=c, t0=t0: e.scalar_tensor_tensor(
                RES[:, c, t0:t0 + TT], XLN[:, c, 2 + t0:2 + t0 + TT],
                LEPW[:, 3 * c + 2:3 * c + 3], TN1[:, 0:TT], ALU.mult, ALU.add),
                [("p1_conv", 1)])

    pl.wait("scalar", "s_ccl", 48)
    for c in range(NCH):
        pl.emit("scalar", lambda e, c=c: e.activation(
            KMM[0:64, c, 2 * c:2 * c + 1], KMG[0:64, c:c + 1], AF.Copy, scale=CSTS[0:64, 0:1]))
        pl.emit("scalar", lambda e, c=c: e.activation(
            KMM[64:128, c, 2 * c + 1:2 * c + 2], KMG[64:128, c:c + 1], AF.Copy,
            scale=CSTS[64:128, 0:1]), [("p1_kmm", 1)] if c == NCH - 1 else [])

    for tt in range(NTT):
        t0 = tt * TT
        for j in range(8):
            def q_out(el, ps, j=j):
                pl.emit("vector", lambda e, el=el, j=j: e.tensor_add(
                    QTI[:, j, 0:TT], TN1[:, 0:TT], ELU[el][:, 0:TT]),
                    [("p2_q", 1)])
                pl.emit("vector", lambda e: e.engine_nop(), [r_mm.end_consume()])

            pl.wait("vector", "s_ccst", 16)
            emit_qk_chunk(j, t0, q_out)
        pl.wait("tensor", "p2_q")
        pl.wait("tensor", "p1_kmm")
        pl.wait("tensor", "p1_kva")
        pl.wait("tensor", "p2_zcons")
        for c in range(NCH):
            pl.emit("tensor", lambda e, c=c: e.matmul(
                PSE[0:NH, 0:TT], KMM[:, c, :], QTI[:, c, 0:TT],
                start=(c == 0), stop=(c == NCH - 1)),
                [("p2_zpe", 1)] if c == NCH - 1 else [])
        pl.wait("vector", "p2_zpe")
        pl.emit("vector", lambda e: e.tensor_scalar_add(ZROW[:, :], PSE[0:NH, 0:TT], 1e-6),
                [("p2_zcons", 1)])
        pl.emit("vector", lambda e: e.reciprocal(ZROW[:, :], ZROW[:, :]))
        pl.emit("vector", lambda e: e.tensor_copy(ZROWB[:, :], ZROW[:, :]), [("p2_z", 1)])
        pl.wait("tensor", "p2_z")
        pl.wait("tensor", "s_ccl", 48)
        pl.wait("vector", "p1_conv")
        for c in range(NCH):
            r_zb.begin_produce("tensor")
            pl.emit("tensor", lambda e, c=c: e.matmul(
                PSE[:, 512:512 + TT], HMB[:, c * 128:(c + 1) * 128], ZROWB[:, 0:TT],
                start=True, stop=True), [r_zb.end_produce()])
            pl.emit("vector", lambda e, c=c: e.stream_shuffle(
                SWP[:, 0:TT], QTI[:, c, 0:TT], PAIRSWAP))
            pl.emit("vector", lambda e, c=c: e.tensor_scalar_mul(
                RT4[:, 0:TT], QTI[:, c, 0:TT], COS[:, c:c + 1]))
            pl.emit("vector", lambda e, c=c: e.scalar_tensor_tensor(
                TN1[:, 0:TT], SWP[:, 0:TT], SIN[:, c:c + 1], RT4[:, 0:TT],
                ALU.mult, ALU.add))
            r_zb.begin_consume("vector")
            pl.emit("vector", lambda e, c=c: e.tensor_tensor(
                QRTI[:, c, 0:TT], TN1[:, 0:TT], PSE[:, 512:512 + TT], ALU.mult),
                [r_zb.end_consume()])
            pl.wait("tensor", "r_zb_c")
            mslot = r_mm.begin_produce("tensor")
            ps = PSMM[mslot]
            pl.emit("tensor", lambda e, c=c, ps=ps: e.matmul(
                ps[0:64, 0:TT], KVGB[0:64, c * 64:(c + 1) * 64],
                QRTI[0:64, c, 0:TT], start=True, stop=True))
            pl.emit("tensor", lambda e, c=c, ps=ps: e.matmul(
                ps[64:128, 0:TT], KVGB[64:128, c * 64:(c + 1) * 64],
                QRTI[64:128, c, 0:TT], start=True, stop=True), [r_mm.end_produce()])
            msl = r_mm.begin_consume("vector")
            pl.emit("vector", lambda e, c=c, t0=t0, msl=msl: e.scalar_tensor_tensor(
                RES[:, c, t0:t0 + TT], PSMM[msl][:, 0:TT], LEPB[:, c:c + 1],
                RES[:, c, t0:t0 + TT], ALU.add, ALU.add),
                [r_mm.end_consume()])
        pl.emit("vector", lambda e: e.engine_nop(), [("p2_res", 1)])
        ln_stats(lambda c, t0=t0: RES[:, c, t0:t0 + TT], TT, src_ready="p2_res")
        for c in range(NCH):
            pl.emit("vector", lambda e, c=c, t0=t0: e.tensor_sub(
                TN1[:, 0:TT], RES[:, c, t0:t0 + TT], PSC[:, 0:TT]))
            pl.emit("vector", lambda e: e.tensor_tensor(
                TN1[:, 0:TT], TN1[:, 0:TT], PSD[:, 0:TT], ALU.mult), [("p2_pn", 1)])
            pl.emit("vector", lambda e, c=c, t0=t0: e.tensor_add(
                X[:, c, t0:t0 + TT], X[:, c, t0:t0 + TT], TN1[:, 0:TT]), [("p2_x2", 1)])
        lns_done("p2_pn")
        pl.wait("sync", "p2_x2")
        pl.emit("sync", lambda e, t0=t0: e.dma_start(
            d_x2[:, t0:t0 + TT].rearrange("(c p) t -> p c t", p=128),
            X[:, :, t0:t0 + TT]), [("s_x2p", 16)])
        if BUILD_PHASE >= 2:
            ln_stats(lambda c, t0=t0: X[:, c, t0:t0 + TT], TT, src_ready="p2_x2")
            pl.wait("vector", "s_x2p")
            for c in range(NCH):
                pl.emit("vector", lambda e, c=c, t0=t0: e.tensor_sub(
                    TN1[:, 0:TT], X[:, c, t0:t0 + TT], PSC[:, 0:TT]))
                pl.emit("vector", lambda e, c=c, t0=t0: e.tensor_tensor(
                    X[:, c, t0:t0 + TT], TN1[:, 0:TT], PSD[:, 0:TT], ALU.mult),
                    [("p2_y0", 1)])
            lns_done("p2_y0")

    if BUILD_PHASE == 1:
        pl.wait("sync", "p2_x2")
        pl.emit("sync", lambda e: e.dma_start(
            dout.rearrange("(c p) t -> p c t", p=128), X[:, :, :]), [("s_out", 16)])
        pl.wait("sync", "s_out")


    if BUILD_PHASE >= 2:
        for tt in range(NTT):
            t0 = tt * TT
            ln_stats(lambda c, t0=t0: X[:, c, t0:t0 + TT], TT, src_ready="p2_y0")
            for c in range(NCH):
                pl.wait("vector", "p2_xn1")
                pl.emit("vector", lambda e, c=c, t0=t0: e.tensor_sub(
                    TN1[:, 0:TT], X[:, c, t0:t0 + TT], PSC[:, 0:TT]))
                pl.emit("vector", lambda e: e.tensor_tensor(
                    TN1[:, 0:TT], TN1[:, 0:TT], PSD[:, 0:TT], ALU.mult), [("p2_ln1", 1)])
                pl.wait("scalar", "p2_ln1")
                pl.emit("scalar", lambda e, c=c: e.activation(
                    XN1T[:, c, 0:TT], TN1[:, 0:TT], AF.Identity,
                    scale=LN1W[:, c:c + 1], bias=LN1B[:, c:c + 1]), [("p2_xn1", 1)])
            lns_done("p2_ln1")
            for jc in range(24):
                wslot = r_wqk.begin_produce("gpsimd")
                pl.emit("gpsimd", lambda e, jc=jc, wslot=wslot: e.dma_start(
                    WQK_SB[wslot][:, :, :],
                    dwqkv[:, jc * 128:(jc + 1) * 128].rearrange("(c p) j -> p c j", p=128)),
                    [r_wqk.end_produce()])
                wsl = r_wqk.begin_consume("tensor")
                pl.wait("tensor", "p2_xn1")
                mslot = r_mm.begin_produce("tensor")
                for c in range(NCH):
                    incs = []
                    if c == NCH - 1:
                        incs = [r_mm.end_produce(), r_wqk.end_consume()]
                    pl.emit("tensor", lambda e, c=c, wsl=wsl, mslot=mslot: e.matmul(
                        PSMM[mslot][:, 0:TT], WQK_SB[wsl][:, c, :],
                        XN1T[:, c, 0:TT], start=(c == 0), stop=(c == NCH - 1)), incs)
                msl = r_mm.begin_consume("scalar")
                ps = PSMM[msl]
                if jc < 8:
                    pl.emit("scalar", lambda e, jc=jc, t0=t0, ps=ps: e.activation(
                        QF[:, jc, t0:t0 + TT], ps[:, 0:TT], AF.Copy,
                        scale=CSTS[:, 2:3]), [r_mm.end_consume()])
                elif jc < 16:
                    pl.emit("scalar", lambda e, jc=jc, t0=t0, ps=ps: e.activation(
                        KL[:, jc - 8, t0:t0 + TT], ps[:, 0:TT], AF.Copy),
                        [r_mm.end_consume()])
                else:
                    vslot = r_vch.begin_produce("scalar")
                    pl.emit("scalar", lambda e, ps=ps, vslot=vslot: e.activation(
                        VCH[vslot][:, 0:TT], ps[:, 0:TT], AF.Copy),
                        [r_mm.end_consume()])
                    r_vch.np_ += 1
                    vch_ready = pl.val("r_mm_c")
                    vsl = r_vch.nc_ % 2
                    r_vch.nc_ += 1
                    pl.wait("tensor", "r_mm_c", vch_ready)
                    h0 = 2 * (jc - 16)
                    for s in range(4):
                        ts_ = tt * 4 + s
                        tsl = r_tp.begin_produce("tensor")
                        incs = [r_tp.end_produce()]
                        if s == 3:
                            incs.append(("r_vch_c", 1))
                        pl.emit("tensor", lambda e, s=s, vsl=vsl, tsl=tsl: e.transpose(
                            PST[:, tsl * 1024:tsl * 1024 + 128],
                            VCH[vsl][:, s * 128:(s + 1) * 128], IDB[:, :]), incs)
                        tc = r_tp.begin_consume("scalar")
                        pl.emit("scalar", lambda e, h0=h0, ts_=ts_, tc=tc: e.activation(
                            VL[:, h0, ts_, 0:64],
                            PST[:, tc * 1024:tc * 1024 + 64], AF.Copy))
                        pl.emit("scalar", lambda e, h0=h0, ts_=ts_, tc=tc: e.activation(
                            VL[:, h0 + 1, ts_, 0:64],
                            PST[:, tc * 1024 + 64:tc * 1024 + 128], AF.Copy),
                            [r_tp.end_consume()])
        QF_DONE = pl.val("r_mm_c")
        VL_TPC = pl.val("r_tp_c")

        pl.wait("gpsimd", "p2_pn")
        pl.wait("gpsimd", "r_tp_c", VL_TPC)
        for h in range(NH):
            pl.emit("gpsimd", lambda e, h=h: e.memset(VL[:, h, :, 64:65], 1.0),
                    [("s_mset", 1)])
        N_MSET_VL = pl.val("s_mset")

        pl.wait("sync", "r_mm_c", QF_DONE)
        pl.emit("sync", lambda e: e.dma_start(
            d_kg_i.rearrange("(c p) t -> p c t", p=128), KL[:, :, :]), [("s_kg", 16)])
        pl.wait("sync", "r_tp_c", VL_TPC)
        pl.wait("sync", "s_mset", N_MSET_VL)
        pl.emit("sync", lambda e: e.dma_start(
            d_vg_i.rearrange("(h ts p) e -> p h ts e", p=128, ts=16), VL[:, :, :, :]),
            [("s_vg", 16)])
        pl.wait("gpsimd", "s_vg")
        pl.emit("gpsimd", lambda e: e.memset(KAUG[0][64:65, :], 1.0), [("s_mset", 1)])
        pl.emit("gpsimd", lambda e: e.memset(KAUG[1][64:65, :], 1.0), [("s_mset", 1)])
        pl.emit("gpsimd", lambda e: e.memset(QAUG[0][64:65, :], 0.0), [("s_mset", 1)])
        pl.emit("gpsimd", lambda e: e.memset(QAUG[1][64:65, :], 0.0), [("s_mset", 1)])
        N_MSET2 = pl.val("s_mset")
        pl.wait("gpsimd", "s_kg")
        pl.wait("gpsimd", "s_vg")
        for half in range(2):
            pl.emit("gpsimd", lambda e, half=half: e.collective_compute(
                "AllGather", ALU.bypass, GROUPS,
                [d_kg_i[half * (C // 2):(half + 1) * (C // 2), :]],
                [d_kg_oh[half][:, :]]), [("s_cc2", 1)])
            pl.emit("gpsimd", lambda e, half=half: e.collective_compute(
                "AllGather", ALU.bypass, GROUPS,
                [d_vg_i[half * (NH // 2) * T:(half + 1) * (NH // 2) * T, :]],
                [d_vg_oh[half][:, :]]), [("s_cc2", 1)])

        st_mark = {}
        av_mark = {}
        hd_mark = {}

        def emit_load(h):
            buf = h % 2
            cQ = h // 2
            rb = (h % 2) * 64
            half = h // 8
            hh = h % 8
            pl.wait("sync", "s_cc2", 2 if h < 8 else 4)
            if h >= 2:
                pl.wait("sync", "r_mm2_p", st_mark[h - 2])
                pl.wait("sync", "b_av", av_mark[h - 2])
            for r in range(4):
                pl.emit("sync", lambda e, hh=hh, r=r, buf=buf, half=half: e.dma_start(
                    KAUG[buf][0:64, r * T:(r + 1) * T],
                    d_kg_oh[half][r * (C // 2) + 64 * hh: r * (C // 2) + 64 * hh + 64, :]),
                    [("s_hd", 16)])
            for r in range(4):
                pl.emit("sync", lambda e, hh=hh, r=r, buf=buf, half=half: e.dma_start(
                    VH[buf][:, r * 16:(r + 1) * 16, :],
                    d_vg_oh[half][(r * 8 + hh) * T:(r * 8 + hh + 1) * T, :]
                    .rearrange("(j p) e -> p j e", p=128)), [("s_hd", 16)])
            hd_mark[h] = pl.val("s_hd")
            pl.wait("vector", "r_mm_c", QF_DONE)
            if h >= 2:
                pl.wait("vector", "r_mm2_p", st_mark[h - 2])
            pl.emit("vector", lambda e, buf=buf, cQ=cQ, rb=rb: e.tensor_copy(
                QAUG[buf][0:64, :], QF[rb:rb + 64, cQ, :]), [("a_qb", 1)])

        def emit_B(h):
            buf = h % 2
            cQ = h // 2
            rb = (h % 2) * 64
            pl.wait("tensor", "a_qb", h + 1)
            pl.wait("tensor", "s_mset", N_MSET2)
            pl.wait("tensor", "s_hd", hd_mark[h])
            for qt4 in range(4):
                pl.wait("tensor", "b_fin")
                pend = None
                for kc in range(65):
                    if kc < 64:
                        m2 = r_mm2.begin_produce("tensor")
                        pl.emit("tensor", lambda e, buf=buf, qt4=qt4, kc=kc, m2=m2: e.matmul(
                            PSMM2[m2][:, 0:TT], KAUG[buf][:, kc * 128:(kc + 1) * 128],
                            QAUG[buf][:, qt4 * TT:(qt4 + 1) * TT],
                            start=True, stop=True), [r_mm2.end_produce()])
                        m2c = r_mm2.begin_consume("scalar")
                        pl.wait("scalar", "b_av", pl.val("b_av"))
                        pslot = pl.val("r_mm2_c") % 2
                        pl.emit("scalar", lambda e, m2c=m2c, pslot=pslot: e.activation(
                            PT[pslot][:, 0:TT], PSMM2[m2c][:, 0:TT], AF.Exp),
                            [r_mm2.end_consume()])
                        nxt = (kc, pslot, pl.val("r_mm2_c"))
                    if pend is not None:
                        kcp, pslotp, expv = pend
                        pl.wait("tensor", "r_mm2_c", expv)
                        pl.emit("tensor", lambda e, buf=buf, kcp=kcp, pslotp=pslotp: e.matmul(
                            PSE[0:65, 0:TT], VH[buf][:, kcp, :], PT[pslotp][:, 0:TT],
                            start=(kcp == 0), stop=(kcp == 63)), [("b_av", 1)])
                    if kc < 64:
                        pend = nxt
                pl.wait("vector", "b_av")
                pl.emit("vector", lambda e: e.reciprocal(RL[:, :], PSE[64:65, 0:TT]),
                        [("b_rl", 1)])
                pl.wait("scalar", "b_rl")
                pl.emit("scalar", lambda e: e.activation(RLB[:, :], RL[:, :], AF.Copy),
                        [("b_rlb", 1)])
                pl.emit("scalar", lambda e: e.activation(OT16[:, :], PSE[0:64, 0:TT],
                                                         AF.Copy), [("b_ot", 1)])
                pl.wait("tensor", "b_rlb")
                pl.emit("tensor", lambda e: e.matmul(
                    PSE[0:64, 512:1024], ONESB[0:1, 0:64], RLB[0:1, :],
                    start=True, stop=True), [("b_lb", 1)])
                pl.wait("vector", "b_lb")
                pl.wait("vector", "b_ot")
                pl.emit("vector", lambda e, rb=rb, cQ=cQ, qt4=qt4: e.tensor_tensor(
                    AOUT[rb:rb + 64, cQ, qt4 * TT:(qt4 + 1) * TT], OT16[:, :],
                    PSE[0:64, 512:1024], ALU.mult), [("b_fin", 1)])
            st_mark[h] = pl.val("r_mm2_p")
            av_mark[h] = pl.val("b_av")

        pl.wait("tensor", "p2_ln1")

        for h in range(NH):
            emit_load(h)
            if h >= 1:
                emit_B(h - 1)
        emit_B(NH - 1)

        AOUT_DONE = pl.val("b_fin")
        pl.wait("gpsimd", "b_av", av_mark[NH - 1])
        for tt in range(NTT):
            t0 = tt * TT
            for oc in range(NCH):
                wslot = r_wqk.begin_produce("gpsimd")
                pl.emit("gpsimd", lambda e, oc=oc, wslot=wslot: e.dma_start(
                    WQK_SB[wslot][:, :, :],
                    dwproj[:, oc * 128:(oc + 1) * 128].rearrange("(c p) j -> p c j", p=128)),
                    [r_wqk.end_produce()])
                wsl = r_wqk.begin_consume("tensor")
                pl.wait("tensor", "b_fin", AOUT_DONE)
                mslot = r_mm.begin_produce("tensor")
                for c in range(NCH):
                    incs = []
                    if c == NCH - 1:
                        incs = [r_mm.end_produce(), r_wqk.end_consume()]
                    pl.emit("tensor", lambda e, c=c, wsl=wsl, mslot=mslot, t0=t0: e.matmul(
                        PSMM[mslot][:, 0:TT], WQK_SB[wsl][:, c, :],
                        AOUT[:, c, t0:t0 + TT], start=(c == 0), stop=(c == NCH - 1)), incs)
                msl = r_mm.begin_consume("vector")
                pl.emit("vector", lambda e, oc=oc, t0=t0, msl=msl: e.tensor_add(
                    Y1T[:, oc, t0:t0 + TT], Y1T[:, oc, t0:t0 + TT], PSMM[msl][:, 0:TT]),
                    [r_mm.end_consume()])
        Y1_DONE = pl.val("r_mm_c")

    if BUILD_PHASE == 2:
        pl.wait("sync", "r_mm_c", Y1_DONE)
        pl.emit("sync", lambda e: e.dma_start(
            dout.rearrange("(c p) t -> p c t", p=128), Y1T[:, :, :]), [("s_out", 16)])
        pl.wait("sync", "s_out")


    if BUILD_PHASE >= 3:
        for tt in range(NTT):
            t0 = tt * TT
            pl.wait("tensor", "p3_msk")
            ln_stats(lambda c, t0=t0: Y1T[:, c, t0:t0 + TT], TT, src_ready="r_mm_c")
            for c in range(NCH):
                pl.wait("vector", "p3_xnf")
                pl.emit("vector", lambda e, c=c, t0=t0: e.tensor_sub(
                    TN1[:, 0:TT], Y1T[:, c, t0:t0 + TT], PSC[:, 0:TT]))
                pl.emit("vector", lambda e: e.tensor_tensor(
                    TN1[:, 0:TT], TN1[:, 0:TT], PSD[:, 0:TT], ALU.mult), [("p3_ln2", 1)])
                pl.wait("scalar", "p3_ln2")
                pl.wait("scalar", "p3_gmm")
                pl.emit("scalar", lambda e, c=c: e.activation(
                    XNF[:, 0:TT], TN1[:, 0:TT], AF.Identity,
                    scale=LN2W[:, c:c + 1], bias=LN2B[:, c:c + 1]), [("p3_xnf", 1)])
                pl.wait("scalar", "s_xnp")
                pl.emit("scalar", lambda e: e.activation(
                    XNB[:, 0:TT], XNF[:, 0:TT], AF.Copy), [("p3_xnb", 1)])
                pl.wait("sync", "p3_xnb")
                pl.emit("sync", lambda e, c=c, t0=t0: e.dma_start(
                    d_xn2[c * 128:(c + 1) * 128, t0:t0 + TT], XNB[:, 0:TT]),
                    [("s_xnp", 16)])
                pl.wait("tensor", "p3_xnf")
                pl.wait("tensor", "p3_lg")
                pl.emit("tensor", lambda e, c=c: e.matmul(
                    PSE[0:NE, 0:TT], GATEW[:, c, :], XNF[:, 0:TT],
                    start=(c == 0), stop=(c == NCH - 1)), [("p3_gmm", 1)])
            lns_done("p3_ln2")
            pl.wait("scalar", "p3_gmm")
            pl.emit("scalar", lambda e: e.activation(
                LG[:, 0:TT], PSE[0:NE, 0:TT], AF.Identity, bias=GATEB[:, 0:1]),
                [("p3_lg", 1)])
            for s in range(4):
                ts_ = tt * 4 + s
                pl.wait("tensor", "p3_lg")
                pl.wait("tensor", "p3_msk")
                pl.emit("tensor", lambda e, s=s: e.transpose(
                    PSB[:, 0:8], LG[0:8, s * 128:(s + 1) * 128], IDF[0:8, 0:8]),
                    [("p3_gt", 1)])
                pl.wait("vector", "p3_gt")
                pl.emit("vector", lambda e: e.reduce_max(MX[:, :], PSB[:, 0:8], axis=AX.X))
                pl.emit("vector", lambda e, ts_=ts_: e.tensor_scalar(
                    MASKTM[:, ts_, :], PSB[:, 0:8], MX[:, 0:1], None, ALU.is_ge),
                    [("p3_msk", 1)])
            pl.wait("scalar", "b_fin", AOUT_DONE)
            for c in range(NCH):
                pl.emit("scalar", lambda e, c=c, t0=t0: e.activation(
                    YACC[:, c, t0:t0 + TT], Y1T[:, c, t0:t0 + TT], AF.Copy),
                    [("p3_yi", 1)])

        m1t_idx = [0]
        m2t_idx = [0]
        for e_ in range(NE):
            if e_ == 0:
                pl.wait("gpsimd", "p3_yi", 32)
            pl.wait("gpsimd", "p3_m1d", e_)
            pl.emit("gpsimd", lambda e, e_=e_: e.dma_start(
                W1SB[:, :, :], dw1[e_, :, :].rearrange("(c p) j -> p c j", p=128)),
                [("s_w1", 16)])
            if e_ == 0:
                pl.wait("gpsimd", "r_mm_c", Y1_DONE)
            pl.wait("gpsimd", "p3_m2d", e_)
            pl.emit("gpsimd", lambda e, e_=e_: e.dma_start(
                W2SB[:, :, :], dw2[e_, :, :].rearrange("(j p) o -> p j o", p=128)),
                [("s_w2", 16)])
            for mt in range(NMTT):
                blk = e_ * NMTT + mt
                xsl = blk % 2
                if blk > 1:
                    pl.wait("sync", "p3_m1t", blk - 1)
                pl.wait("sync", "s_xnp")
                pl.emit("sync", lambda e, mt=mt, xsl=xsl: e.dma_start(
                    XNT[xsl][:, :, :],
                    d_xn2[:, mt * MTT:(mt + 1) * MTT].rearrange("(c p) t -> p c t", p=128)),
                    [("s_xnl", 16)])
                XNT_RDY = pl.val("s_xnl")
                pl.wait("tensor", "s_w1", 16 * (e_ + 1))
                pl.wait("tensor", "s_xnl", XNT_RDY)
                pl.wait("scalar", "p3_m2t", blk)
                for j in range(32):
                    mslot = r_mm.begin_produce("tensor")
                    for c in range(NCH):
                        incs = []
                        if c == NCH - 1:
                            incs = [r_mm.end_produce()]
                            if j == 31:
                                incs.append(("p3_m1t", 1))
                                if mt == NMTT - 1:
                                    incs.append(("p3_m1d", 1))
                        pl.emit("tensor", lambda e, c=c, j=j, mslot=mslot, xsl=xsl: e.matmul(
                            PSMM[mslot][:, 0:MTT], W1SB[:, c, j * 128:(j + 1) * 128],
                            XNT[xsl][:, c, 0:MTT], start=(c == 0), stop=(c == NCH - 1)), incs)
                    msl = r_mm.begin_consume("scalar")
                    pl.emit("scalar", lambda e, j=j, msl=msl: e.activation(
                        HSB[:, j, 0:MTT], PSMM[msl][:, 0:MTT], AF.Gelu_apprx_tanh),
                        [r_mm.end_consume()])
                HSB_RDY = pl.val("r_mm_c")
                pl.wait("tensor", "p3_msk", 4 * NTT)
                for half in range(2):
                    ts_ = mt * 2 + half
                    pl.wait("tensor", "p3_mr")
                    pl.emit("tensor", lambda e, ts_=ts_, e_=e_: e.matmul(
                        PSE[0:1, 512:640], MASKTM[:, ts_, e_:e_ + 1], IDF[:, :],
                        start=True, stop=True), [("p3_mt", 1)])
                    pl.wait("scalar", "p3_mt")
                    pl.emit("scalar", lambda e, half=half: e.activation(
                        MROWE[0:1, half * 128:(half + 1) * 128], PSE[0:1, 512:640],
                        AF.Copy), [("p3_mr", 1)])
                pl.wait("tensor", "p3_mr")
                pl.wait("tensor", "p3_mbs")
                pl.emit("tensor", lambda e: e.matmul(
                    PSE[:, 0:MTT], ONESB[0:1, :], MROWE[0:1, 0:MTT],
                    start=True, stop=True), [("p3_mb", 1)])
                pl.wait("scalar", "p3_mb")
                pl.wait("scalar", "p3_moe")
                pl.emit("scalar", lambda e: e.activation(
                    MBSB[:, 0:MTT], PSE[:, 0:MTT], AF.Copy), [("p3_mbs", 1)])
                pl.wait("tensor", "s_w2", 16 * (e_ + 1))
                pl.wait("tensor", "r_mm_c", HSB_RDY)
                for o in range(NCH):
                    m2 = r_mm2.begin_produce("tensor")
                    for j in range(32):
                        incs = []
                        if j == 31:
                            incs = [r_mm2.end_produce()]
                            if o == NCH - 1:
                                incs.append(("p3_m2t", 1))
                                if mt == NMTT - 1:
                                    incs.append(("p3_m2d", 1))
                        pl.emit("tensor", lambda e, o=o, j=j, m2=m2: e.matmul(
                            PSMM2[m2][:, 0:MTT], W2SB[:, j, o * 128:(o + 1) * 128],
                            HSB[:, j, 0:MTT], start=(j == 0), stop=(j == 31)), incs)
                    m2c = r_mm2.begin_consume("vector")
                    pl.wait("vector", "p3_mbs")
                    pl.emit("vector", lambda e, m2c=m2c: e.tensor_tensor(
                        T1M[:, 0:MTT], PSMM2[m2c][:, 0:MTT], MBSB[:, 0:MTT], ALU.mult),
                        [r_mm2.end_consume()])
                    pl.emit("vector", lambda e, o=o, mt=mt: e.tensor_add(
                        YACC[:, o, mt * MTT:(mt + 1) * MTT],
                        YACC[:, o, mt * MTT:(mt + 1) * MTT], T1M[:, 0:MTT]),
                        [("p3_moe", 1)])

        for tt in range(NTT):
            t0 = tt * TT
            pl.wait("sync", "p3_m1d", NE)
            pl.wait("sync", "p3_out")
            pl.emit("sync", lambda e, t0=t0: e.dma_start(
                X2T[:, :, :],
                d_x2[:, t0:t0 + TT].rearrange("(c p) t -> p c t", p=128)),
                [("s_x2r", 16)])
            ln_stats(lambda c, t0=t0: YACC[:, c, t0:t0 + TT], TT, src_f32=False,
                     src_ready="p3_moe")
            pl.wait("vector", "s_x2r")
            pl.wait("vector", "s_out")
            for c in range(NCH):
                pl.emit("vector", lambda e, c=c, t0=t0: e.tensor_sub(
                    TN1[:, 0:TT], YACC[:, c, t0:t0 + TT], PSC[:, 0:TT]))
                incs = [("p3_fn", 1)] if c == NCH - 1 else []
                pl.emit("vector", lambda e: e.tensor_tensor(
                    TN1[:, 0:TT], TN1[:, 0:TT], PSD[:, 0:TT], ALU.mult), incs)
                pl.emit("vector", lambda e, c=c: e.tensor_add(
                    OUTT[:, c, 0:TT], X2T[:, c, 0:TT], TN1[:, 0:TT]), [("p3_out", 1)])
            lns_done("p3_fn")
            pl.wait("sync", "p3_out")
            pl.emit("sync", lambda e, t0=t0: e.dma_start(
                dout[:, t0:t0 + TT].rearrange("(c p) t -> p c t", p=128),
                OUTT[:, :, :]), [("s_out", 16)])
        pl.wait("sync", "s_out")

    emit(nc, pl)
    return nc


def emit(nc, pl):
    with ExitStack() as st:
        sems = {n: st.enter_context(nc.semaphore(n)) for n in pl.cnt}
        nums = sorted(s.num for s in sems.values())
        assert nums[-1] - nums[0] + 1 == len(nums), "sem numbers not contiguous"
        clear_range = range(nums[0], nums[-1] + 1)
        with nc.Block() as block0:
            @block0.gpsimd
            def _(g):
                g.dma_reset(clear_range)
                g.sem_clear(clear_range)
        nc.all_engine_barrier()
        with nc.Block() as block:
            for eng_name, deco in (("tensor", block.tensor), ("vector", block.vector),
                                   ("scalar", block.scalar), ("gpsimd", block.gpsimd),
                                   ("sync", block.sync)):
                def runner(e, eng_name=eng_name):
                    for item in pl.ops[eng_name]:
                        if item[0] == "wait":
                            e.wait_ge(sems[item[1]], item[2])
                        else:
                            ins = item[1](e)
                            incs = item[2]
                            if eng_name != "tensor":
                                assert len(incs) <= 1, (eng_name, incs)
                            if len(incs) > 1:
                                ins.then_inc(sems[incs[0][0]], incs[0][1])
                                for s, a in incs[1:]:
                                    e.nop().then_inc(sems[s], a)
                            else:
                                for s, a in incs:
                                    ins.then_inc(sems[s], a)
                deco(runner)



def _stage(inputs):
    x = np.asarray(inputs["L0_attn_x"])
    W_qk = np.asarray(inputs["L0_attn_W_qk"])
    b_qk = np.asarray(inputs["L0_attn_b_qk"])
    lepe_w = np.asarray(inputs["L0_attn_lepe_w"])
    lepe_b = np.asarray(inputs["L0_attn_lepe_b"])
    ln1w = np.asarray(inputs["L0_ffn_ln1_w"]); ln1b = np.asarray(inputs["L0_ffn_ln1_b"])
    ln2w = np.asarray(inputs["L0_ffn_ln2_w"]); ln2b = np.asarray(inputs["L0_ffn_ln2_b"])
    wqkv = np.asarray(inputs["L0_ffn_w_qkv"]); wproj = np.asarray(inputs["L0_ffn_w_proj"])
    gatew = np.asarray(inputs["L0_ffn_gate_w"]); gateb = np.asarray(inputs["L0_ffn_gate_b"])
    w1 = np.asarray(inputs["L0_ffn_w1"]); w2 = np.asarray(inputs["L0_ffn_w2"])

    k_max = C // 2
    theta = 1.0 / (10000.0 ** (np.arange(k_max, dtype=np.float64) / k_max))
    cosv = np.repeat(np.cos(theta), 2).astype(np.float32)
    sinv = np.zeros(C, dtype=np.float32)
    sinv[0::2] = -np.sin(theta)
    sinv[1::2] = np.sin(theta)

    def colview(v):
        return np.ascontiguousarray(np.asarray(v, dtype=np.float32).reshape(NCH, 128).T)

    hmask = np.zeros((NCH, NH, 128), dtype=np.float32)
    for c in range(NCH):
        hmask[c, 2 * c, 0:64] = 1.0
        hmask[c, 2 * c + 1, 64:128] = 1.0

    shared = {
        "wqkT": np.ascontiguousarray(W_qk.T),
        "bqk": np.ascontiguousarray(b_qk.reshape(16, 128).T),
        "ropecos": colview(cosv), "ropesin": colview(sinv),
        "lepew": np.ascontiguousarray(
            lepe_w[:, 0, :].reshape(NCH, 128, 3).transpose(1, 0, 2).reshape(128, NCH * 3)),
        "lepeb": colview(lepe_b),
        "ln1w": colview(ln1w), "ln1b": colview(ln1b),
        "ln2w": colview(ln2w), "ln2b": colview(ln2b),
        "wqkv": np.ascontiguousarray(wqkv), "wproj": np.ascontiguousarray(wproj),
        "gatew": np.ascontiguousarray(gatew),
        "gateb": np.ascontiguousarray(np.asarray(gateb).reshape(NE, 1)),
        "w1": np.ascontiguousarray(w1), "w2": np.ascontiguousarray(w2),
        "ident": np.eye(128, dtype=np.float32),
        "csts": np.broadcast_to(np.array(
            [1.0 / SEQ, -1.0, 0.125, 0.5, EPS, 0.0, 0.0, 0.0], dtype=np.float32),
            (128, 8)).copy(),
        "hmask": hmask,
    }
    in_maps = []
    for core in range(NCORES):
        b = core // 4
        t0 = (core % 4) * T
        m = dict(shared)
        m["x_fm"] = np.ascontiguousarray(x[b, t0:t0 + T].T)
        halo = np.zeros((C, 2), dtype=np.float32)
        if t0 > 0:
            halo[:, 0] = x[b, t0 - 1]
        if t0 + T < SEQ:
            halo[:, 1] = x[b, t0 + T]
        m["x_halo"] = halo
        in_maps.append(m)
    return in_maps


def kernel(**inputs):
    in_maps = _stage(inputs)
    nc = build_program()
    res = run_bass_kernel_spmd(nc, in_maps, core_ids=list(range(NCORES)))
    out = np.empty((NB, SEQ, C), dtype=np.float32)
    for core in range(NCORES):
        b = core // 4
        t0 = (core % 4) * T
        out[b, t0:t0 + T] = res.results[core]["out_fm"].T
    return out


# version 09
# speedup vs baseline: 2.0411x; 1.0053x over previous
from contextlib import ExitStack

import numpy as np
import concourse.bass as bass
import concourse.mybir as mybir
from concourse.bass_utils import run_bass_kernel_spmd

F32 = mybir.dt.float32
BF16 = mybir.dt.bfloat16
AF = mybir.ActivationFunctionType
ALU = mybir.AluOpType
AX = mybir.AxisListType

NCORES = 8
NB = 2
SEQ = 8192
T = 2048
C = 1024
NCH = 8
NH = 16
DK = 64
TT = 512
NTT = T // TT
HID = 4096
NE = 8
MTT = 256
NMTT = T // MTT
EPS = 1e-5
GROUPS = [[0, 1, 2, 3], [4, 5, 6, 7]]
PAIRSWAP = [i + 1 if i % 2 == 0 else i - 1 for i in range(32)]

ENGS = ("tensor", "vector", "scalar", "gpsimd", "sync")
CARVED = {}

BUILD_PHASE = 3


class Plan:
    CHAINED = ("vector", "scalar")

    def __init__(self):
        self.ops = {e: [] for e in ENGS}
        self.cnt = {}
        self._lastwait = {e: {} for e in ENGS}
        self._lastop = {e: None for e in ENGS}

    def sem(self, name):
        if name not in self.cnt:
            self.cnt[name] = 0
        return name

    def emit(self, eng, fn, incs=()):
        incs = tuple(incs)
        if eng in self.CHAINED:
            prev = self._lastop[eng]
            if prev is not None:
                self.wait(eng, prev[0], prev[1])
            if not incs:
                incs = (("ch_" + eng, 1),)
        for s, a in incs:
            self.cnt[self.sem(s)] += a
        if eng in self.CHAINED:
            self._lastop[eng] = (incs[0][0], self.cnt[incs[0][0]])
        self.ops[eng].append(("op", fn, incs))

    def wait(self, eng, sem, val=None):
        self.sem(sem)
        v = self.cnt[sem] if val is None else val
        if v <= 0:
            return
        lw = self._lastwait[eng]
        if lw.get(sem, -1) >= v:
            return
        lw[sem] = v
        self.ops[eng].append(("wait", sem, v))

    def val(self, sem):
        return self.cnt[self.sem(sem)]


class Rot:
    def __init__(self, pl, name, depth, unit_p=1, unit_c=1):
        self.pl = pl
        self.depth = depth
        self.up = unit_p
        self.uc = unit_c
        self.sp = pl.sem(name + "_p")
        self.sc = pl.sem(name + "_c")
        self.np_ = 0
        self.nc_ = 0

    def begin_produce(self, eng):
        if self.np_ >= self.depth:
            self.pl.wait(eng, self.sc, (self.np_ - self.depth + 1) * self.uc)
        return self.np_ % self.depth

    def end_produce(self):
        self.np_ += 1
        return (self.sp, self.up)

    def begin_consume(self, eng):
        self.pl.wait(eng, self.sp, (self.nc_ + 1) * self.up)
        return self.nc_ % self.depth

    def end_consume(self):
        self.nc_ += 1
        return (self.sc, self.uc)

    def drain(self, eng):
        self.pl.wait(eng, self.sc, self.np_ * self.uc)


def build_program():
    nc = bass.Bass("TRN2", target_bir_lowering=False, detect_race_conditions=False)

    dx = nc.dram_tensor("x_fm", [C, T], F32, kind="ExternalInput").ap()
    dxh = nc.dram_tensor("x_halo", [C, 2], F32, kind="ExternalInput").ap()
    dwqkT = nc.dram_tensor("wqkT", [C, 2 * C], F32, kind="ExternalInput").ap()
    dbqk = nc.dram_tensor("bqk", [128, 16], F32, kind="ExternalInput").ap()
    dcos = nc.dram_tensor("ropecos", [128, NCH], F32, kind="ExternalInput").ap()
    dsin = nc.dram_tensor("ropesin", [128, NCH], F32, kind="ExternalInput").ap()
    dlepw = nc.dram_tensor("lepew", [128, NCH * 3], F32, kind="ExternalInput").ap()
    dlepb = nc.dram_tensor("lepeb", [128, NCH], F32, kind="ExternalInput").ap()
    dln1w = nc.dram_tensor("ln1w", [128, NCH], F32, kind="ExternalInput").ap()
    dln1b = nc.dram_tensor("ln1b", [128, NCH], F32, kind="ExternalInput").ap()
    dln2w = nc.dram_tensor("ln2w", [128, NCH], F32, kind="ExternalInput").ap()
    dln2b = nc.dram_tensor("ln2b", [128, NCH], F32, kind="ExternalInput").ap()
    dwqkv = nc.dram_tensor("wqkv", [C, 3 * C], F32, kind="ExternalInput").ap()
    dwproj = nc.dram_tensor("wproj", [C, C], F32, kind="ExternalInput").ap()
    dgatew = nc.dram_tensor("gatew", [C, NE], F32, kind="ExternalInput").ap()
    dgateb = nc.dram_tensor("gateb", [NE, 1], F32, kind="ExternalInput").ap()
    dw1 = nc.dram_tensor("w1", [NE, C, HID], F32, kind="ExternalInput").ap()
    dw2 = nc.dram_tensor("w2", [NE, HID, C], F32, kind="ExternalInput").ap()
    dident = nc.dram_tensor("ident", [128, 128], F32, kind="ExternalInput").ap()
    dhmask = nc.dram_tensor("hmask", [NCH, NH, 128], F32, kind="ExternalInput").ap()
    dcsts = nc.dram_tensor("csts", [128, 8], F32, kind="ExternalInput").ap()
    dout = nc.dram_tensor("out_fm", [C, T], F32, kind="ExternalOutput").ap()

    d_ckx_i = nc.dram_tensor("ckx_i", [64, C + 16], F32).ap()
    d_ckx_o = nc.dram_tensor("ckx_o", [64, C + 16], F32).ap()
    d_kg_i = nc.dram_tensor("kg_i", [C, T], BF16).ap()
    d_vg_i = nc.dram_tensor("vg_i", [NH * T, DK + 1], BF16).ap()
    d_kg_oh = [nc.dram_tensor("kg_o%d" % i, [4 * (C // 2), T], BF16).ap()
               for i in range(2)]
    d_vg_oh = [nc.dram_tensor("vg_o%d" % i, [4 * (NH // 2) * T, DK + 1], BF16).ap()
               for i in range(2)]
    d_x2 = nc.dram_tensor("park_x2", [C, T], F32).ap()
    d_y0 = nc.dram_tensor("park_y0", [C, T], F32).ap()
    d_xn2 = nc.dram_tensor("park_xn2", [C, T], BF16).ap()

    remaining = nc.sbuf_bytes_remaining
    A0 = 0
    B0 = A0 + 65536
    C0 = B0 + 65536
    D0 = C0 + 33792
    E0 = D0 + 33344
    TOTAL = (remaining - 64) & ~31
    assert TOTAL >= E0 + 14400, (TOTAL, E0)
    arena_h = nc.alloc_sbuf_tensor("arena", [128, TOTAL], mybir.dt.uint8)
    arena_addr = nc.lookup_mloc(arena_h).addr

    def carve(name, shape, dtype, off):
        nbytes = int(np.prod(shape[1:])) * mybir.dt.size(dtype)
        assert off >= 0 and off + nbytes <= TOTAL, (name, off, nbytes, TOTAL)
        h = nc.virtual_sbuf_tensor(name, shape, dtype)
        nc._tensor(h.name, h.shape, h.dtype, type="SB")
        mloc = nc.lookup_mloc(h)
        mloc.allocated = True
        mloc.addr = arena_addr + off
        CARVED[name] = h.name
        return h

    X = carve("X", [128, NCH, T], F32, A0)
    RES = carve("RES", [128, NCH, T], F32, B0)
    XLN = carve("XLN", [128, NCH, T + 2], BF16, C0)

    eoff = [E0]

    def carve_e(name, shape, dtype):
        h = carve(name, shape, dtype, eoff[0])
        nbytes = int(np.prod(shape[1:])) * mybir.dt.size(dtype)
        eoff[0] = (eoff[0] + nbytes + 31) & ~31
        return h

    IDF = carve_e("IDF", [128, 128], F32)
    IDB = carve_e("IDB", [128, 128], BF16)
    ONESB = carve_e("ONESB", [128, 128], BF16)
    COS = carve_e("COS", [128, NCH], F32)
    SIN = carve_e("SIN", [128, NCH], F32)
    LEPW = carve_e("LEPW", [128, NCH * 3], F32)
    LEPB = carve_e("LEPB", [128, NCH], F32)
    BQK = carve_e("BQK", [128, 16], F32)
    LN1W = carve_e("LN1W", [128, NCH], F32)
    LN1B = carve_e("LN1B", [128, NCH], F32)
    LN2W = carve_e("LN2W", [128, NCH], F32)
    LN2B = carve_e("LN2B", [128, NCH], F32)
    GATEW = carve_e("GATEW", [128, NCH, NE], F32)
    GATEB = carve_e("GATEB", [NE, 1], F32)
    HMB = carve_e("HMB", [NH, NCH * 128], BF16)
    CSTS = carve_e("CSTS", [128, 8], F32)
    MUR = carve_e("MUR", [1, TT], F32)
    VARR = carve_e("VARR", [1, TT], F32)
    RSR = carve_e("RSR", [1, TT], F32)
    MURB = carve_e("MURB", [1, TT], BF16)
    RSRB = carve_e("RSRB", [1, TT], BF16)
    KVGB = carve_e("KVGB", [128, C // 2], BF16)
    KMG = carve_e("KMG", [128, NCH], F32)
    KMM = carve_e("KMM", [128, NCH, NH], BF16)
    assert eoff[0] <= TOTAL, (eoff[0], TOTAL)

    WQK_SB = [carve("WQK_SB%d" % i, [128, NCH, 128], BF16, D0 + i * 2048) for i in range(2)]
    SQT = [carve("SQT%d" % i, [128, TT], BF16, D0 + 4096 + i * 1024) for i in range(2)]
    XBT = [carve("XBT%d" % i, [128, TT], BF16, D0 + 6144 + i * 1024) for i in range(2)]
    TN1 = carve("TN1", [128, TT], F32, D0 + 8192)
    ELU = [carve("ELU%d" % i, [128, TT], BF16, D0 + 10240 + i * 1024) for i in range(2)]
    KCH = carve("KCH", [128, TT], BF16, D0 + 12288)
    SWP = carve("SWP", [128, TT], BF16, D0 + 13312)
    RT4 = carve("RT4", [128, TT], BF16, D0 + 14336)
    KRCH = carve("KRCH", [128, TT], BF16, D0 + 15360)
    KRT = carve("KRT", [128, 4, C], BF16, D0 + 16384)
    VT = carve("VT", [128, 4, C], BF16, D0 + 24576)
    KMEAN = carve("KMEAN", [128, NCH], F32, D0 + 32768)
    KMT = carve("KMT", [128, 1], F32, D0 + 32800)
    QF = carve("QF", [128, NCH, T], BF16, B0)
    KL = carve("KL", [128, NCH, T], BF16, B0 + 32768)
    AOUT = carve("AOUT", [128, NCH, T], BF16, B0 + 32768)
    VL = carve("VL", [128, NH, 16, DK + 1], BF16, C0)
    KAUG = [carve("KAUG%d" % i, [65, 4 * T], BF16, C0 + i * 16384) for i in range(2)]
    VH = [carve("VH%d" % i, [128, 64, DK + 1], BF16, D0 + i * 8352) for i in range(2)]
    QAUG = [carve("QAUG%d" % i, [65, T], BF16, D0 + 16704 + i * 4096) for i in range(2)]
    PT = [carve("PT%d" % i, [128, 512], BF16, D0 + 24896 + i * 1024) for i in range(2)]
    MROW = carve("MROW", [128, 1], F32, D0 + 26944)
    MTMP = carve("MTMP", [128, 1], F32, D0 + 26976)
    RL = carve("RL", [1, 512], F32, D0 + 27008)
    RLB = carve("RLB", [1, 512], BF16, D0 + 29088)
    OT16 = carve("OT16", [64, 512], BF16, D0 + 30144)
    XN1T = carve("XN1T", [128, NCH, TT], BF16, D0 + 16384)
    VCH = [carve("VCH%d" % i, [128, TT], BF16, D0 + 12288 + i * 1024) for i in range(2)]
    SROW = carve("SROW", [128, 2, 4 * T], BF16, A0)
    Y1T = carve("Y1T", [128, NCH, T], F32, A0)
    W1SB = carve("W1SB", [128, NCH, HID], BF16, A0)
    W2SB = carve("W2SB", [128, 32, C], BF16, B0)
    YACC = carve("YACC", [128, NCH, T], BF16, C0)
    HSB = carve("HSB", [128, 32, MTT], BF16, D0)
    XNT = [carve("XNTa", [128, NCH, MTT], BF16, D0 + 16384),
           carve("XNTb", [128, NCH, MTT], BF16, D0 + 28704)]
    MBSB = carve("MBSB", [128, MTT], BF16, D0 + 20480)
    T1M = carve("T1M", [128, MTT], F32, D0 + 21504)
    LG = carve("LG", [8, TT], F32, D0 + 22528)
    MASKTM = carve("MASKTM", [128, 16, NE], F32, D0 + 24576)
    MROWE = carve("MROWE", [1, MTT], BF16, D0 + 25088)
    XNF = carve("XNF", [128, TT], F32, D0 + 25600)
    XNB = carve("XNB", [128, TT], BF16, D0 + 27648)
    MX = carve("MX", [128, 1], F32, D0 + 28672)
    X2T = carve("X2T", [128, NCH, TT], F32, A0)
    OUTT = carve("OUTT", [128, NCH, TT], F32, A0 + 16384)
    HLO = carve("HLO", [128, 16], F32, D0 + 32832)
    HSC = carve("HSC", [128, 32], F32, D0 + 32896)
    KVS = carve("KVS", [64, C], F32, B0)
    QTI = carve("QTI", [128, NCH, TT], BF16, D0 + 16384)
    QRTI = carve("QRTI", [128, NCH, TT], BF16, D0 + 24576)
    ATB = carve("ATB", [128, TT], BF16, D0 + 12288)
    ZROW = carve("ZROW", [NH, TT], F32, D0 + 13312)
    ZROWB = carve("ZROWB", [NH, TT], BF16, D0 + 15360)

    PSA = nc.alloc_psum_tensor("PSA", [128, 512], F32)
    PSB = nc.alloc_psum_tensor("PSB", [128, 512], F32)
    PSC = nc.alloc_psum_tensor("PSC", [128, 512], F32)
    PSD = nc.alloc_psum_tensor("PSD", [128, 512], F32)
    PSE = nc.alloc_psum_tensor("PSE", [128, 1024], F32)
    PST = nc.alloc_psum_tensor("PST", [128, 2048], BF16)
    PSMM = [PSA, PSB]
    PSMM2 = [PSC, PSD]

    pl = Plan()
    r_mm = Rot(pl, "r_mm", 2)
    r_tp = Rot(pl, "r_tp", 2)
    r_wqk = Rot(pl, "r_wqk", 2, unit_p=16)
    r_sq = Rot(pl, "r_sq", 2)
    r_xb = Rot(pl, "r_xb", 2)
    r_elu = Rot(pl, "r_elu", 2)
    r_zb = Rot(pl, "r_zb", 1)
    r_mm2 = Rot(pl, "r_mm2", 2)
    r_vch = Rot(pl, "r_vch", 2)

    lns_prev = [None]

    def ln_stats(src_chunk, W, src_f32=True, src_ready=None):
        if src_ready is not None:
            pl.wait("scalar", src_ready)
            pl.wait("tensor", src_ready)

        r_mm.drain("tensor")
        r_mm2.drain("tensor")
        for c in range(NCH):
            if src_f32:
                slot = r_xb.begin_produce("scalar")
                pl.emit("scalar", lambda e, c=c, slot=slot: e.activation(
                    XBT[slot][:, 0:W], src_chunk(c), AF.Copy), [r_xb.end_produce()])
            slot = r_sq.begin_produce("scalar")
            pl.emit("scalar", lambda e, c=c, slot=slot: e.activation(
                SQT[slot][:, 0:W], src_chunk(c), AF.Square), [r_sq.end_produce()])
            if src_f32:
                sl = r_xb.begin_consume("tensor")
                incs_a = [r_xb.end_consume()]
                if c == NCH - 1:
                    incs_a.append(("lns_pe", 1))
                pl.emit("tensor", lambda e, sl=sl, c=c: e.matmul(
                    PSA[0:1, 0:W], ONESB[:, 0:1], XBT[sl][:, 0:W],
                    start=(c == 0), stop=(c == NCH - 1)), incs_a)
            else:
                pl.emit("tensor", lambda e, c=c: e.matmul(
                    PSA[0:1, 0:W], ONESB[:, 0:1], src_chunk(c),
                    start=(c == 0), stop=(c == NCH - 1)),
                    [("lns_pe", 1)] if c == NCH - 1 else [])
            sl = r_sq.begin_consume("tensor")
            incs = [r_sq.end_consume()]
            if c == NCH - 1:
                incs.append(("lns_pe", 1))
            pl.emit("tensor", lambda e, sl=sl, c=c: e.matmul(
                PSB[0:1, 0:W], ONESB[:, 0:1], SQT[sl][:, 0:W],
                start=(c == 0), stop=(c == NCH - 1)), incs)
        pl.wait("vector", "lns_pe")
        pl.emit("vector", lambda e: e.tensor_scalar_mul(MUR[:, 0:W], PSA[0:1, 0:W], 1.0 / C))
        pl.emit("vector", lambda e: e.tensor_scalar_mul(VARR[:, 0:W], PSB[0:1, 0:W], 1.0 / C))
        pl.emit("vector", lambda e: e.tensor_mul(RSR[:, 0:W], MUR[:, 0:W], MUR[:, 0:W]))
        pl.emit("vector", lambda e: e.scalar_tensor_tensor(
            VARR[:, 0:W], VARR[:, 0:W], float(EPS), RSR[:, 0:W], ALU.add, ALU.subtract),
                [("lns_dv", 1)])
        pl.wait("scalar", "lns_dv")
        pl.emit("scalar", lambda e: e.activation(VARR[:, 0:W], VARR[:, 0:W], AF.Sqrt))
        pl.emit("scalar", lambda e: e.activation(MURB[:, 0:W], MUR[:, 0:W], AF.Copy),
                [("lns_ac", 1)])
        pl.wait("vector", "lns_ac")
        pl.emit("vector", lambda e: e.reciprocal(RSR[:, 0:W], VARR[:, 0:W]), [("lns_dv2", 1)])
        pl.wait("scalar", "lns_dv2")
        pl.emit("scalar", lambda e: e.activation(RSRB[:, 0:W], RSR[:, 0:W], AF.Copy),
                [("lns_ac2", 1)])
        pl.wait("tensor", "lns_ac2")
        if lns_prev[0] is not None:
            pl.wait("tensor", lns_prev[0][0], lns_prev[0][1])
        pl.emit("tensor", lambda e: e.matmul(PSC[:, 0:W], ONESB[0:1, :], MURB[0:1, 0:W],
                                             start=True, stop=True))
        pl.emit("tensor", lambda e: e.matmul(PSD[:, 0:W], ONESB[0:1, :], RSRB[0:1, 0:W],
                                             start=True, stop=True), [("lns_bc", 1)])
        pl.wait("vector", "lns_bc")

    def lns_done(sem):
        lns_prev[0] = (sem, pl.val(sem))

    pl.emit("sync", lambda e: e.dma_start(
        X[:, :, :], dx.rearrange("(c p) t -> p c t", p=128)), [("s_x", 16)])
    pl.emit("sync", lambda e: e.dma_start(
        HLO[:, 0:16].rearrange("p (c k) -> p c k", k=2),
        dxh.rearrange("(c p) k -> p c k", p=128)), [("s_x", 16)])
    for sb, dr in [(IDF, dident), (COS, dcos), (SIN, dsin), (LEPW, dlepw), (LEPB, dlepb),
                   (BQK, dbqk), (LN1W, dln1w), (LN1B, dln1b), (LN2W, dln2w), (LN2B, dln2b),
                   (GATEB, dgateb), (CSTS, dcsts)]:
        pl.emit("sync", lambda e, sb=sb, dr=dr: e.dma_start(sb[:, :], dr[:, :]),
                [("s_const", 16)])
    pl.emit("sync", lambda e: e.dma_start(
        GATEW[:, :, :], dgatew.rearrange("(c p) e -> p c e", p=128)), [("s_const", 16)])
    pl.emit("gpsimd", lambda e: e.dma_start(IDB[:, :], dident[:, :]), [("s_constg", 16)])
    pl.emit("gpsimd", lambda e: e.dma_start(
        HMB[:, :].rearrange("h (c p) -> h c p", c=NCH),
        dhmask.rearrange("c h p -> h c p")), [("s_constg", 16)])
    pl.emit("gpsimd", lambda e: e.memset(ONESB[:, :], 1.0), [("s_mset", 1)])
    pl.emit("gpsimd", lambda e: e.memset(KMEAN[:, :], 0.0), [("s_mset", 1)])
    pl.emit("gpsimd", lambda e: e.memset(KMM[:, :, :], 0.0), [("s_mset", 1)])
    pl.emit("gpsimd", lambda e: e.memset(KVS[:, :], 0.0), [("s_mset", 1)])
    N_CONST = pl.val("s_const")

    for eng in ("tensor", "vector", "scalar"):
        pl.wait(eng, "s_const", N_CONST)
        pl.wait(eng, "s_constg", 32)
        pl.wait(eng, "s_mset", 4)
        pl.wait(eng, "s_x", 32)

    for c in range(NCH):
        pl.emit("scalar", lambda e, c=c: e.activation(
            SQT[0][:, 2 * c:2 * c + 2], HLO[:, 2 * c:2 * c + 2], AF.Copy))
        pl.emit("scalar", lambda e, c=c: e.activation(
            SQT[0][:, 16 + 2 * c:16 + 2 * c + 2], HLO[:, 2 * c:2 * c + 2], AF.Square),
            [("h_st", 1)] if c == NCH - 1 else [])
    pl.wait("tensor", "h_st")
    for c in range(NCH):
        pl.emit("tensor", lambda e, c=c: e.matmul(
            PSA[0:1, 0:2], ONESB[:, 0:1], SQT[0][:, 2 * c:2 * c + 2],
            start=(c == 0), stop=(c == NCH - 1)))
        pl.emit("tensor", lambda e, c=c: e.matmul(
            PSB[0:1, 0:2], ONESB[:, 0:1], SQT[0][:, 16 + 2 * c:16 + 2 * c + 2],
            start=(c == 0), stop=(c == NCH - 1)),
            [("h_mm", 1)] if c == NCH - 1 else [])
    pl.wait("vector", "h_mm")
    pl.emit("vector", lambda e: e.tensor_scalar_mul(MUR[:, 0:2], PSA[0:1, 0:2], 1.0 / C))
    pl.emit("vector", lambda e: e.tensor_scalar_mul(VARR[:, 0:2], PSB[0:1, 0:2], 1.0 / C))
    pl.emit("vector", lambda e: e.tensor_mul(RSR[:, 0:2], MUR[:, 0:2], MUR[:, 0:2]))
    pl.emit("vector", lambda e: e.scalar_tensor_tensor(
        VARR[:, 0:2], VARR[:, 0:2], float(EPS), RSR[:, 0:2], ALU.add, ALU.subtract),
        [("h_rw", 1)])
    pl.wait("scalar", "h_rw")
    pl.emit("scalar", lambda e: e.activation(VARR[:, 0:2], VARR[:, 0:2], AF.Sqrt))
    pl.emit("scalar", lambda e: e.activation(MURB[:, 0:2], MUR[:, 0:2], AF.Copy),
            [("h_sq", 1)])
    pl.wait("vector", "h_sq")
    pl.emit("vector", lambda e: e.reciprocal(RSR[:, 0:2], VARR[:, 0:2]), [("h_rc", 1)])
    pl.wait("scalar", "h_rc")
    pl.emit("scalar", lambda e: e.activation(RSRB[:, 0:2], RSR[:, 0:2], AF.Copy),
            [("h_sq2", 1)])
    pl.wait("tensor", "h_sq2")
    pl.emit("tensor", lambda e: e.matmul(PSC[:, 0:2], ONESB[0:1, :], MURB[0:1, 0:2],
                                         start=True, stop=True))
    pl.emit("tensor", lambda e: e.matmul(PSD[:, 0:2], ONESB[0:1, :], RSRB[0:1, 0:2],
                                         start=True, stop=True), [("h_bc", 1)])
    pl.wait("vector", "h_bc")
    for c in range(NCH):
        for k in range(2):
            col = 0 if k == 0 else T + 1
            pl.emit("vector", lambda e, c=c, k=k: e.tensor_sub(
                HSC[:, 4 * c + k:4 * c + k + 1], HLO[:, 2 * c + k:2 * c + k + 1],
                PSC[:, k:k + 1]))
            pl.emit("vector", lambda e, c=c, k=k, col=col: e.tensor_tensor(
                XLN[:, c, col:col + 1], HSC[:, 4 * c + k:4 * c + k + 1],
                PSD[:, k:k + 1], ALU.mult),
                [("h_done", 1)] if (c == NCH - 1 and k == 1) else [])
    lns_done("h_done")
    pl.wait("scalar", "h_mm")
    pl.wait("tensor", "h_rw")

    def emit_qk_chunk(j, t0, out_dve):
        wslot = r_wqk.begin_produce("gpsimd")
        pl.emit("gpsimd", lambda e, j=j, wslot=wslot: e.dma_start(
            WQK_SB[wslot][:, :, :],
            dwqkT[:, j * 128:(j + 1) * 128].rearrange("(c p) j -> p c j", p=128)),
            [r_wqk.end_produce()])
        wsl = r_wqk.begin_consume("tensor")
        pl.wait("tensor", "p1_xln")
        mslot = r_mm.begin_produce("tensor")
        for c in range(NCH):
            incs = []
            if c == NCH - 1:
                incs = [r_mm.end_produce(), r_wqk.end_consume()]
            pl.emit("tensor", lambda e, c=c, wsl=wsl, mslot=mslot, t0=t0: e.matmul(
                PSMM[mslot][:, 0:TT], WQK_SB[wsl][:, c, :],
                XLN[:, c, 1 + t0:1 + t0 + TT], start=(c == 0), stop=(c == NCH - 1)), incs)
        msl = r_mm.begin_consume("vector")
        ps = PSMM[msl]
        eslot = r_elu.begin_produce("vector")
        pl.emit("vector", lambda e, ps=ps, j=j, eslot=eslot: e.tensor_scalar(
            ELU[eslot][:, 0:TT], ps[:, 0:TT], BQK[:, j:j + 1], 0.0, ALU.add, ALU.min),
            [r_elu.end_produce()])
        el = r_elu.begin_consume("scalar")
        pl.emit("scalar", lambda e, el=el: e.activation(
            ELU[el][:, 0:TT], ELU[el][:, 0:TT], AF.Exp), [r_elu.end_consume()])
        pl.wait("vector", "r_elu_c")
        pl.emit("vector", lambda e, ps=ps, j=j: e.tensor_scalar(
            TN1[:, 0:TT], ps[:, 0:TT], BQK[:, j:j + 1], 0.0, ALU.add, ALU.max))
        out_dve(el, ps)

    for tt in range(NTT):
        t0 = tt * TT
        ln_stats(lambda c, t0=t0: X[:, c, t0:t0 + TT], TT)
        for c in range(NCH):
            pl.emit("vector", lambda e, c=c, t0=t0: e.tensor_sub(
                TN1[:, 0:TT], X[:, c, t0:t0 + TT], PSC[:, 0:TT]))
            pl.emit("vector", lambda e, c=c, t0=t0: e.tensor_tensor(
                XLN[:, c, 1 + t0:1 + t0 + TT], TN1[:, 0:TT], PSD[:, 0:TT], ALU.mult),
                [("p1_xln", 1)])
        lns_done("p1_xln")

        for jj in range(8):
            j = 8 + jj

            def k_out(el, ps, jj=jj, t0=t0):
                pl.emit("vector", lambda e, el=el: e.tensor_add(
                    KCH[:, 0:TT], TN1[:, 0:TT], ELU[el][:, 0:TT]), [r_mm.end_consume()])
                pl.emit("vector", lambda e: e.reduce_sum(KMT[:, :], KCH[:, 0:TT], axis=AX.X))
                pl.emit("vector", lambda e, jj=jj: e.tensor_add(
                    KMEAN[:, jj:jj + 1], KMEAN[:, jj:jj + 1], KMT[:, :]), [("p1_km", 1)])
                pl.emit("vector", lambda e: e.stream_shuffle(SWP[:, 0:TT], KCH[:, 0:TT],
                                                             PAIRSWAP))
                pl.emit("vector", lambda e, jj=jj: e.tensor_scalar_mul(
                    RT4[:, 0:TT], KCH[:, 0:TT], COS[:, jj:jj + 1]))
                pl.emit("vector", lambda e, jj=jj: e.scalar_tensor_tensor(
                    KRCH[:, 0:TT], SWP[:, 0:TT], SIN[:, jj:jj + 1], RT4[:, 0:TT],
                    ALU.mult, ALU.add), [("p1_kr", 1)])
                for s in range(4):
                    tsl = r_tp.begin_produce("tensor")
                    pl.wait("tensor", "p1_kr")
                    pl.emit("tensor", lambda e, s=s, tsl=tsl: e.transpose(
                        PST[:, tsl * 1024:tsl * 1024 + 128],
                        KRCH[:, s * 128:(s + 1) * 128], IDB[:, :]), [r_tp.end_produce()])
                    tc = r_tp.begin_consume("scalar")
                    pl.emit("scalar", lambda e, s=s, jj=jj, tc=tc: e.activation(
                        KRT[:, s, jj * 128:(jj + 1) * 128],
                        PST[:, tc * 1024:tc * 1024 + 128], AF.Copy),
                        [r_tp.end_consume()])
                    tsl = r_tp.begin_produce("tensor")
                    pl.emit("tensor", lambda e, s=s, jj=jj, t0=t0, tsl=tsl: e.transpose(
                        PST[:, tsl * 1024:tsl * 1024 + 128],
                        XLN[:, jj, 1 + t0 + s * 128:1 + t0 + (s + 1) * 128], IDB[:, :]),
                        [r_tp.end_produce()])
                    tc = r_tp.begin_consume("scalar")
                    pl.emit("scalar", lambda e, s=s, jj=jj, tc=tc: e.activation(
                        VT[:, s, jj * 128:(jj + 1) * 128],
                        PST[:, tc * 1024:tc * 1024 + 128], AF.Copy),
                        [r_tp.end_consume()])

            emit_qk_chunk(j, t0, k_out)

        pl.wait("tensor", "h_done")
        pl.wait("tensor", "r_tp_c")
        pl.wait("tensor", "p1_kva")
        for h in range(NH):
            for s in range(4):
                incs = [("p1_kv", 1)] if (s == 3 and h == NH - 1) else []
                pl.emit("tensor", lambda e, h=h, s=s: e.matmul(
                    PSE[0:64, h * 64:(h + 1) * 64],
                    KRT[:, s, h * 64:(h + 1) * 64], VT[:, s, h * 64:(h + 1) * 64],
                    start=(s == 0), stop=(s == 3)), incs)
        pl.wait("vector", "p1_kv")
        pl.wait("vector", "s_mset", 4)
        pl.emit("vector", lambda e: e.scalar_tensor_tensor(
            KVS[:, :], PSE[0:64, 0:1024], 1.0 / SEQ, KVS[:, :], ALU.mult, ALU.add),
            [("p1_kva", 1)])

    pl.wait("gpsimd", "p1_kva")
    pl.emit("gpsimd", lambda e: e.dma_start(d_ckx_i[:, 0:C], KVS[:, :]), [("s_ccst", 16)])
    pl.wait("gpsimd", "p1_km", 32)
    pl.emit("gpsimd", lambda e: e.dma_start(d_ckx_i[:, C:C + 8], KMEAN[0:64, :]),
            [("s_ccst", 16)])
    pl.emit("gpsimd", lambda e: e.dma_start(d_ckx_i[:, C + 8:C + 16], KMEAN[64:128, :]),
            [("s_ccst", 16)])
    pl.wait("gpsimd", "s_ccst")
    pl.emit("gpsimd", lambda e: e.collective_compute(
        "AllReduce", ALU.add, GROUPS, [d_ckx_i[:, :]], [d_ckx_o[:, :]]), [("s_cc", 1)])
    pl.wait("gpsimd", "s_cc")
    ckv_he = d_ckx_o[:, 0:C].rearrange("d (hh two e) -> d hh two e", two=2, e=64)
    pl.emit("gpsimd", lambda e: e.dma_start(
        KVGB[0:64, :].rearrange("p (hh e) -> p hh e", e=64), ckv_he[:, :, 0, :]),
        [("s_ccl", 16)])
    pl.emit("gpsimd", lambda e: e.dma_start(
        KVGB[64:128, :].rearrange("p (hh e) -> p hh e", e=64), ckv_he[:, :, 1, :]),
        [("s_ccl", 16)])
    pl.emit("gpsimd", lambda e: e.dma_start(KMG[0:64, :], d_ckx_o[:, C:C + 8]),
            [("s_ccl", 16)])
    pl.emit("gpsimd", lambda e: e.dma_start(KMG[64:128, :], d_ckx_o[:, C + 8:C + 16]),
            [("s_ccl", 16)])

    pl.wait("vector", "s_ccst", 16)
    for tt in range(NTT):
        t0 = tt * TT
        for c in range(NCH):
            pl.emit("vector", lambda e, c=c, t0=t0: e.tensor_scalar_mul(
                TN1[:, 0:TT], XLN[:, c, t0:t0 + TT], LEPW[:, 3 * c:3 * c + 1]))
            pl.emit("vector", lambda e, c=c, t0=t0: e.scalar_tensor_tensor(
                TN1[:, 0:TT], XLN[:, c, 1 + t0:1 + t0 + TT], LEPW[:, 3 * c + 1:3 * c + 2],
                TN1[:, 0:TT], ALU.mult, ALU.add))
            pl.emit("vector", lambda e, c=c, t0=t0: e.scalar_tensor_tensor(
                RES[:, c, t0:t0 + TT], XLN[:, c, 2 + t0:2 + t0 + TT],
                LEPW[:, 3 * c + 2:3 * c + 3], TN1[:, 0:TT], ALU.mult, ALU.add),
                [("p1_conv", 1)])

    pl.wait("scalar", "s_ccl", 64)
    for c in range(NCH):
        pl.emit("scalar", lambda e, c=c: e.activation(
            KMM[0:64, c, 2 * c:2 * c + 1], KMG[0:64, c:c + 1], AF.Copy, scale=CSTS[0:64, 0:1]))
        pl.emit("scalar", lambda e, c=c: e.activation(
            KMM[64:128, c, 2 * c + 1:2 * c + 2], KMG[64:128, c:c + 1], AF.Copy,
            scale=CSTS[64:128, 0:1]), [("p1_kmm", 1)] if c == NCH - 1 else [])

    for tt in range(NTT):
        t0 = tt * TT
        for j in range(8):
            def q_out(el, ps, j=j):
                pl.emit("vector", lambda e, el=el, j=j: e.tensor_add(
                    QTI[:, j, 0:TT], TN1[:, 0:TT], ELU[el][:, 0:TT]),
                    [("p2_q", 1)])
                pl.emit("vector", lambda e: e.engine_nop(), [r_mm.end_consume()])

            pl.wait("vector", "s_ccst", 16)
            emit_qk_chunk(j, t0, q_out)
        pl.wait("tensor", "p2_q")
        pl.wait("tensor", "p1_kmm")
        pl.wait("tensor", "p1_kva")
        pl.wait("tensor", "p2_zcons")
        for c in range(NCH):
            pl.emit("tensor", lambda e, c=c: e.matmul(
                PSE[0:NH, 0:TT], KMM[:, c, :], QTI[:, c, 0:TT],
                start=(c == 0), stop=(c == NCH - 1)),
                [("p2_zpe", 1)] if c == NCH - 1 else [])
        pl.wait("vector", "p2_zpe")
        pl.emit("vector", lambda e: e.tensor_scalar_add(ZROW[:, :], PSE[0:NH, 0:TT], 1e-6),
                [("p2_zcons", 1)])
        pl.emit("vector", lambda e: e.reciprocal(ZROW[:, :], ZROW[:, :]))
        pl.emit("vector", lambda e: e.tensor_copy(ZROWB[:, :], ZROW[:, :]), [("p2_z", 1)])
        pl.wait("tensor", "p2_z")
        pl.wait("tensor", "s_ccl", 64)
        pl.wait("vector", "p1_conv")
        for c in range(NCH):
            r_zb.begin_produce("tensor")
            pl.emit("tensor", lambda e, c=c: e.matmul(
                PSE[:, 512:512 + TT], HMB[:, c * 128:(c + 1) * 128], ZROWB[:, 0:TT],
                start=True, stop=True), [r_zb.end_produce()])
            pl.emit("vector", lambda e, c=c: e.stream_shuffle(
                SWP[:, 0:TT], QTI[:, c, 0:TT], PAIRSWAP))
            pl.emit("vector", lambda e, c=c: e.tensor_scalar_mul(
                RT4[:, 0:TT], QTI[:, c, 0:TT], COS[:, c:c + 1]))
            pl.emit("vector", lambda e, c=c: e.scalar_tensor_tensor(
                TN1[:, 0:TT], SWP[:, 0:TT], SIN[:, c:c + 1], RT4[:, 0:TT],
                ALU.mult, ALU.add))
            r_zb.begin_consume("vector")
            pl.emit("vector", lambda e, c=c: e.tensor_tensor(
                QRTI[:, c, 0:TT], TN1[:, 0:TT], PSE[:, 512:512 + TT], ALU.mult),
                [r_zb.end_consume()])
            pl.wait("tensor", "r_zb_c")
            mslot = r_mm.begin_produce("tensor")
            ps = PSMM[mslot]
            pl.emit("tensor", lambda e, c=c, ps=ps: e.matmul(
                ps[0:64, 0:TT], KVGB[0:64, c * 64:(c + 1) * 64],
                QRTI[0:64, c, 0:TT], start=True, stop=True))
            pl.emit("tensor", lambda e, c=c, ps=ps: e.matmul(
                ps[64:128, 0:TT], KVGB[64:128, c * 64:(c + 1) * 64],
                QRTI[64:128, c, 0:TT], start=True, stop=True), [r_mm.end_produce()])
            msl = r_mm.begin_consume("vector")
            pl.emit("vector", lambda e, c=c, t0=t0, msl=msl: e.scalar_tensor_tensor(
                RES[:, c, t0:t0 + TT], PSMM[msl][:, 0:TT], LEPB[:, c:c + 1],
                RES[:, c, t0:t0 + TT], ALU.add, ALU.add),
                [r_mm.end_consume()])
        pl.emit("vector", lambda e: e.engine_nop(), [("p2_res", 1)])
        ln_stats(lambda c, t0=t0: RES[:, c, t0:t0 + TT], TT, src_ready="p2_res")
        for c in range(NCH):
            pl.emit("vector", lambda e, c=c, t0=t0: e.tensor_sub(
                TN1[:, 0:TT], RES[:, c, t0:t0 + TT], PSC[:, 0:TT]))
            pl.emit("vector", lambda e: e.tensor_tensor(
                TN1[:, 0:TT], TN1[:, 0:TT], PSD[:, 0:TT], ALU.mult), [("p2_pn", 1)])
            pl.emit("vector", lambda e, c=c, t0=t0: e.tensor_add(
                X[:, c, t0:t0 + TT], X[:, c, t0:t0 + TT], TN1[:, 0:TT]), [("p2_x2", 1)])
        lns_done("p2_pn")
        pl.wait("sync", "p2_x2")
        pl.emit("sync", lambda e, t0=t0: e.dma_start(
            d_x2[:, t0:t0 + TT].rearrange("(c p) t -> p c t", p=128),
            X[:, :, t0:t0 + TT]), [("s_x2p", 16)])
        if BUILD_PHASE >= 2:
            ln_stats(lambda c, t0=t0: X[:, c, t0:t0 + TT], TT, src_ready="p2_x2")
            pl.wait("vector", "s_x2p")
            for c in range(NCH):
                pl.emit("vector", lambda e, c=c, t0=t0: e.tensor_sub(
                    TN1[:, 0:TT], X[:, c, t0:t0 + TT], PSC[:, 0:TT]))
                pl.emit("vector", lambda e, c=c, t0=t0: e.tensor_tensor(
                    X[:, c, t0:t0 + TT], TN1[:, 0:TT], PSD[:, 0:TT], ALU.mult),
                    [("p2_y0", 1)])
            lns_done("p2_y0")

    if BUILD_PHASE == 1:
        pl.wait("sync", "p2_x2")
        pl.emit("sync", lambda e: e.dma_start(
            dout.rearrange("(c p) t -> p c t", p=128), X[:, :, :]), [("s_out", 16)])
        pl.wait("sync", "s_out")


    if BUILD_PHASE >= 2:
        for tt in range(NTT):
            t0 = tt * TT
            ln_stats(lambda c, t0=t0: X[:, c, t0:t0 + TT], TT, src_ready="p2_y0")
            for c in range(NCH):
                pl.wait("vector", "p2_xn1")
                pl.emit("vector", lambda e, c=c, t0=t0: e.tensor_sub(
                    TN1[:, 0:TT], X[:, c, t0:t0 + TT], PSC[:, 0:TT]))
                pl.emit("vector", lambda e: e.tensor_tensor(
                    TN1[:, 0:TT], TN1[:, 0:TT], PSD[:, 0:TT], ALU.mult), [("p2_ln1", 1)])
                pl.wait("scalar", "p2_ln1")
                pl.emit("scalar", lambda e, c=c: e.activation(
                    XN1T[:, c, 0:TT], TN1[:, 0:TT], AF.Identity,
                    scale=LN1W[:, c:c + 1], bias=LN1B[:, c:c + 1]), [("p2_xn1", 1)])
            lns_done("p2_ln1")
            for jc in range(24):
                wslot = r_wqk.begin_produce("gpsimd")
                pl.emit("gpsimd", lambda e, jc=jc, wslot=wslot: e.dma_start(
                    WQK_SB[wslot][:, :, :],
                    dwqkv[:, jc * 128:(jc + 1) * 128].rearrange("(c p) j -> p c j", p=128)),
                    [r_wqk.end_produce()])
                wsl = r_wqk.begin_consume("tensor")
                pl.wait("tensor", "p2_xn1")
                mslot = r_mm.begin_produce("tensor")
                for c in range(NCH):
                    incs = []
                    if c == NCH - 1:
                        incs = [r_mm.end_produce(), r_wqk.end_consume()]
                    pl.emit("tensor", lambda e, c=c, wsl=wsl, mslot=mslot: e.matmul(
                        PSMM[mslot][:, 0:TT], WQK_SB[wsl][:, c, :],
                        XN1T[:, c, 0:TT], start=(c == 0), stop=(c == NCH - 1)), incs)
                msl = r_mm.begin_consume("scalar")
                ps = PSMM[msl]
                if jc < 8:
                    pl.emit("scalar", lambda e, jc=jc, t0=t0, ps=ps: e.activation(
                        QF[:, jc, t0:t0 + TT], ps[:, 0:TT], AF.Copy,
                        scale=CSTS[:, 2:3]), [r_mm.end_consume()])
                elif jc < 16:
                    pl.emit("scalar", lambda e, jc=jc, t0=t0, ps=ps: e.activation(
                        KL[:, jc - 8, t0:t0 + TT], ps[:, 0:TT], AF.Copy),
                        [r_mm.end_consume()])
                else:
                    vslot = r_vch.begin_produce("scalar")
                    pl.emit("scalar", lambda e, ps=ps, vslot=vslot: e.activation(
                        VCH[vslot][:, 0:TT], ps[:, 0:TT], AF.Copy),
                        [r_mm.end_consume()])
                    r_vch.np_ += 1
                    vch_ready = pl.val("r_mm_c")
                    vsl = r_vch.nc_ % 2
                    r_vch.nc_ += 1
                    pl.wait("tensor", "r_mm_c", vch_ready)
                    h0 = 2 * (jc - 16)
                    for s in range(4):
                        ts_ = tt * 4 + s
                        tsl = r_tp.begin_produce("tensor")
                        incs = [r_tp.end_produce()]
                        if s == 3:
                            incs.append(("r_vch_c", 1))
                        pl.emit("tensor", lambda e, s=s, vsl=vsl, tsl=tsl: e.transpose(
                            PST[:, tsl * 1024:tsl * 1024 + 128],
                            VCH[vsl][:, s * 128:(s + 1) * 128], IDB[:, :]), incs)
                        tc = r_tp.begin_consume("scalar")
                        pl.emit("scalar", lambda e, h0=h0, ts_=ts_, tc=tc: e.activation(
                            VL[:, h0, ts_, 0:64],
                            PST[:, tc * 1024:tc * 1024 + 64], AF.Copy))
                        pl.emit("scalar", lambda e, h0=h0, ts_=ts_, tc=tc: e.activation(
                            VL[:, h0 + 1, ts_, 0:64],
                            PST[:, tc * 1024 + 64:tc * 1024 + 128], AF.Copy),
                            [r_tp.end_consume()])
        QF_DONE = pl.val("r_mm_c")
        VL_TPC = pl.val("r_tp_c")

        pl.wait("gpsimd", "p2_pn")
        pl.wait("gpsimd", "r_tp_c", VL_TPC)
        for h in range(NH):
            pl.emit("gpsimd", lambda e, h=h: e.memset(VL[:, h, :, 64:65], 1.0),
                    [("s_mset", 1)])
        N_MSET_VL = pl.val("s_mset")

        pl.wait("sync", "r_mm_c", QF_DONE)
        pl.emit("sync", lambda e: e.dma_start(
            d_kg_i.rearrange("(c p) t -> p c t", p=128), KL[:, :, :]), [("s_kg", 16)])
        pl.wait("sync", "r_tp_c", VL_TPC)
        pl.wait("sync", "s_mset", N_MSET_VL)
        pl.emit("sync", lambda e: e.dma_start(
            d_vg_i.rearrange("(h ts p) e -> p h ts e", p=128, ts=16), VL[:, :, :, :]),
            [("s_vg", 16)])
        pl.wait("gpsimd", "s_vg")
        pl.emit("gpsimd", lambda e: e.memset(KAUG[0][64:65, :], 1.0), [("s_mset", 1)])
        pl.emit("gpsimd", lambda e: e.memset(KAUG[1][64:65, :], 1.0), [("s_mset", 1)])
        pl.emit("gpsimd", lambda e: e.memset(QAUG[0][64:65, :], 0.0), [("s_mset", 1)])
        pl.emit("gpsimd", lambda e: e.memset(QAUG[1][64:65, :], 0.0), [("s_mset", 1)])
        N_MSET2 = pl.val("s_mset")
        pl.wait("gpsimd", "s_kg")
        pl.wait("gpsimd", "s_vg")
        for half in range(2):
            pl.emit("gpsimd", lambda e, half=half: e.collective_compute(
                "AllGather", ALU.bypass, GROUPS,
                [d_kg_i[half * (C // 2):(half + 1) * (C // 2), :]],
                [d_kg_oh[half][:, :]]), [("s_cc2", 1)])
            pl.emit("gpsimd", lambda e, half=half: e.collective_compute(
                "AllGather", ALU.bypass, GROUPS,
                [d_vg_i[half * (NH // 2) * T:(half + 1) * (NH // 2) * T, :]],
                [d_vg_oh[half][:, :]]), [("s_cc2", 1)])

        st_mark = {}
        av_mark = {}
        hd_mark = {}

        def emit_load(h):
            buf = h % 2
            cQ = h // 2
            rb = (h % 2) * 64
            half = h // 8
            hh = h % 8
            pl.wait("sync", "s_cc2", 2 if h < 8 else 4)
            if h >= 2:
                pl.wait("sync", "r_mm2_p", st_mark[h - 2])
                pl.wait("sync", "b_av", av_mark[h - 2])
            for r in range(4):
                pl.emit("sync", lambda e, hh=hh, r=r, buf=buf, half=half: e.dma_start(
                    KAUG[buf][0:64, r * T:(r + 1) * T],
                    d_kg_oh[half][r * (C // 2) + 64 * hh: r * (C // 2) + 64 * hh + 64, :]),
                    [("s_hd", 16)])
            for r in range(4):
                pl.emit("sync", lambda e, hh=hh, r=r, buf=buf, half=half: e.dma_start(
                    VH[buf][:, r * 16:(r + 1) * 16, :],
                    d_vg_oh[half][(r * 8 + hh) * T:(r * 8 + hh + 1) * T, :]
                    .rearrange("(j p) e -> p j e", p=128)), [("s_hd", 16)])
            hd_mark[h] = pl.val("s_hd")
            pl.wait("vector", "r_mm_c", QF_DONE)
            if h >= 2:
                pl.wait("vector", "r_mm2_p", st_mark[h - 2])
            pl.emit("vector", lambda e, buf=buf, cQ=cQ, rb=rb: e.tensor_copy(
                QAUG[buf][0:64, :], QF[rb:rb + 64, cQ, :]), [("a_qb", 1)])

        def emit_B(h):
            buf = h % 2
            cQ = h // 2
            rb = (h % 2) * 64
            pl.wait("tensor", "a_qb", h + 1)
            pl.wait("tensor", "s_mset", N_MSET2)
            pl.wait("tensor", "s_hd", hd_mark[h])
            for qt4 in range(4):
                pl.wait("tensor", "b_fin")
                pend = None
                for kc in range(65):
                    if kc < 64:
                        m2 = r_mm2.begin_produce("tensor")
                        pl.emit("tensor", lambda e, buf=buf, qt4=qt4, kc=kc, m2=m2: e.matmul(
                            PSMM2[m2][:, 0:TT], KAUG[buf][:, kc * 128:(kc + 1) * 128],
                            QAUG[buf][:, qt4 * TT:(qt4 + 1) * TT],
                            start=True, stop=True), [r_mm2.end_produce()])
                        m2c = r_mm2.begin_consume("scalar")
                        pl.wait("scalar", "b_av", pl.val("b_av"))
                        pslot = pl.val("r_mm2_c") % 2
                        pl.emit("scalar", lambda e, m2c=m2c, pslot=pslot: e.activation(
                            PT[pslot][:, 0:TT], PSMM2[m2c][:, 0:TT], AF.Exp),
                            [r_mm2.end_consume()])
                        nxt = (kc, pslot, pl.val("r_mm2_c"))
                    if pend is not None:
                        kcp, pslotp, expv = pend
                        pl.wait("tensor", "r_mm2_c", expv)
                        pl.emit("tensor", lambda e, buf=buf, kcp=kcp, pslotp=pslotp: e.matmul(
                            PSE[0:65, 0:TT], VH[buf][:, kcp, :], PT[pslotp][:, 0:TT],
                            start=(kcp == 0), stop=(kcp == 63)), [("b_av", 1)])
                    if kc < 64:
                        pend = nxt
                pl.wait("vector", "b_av")
                pl.emit("vector", lambda e: e.reciprocal(RL[:, :], PSE[64:65, 0:TT]),
                        [("b_rl", 1)])
                pl.wait("scalar", "b_rl")
                pl.emit("scalar", lambda e: e.activation(RLB[:, :], RL[:, :], AF.Copy),
                        [("b_rlb", 1)])
                pl.emit("scalar", lambda e: e.activation(OT16[:, :], PSE[0:64, 0:TT],
                                                         AF.Copy), [("b_ot", 1)])
                pl.wait("tensor", "b_rlb")
                pl.emit("tensor", lambda e: e.matmul(
                    PSE[0:64, 512:1024], ONESB[0:1, 0:64], RLB[0:1, :],
                    start=True, stop=True), [("b_lb", 1)])
                pl.wait("vector", "b_lb")
                pl.wait("vector", "b_ot")
                pl.emit("vector", lambda e, rb=rb, cQ=cQ, qt4=qt4: e.tensor_tensor(
                    AOUT[rb:rb + 64, cQ, qt4 * TT:(qt4 + 1) * TT], OT16[:, :],
                    PSE[0:64, 512:1024], ALU.mult), [("b_fin", 1)])
            st_mark[h] = pl.val("r_mm2_p")
            av_mark[h] = pl.val("b_av")

        pl.wait("tensor", "p2_ln1")

        for h in range(NH):
            emit_load(h)
            if h >= 1:
                emit_B(h - 1)
        emit_B(NH - 1)

        AOUT_DONE = pl.val("b_fin")
        pl.wait("gpsimd", "b_av", av_mark[NH - 1])
        for tt in range(NTT):
            t0 = tt * TT
            for oc in range(NCH):
                wslot = r_wqk.begin_produce("gpsimd")
                pl.emit("gpsimd", lambda e, oc=oc, wslot=wslot: e.dma_start(
                    WQK_SB[wslot][:, :, :],
                    dwproj[:, oc * 128:(oc + 1) * 128].rearrange("(c p) j -> p c j", p=128)),
                    [r_wqk.end_produce()])
                wsl = r_wqk.begin_consume("tensor")
                pl.wait("tensor", "b_fin", AOUT_DONE)
                mslot = r_mm.begin_produce("tensor")
                for c in range(NCH):
                    incs = []
                    if c == NCH - 1:
                        incs = [r_mm.end_produce(), r_wqk.end_consume()]
                    pl.emit("tensor", lambda e, c=c, wsl=wsl, mslot=mslot, t0=t0: e.matmul(
                        PSMM[mslot][:, 0:TT], WQK_SB[wsl][:, c, :],
                        AOUT[:, c, t0:t0 + TT], start=(c == 0), stop=(c == NCH - 1)), incs)
                msl = r_mm.begin_consume("vector")
                pl.emit("vector", lambda e, oc=oc, t0=t0, msl=msl: e.tensor_add(
                    Y1T[:, oc, t0:t0 + TT], Y1T[:, oc, t0:t0 + TT], PSMM[msl][:, 0:TT]),
                    [r_mm.end_consume()])
        Y1_DONE = pl.val("r_mm_c")

    if BUILD_PHASE == 2:
        pl.wait("sync", "r_mm_c", Y1_DONE)
        pl.emit("sync", lambda e: e.dma_start(
            dout.rearrange("(c p) t -> p c t", p=128), Y1T[:, :, :]), [("s_out", 16)])
        pl.wait("sync", "s_out")


    if BUILD_PHASE >= 3:
        for tt in range(NTT):
            t0 = tt * TT
            pl.wait("tensor", "p3_msk")
            ln_stats(lambda c, t0=t0: Y1T[:, c, t0:t0 + TT], TT, src_ready="r_mm_c")
            for c in range(NCH):
                pl.wait("vector", "p3_xnf")
                pl.emit("vector", lambda e, c=c, t0=t0: e.tensor_sub(
                    TN1[:, 0:TT], Y1T[:, c, t0:t0 + TT], PSC[:, 0:TT]))
                pl.emit("vector", lambda e: e.tensor_tensor(
                    TN1[:, 0:TT], TN1[:, 0:TT], PSD[:, 0:TT], ALU.mult), [("p3_ln2", 1)])
                pl.wait("scalar", "p3_ln2")
                pl.wait("scalar", "p3_gmm")
                pl.emit("scalar", lambda e, c=c: e.activation(
                    XNF[:, 0:TT], TN1[:, 0:TT], AF.Identity,
                    scale=LN2W[:, c:c + 1], bias=LN2B[:, c:c + 1]), [("p3_xnf", 1)])
                pl.wait("scalar", "s_xnp")
                pl.emit("scalar", lambda e: e.activation(
                    XNB[:, 0:TT], XNF[:, 0:TT], AF.Copy), [("p3_xnb", 1)])
                pl.wait("sync", "p3_xnb")
                pl.emit("sync", lambda e, c=c, t0=t0: e.dma_start(
                    d_xn2[c * 128:(c + 1) * 128, t0:t0 + TT], XNB[:, 0:TT]),
                    [("s_xnp", 16)])
                pl.wait("tensor", "p3_xnf")
                pl.wait("tensor", "p3_lg")
                pl.emit("tensor", lambda e, c=c: e.matmul(
                    PSE[0:NE, 0:TT], GATEW[:, c, :], XNF[:, 0:TT],
                    start=(c == 0), stop=(c == NCH - 1)), [("p3_gmm", 1)])
            lns_done("p3_ln2")
            pl.wait("scalar", "p3_gmm")
            pl.emit("scalar", lambda e: e.activation(
                LG[:, 0:TT], PSE[0:NE, 0:TT], AF.Identity, bias=GATEB[:, 0:1]),
                [("p3_lg", 1)])
            for s in range(4):
                ts_ = tt * 4 + s
                pl.wait("tensor", "p3_lg")
                pl.wait("tensor", "p3_msk")
                pl.emit("tensor", lambda e, s=s: e.transpose(
                    PSB[:, 0:8], LG[0:8, s * 128:(s + 1) * 128], IDF[0:8, 0:8]),
                    [("p3_gt", 1)])
                pl.wait("vector", "p3_gt")
                pl.emit("vector", lambda e: e.reduce_max(MX[:, :], PSB[:, 0:8], axis=AX.X))
                pl.emit("vector", lambda e, ts_=ts_: e.tensor_scalar(
                    MASKTM[:, ts_, :], PSB[:, 0:8], MX[:, 0:1], None, ALU.is_ge),
                    [("p3_msk", 1)])
            pl.wait("scalar", "b_fin", AOUT_DONE)
            for c in range(NCH):
                pl.emit("scalar", lambda e, c=c, t0=t0: e.activation(
                    YACC[:, c, t0:t0 + TT], Y1T[:, c, t0:t0 + TT], AF.Copy),
                    [("p3_yi", 1)])

        m1t_idx = [0]
        m2t_idx = [0]
        for e_ in range(NE):
            if e_ == 0:
                pl.wait("gpsimd", "p3_yi", 32)
            pl.wait("gpsimd", "p3_m1d", e_)
            pl.emit("gpsimd", lambda e, e_=e_: e.dma_start(
                W1SB[:, :, :], dw1[e_, :, :].rearrange("(c p) j -> p c j", p=128)),
                [("s_w1", 16)])
            if e_ == 0:
                pl.wait("gpsimd", "r_mm_c", Y1_DONE)
            pl.wait("gpsimd", "p3_m2d", e_)
            pl.emit("gpsimd", lambda e, e_=e_: e.dma_start(
                W2SB[:, :, :], dw2[e_, :, :].rearrange("(j p) o -> p j o", p=128)),
                [("s_w2", 16)])
            for mt in range(NMTT):
                blk = e_ * NMTT + mt
                xsl = blk % 2
                if blk > 1:
                    pl.wait("sync", "p3_m1t", blk - 1)
                pl.wait("sync", "s_xnp")
                pl.emit("sync", lambda e, mt=mt, xsl=xsl: e.dma_start(
                    XNT[xsl][:, :, :],
                    d_xn2[:, mt * MTT:(mt + 1) * MTT].rearrange("(c p) t -> p c t", p=128)),
                    [("s_xnl", 16)])
                XNT_RDY = pl.val("s_xnl")
                pl.wait("tensor", "s_w1", 16 * (e_ + 1))
                pl.wait("tensor", "s_xnl", XNT_RDY)
                pl.wait("scalar", "p3_m2t", blk)
                for j in range(32):
                    mslot = r_mm.begin_produce("tensor")
                    for c in range(NCH):
                        incs = []
                        if c == NCH - 1:
                            incs = [r_mm.end_produce()]
                            if j == 31:
                                incs.append(("p3_m1t", 1))
                                if mt == NMTT - 1:
                                    incs.append(("p3_m1d", 1))
                        pl.emit("tensor", lambda e, c=c, j=j, mslot=mslot, xsl=xsl: e.matmul(
                            PSMM[mslot][:, 0:MTT], W1SB[:, c, j * 128:(j + 1) * 128],
                            XNT[xsl][:, c, 0:MTT], start=(c == 0), stop=(c == NCH - 1)), incs)
                    msl = r_mm.begin_consume("scalar")
                    pl.emit("scalar", lambda e, j=j, msl=msl: e.activation(
                        HSB[:, j, 0:MTT], PSMM[msl][:, 0:MTT], AF.Gelu_apprx_tanh),
                        [r_mm.end_consume()])
                HSB_RDY = pl.val("r_mm_c")
                pl.wait("tensor", "p3_msk", 4 * NTT)
                for half in range(2):
                    ts_ = mt * 2 + half
                    pl.wait("tensor", "p3_mr")
                    pl.emit("tensor", lambda e, ts_=ts_, e_=e_: e.matmul(
                        PSE[0:1, 512:640], MASKTM[:, ts_, e_:e_ + 1], IDF[:, :],
                        start=True, stop=True), [("p3_mt", 1)])
                    pl.wait("scalar", "p3_mt")
                    pl.emit("scalar", lambda e, half=half: e.activation(
                        MROWE[0:1, half * 128:(half + 1) * 128], PSE[0:1, 512:640],
                        AF.Copy), [("p3_mr", 1)])
                pl.wait("tensor", "p3_mr")
                pl.wait("tensor", "p3_mbs")
                pl.emit("tensor", lambda e: e.matmul(
                    PSE[:, 0:MTT], ONESB[0:1, :], MROWE[0:1, 0:MTT],
                    start=True, stop=True), [("p3_mb", 1)])
                pl.wait("scalar", "p3_mb")
                pl.wait("scalar", "p3_moe")
                pl.emit("scalar", lambda e: e.activation(
                    MBSB[:, 0:MTT], PSE[:, 0:MTT], AF.Copy), [("p3_mbs", 1)])
                pl.wait("tensor", "s_w2", 16 * (e_ + 1))
                pl.wait("tensor", "r_mm_c", HSB_RDY)
                for o in range(NCH):
                    m2 = r_mm2.begin_produce("tensor")
                    for j in range(32):
                        incs = []
                        if j == 31:
                            incs = [r_mm2.end_produce()]
                            if o == NCH - 1:
                                incs.append(("p3_m2t", 1))
                                if mt == NMTT - 1:
                                    incs.append(("p3_m2d", 1))
                        pl.emit("tensor", lambda e, o=o, j=j, m2=m2: e.matmul(
                            PSMM2[m2][:, 0:MTT], W2SB[:, j, o * 128:(o + 1) * 128],
                            HSB[:, j, 0:MTT], start=(j == 0), stop=(j == 31)), incs)
                    m2c = r_mm2.begin_consume("vector")
                    pl.wait("vector", "p3_mbs")
                    pl.emit("vector", lambda e, m2c=m2c: e.tensor_tensor(
                        T1M[:, 0:MTT], PSMM2[m2c][:, 0:MTT], MBSB[:, 0:MTT], ALU.mult),
                        [r_mm2.end_consume()])
                    pl.emit("vector", lambda e, o=o, mt=mt: e.tensor_add(
                        YACC[:, o, mt * MTT:(mt + 1) * MTT],
                        YACC[:, o, mt * MTT:(mt + 1) * MTT], T1M[:, 0:MTT]),
                        [("p3_moe", 1)])

        for tt in range(NTT):
            t0 = tt * TT
            pl.wait("sync", "p3_m1d", NE)
            pl.wait("sync", "p3_out")
            pl.emit("sync", lambda e, t0=t0: e.dma_start(
                X2T[:, :, :],
                d_x2[:, t0:t0 + TT].rearrange("(c p) t -> p c t", p=128)),
                [("s_x2r", 16)])
            ln_stats(lambda c, t0=t0: YACC[:, c, t0:t0 + TT], TT, src_f32=False,
                     src_ready="p3_moe")
            pl.wait("vector", "s_x2r")
            pl.wait("vector", "s_out")
            for c in range(NCH):
                pl.emit("vector", lambda e, c=c, t0=t0: e.tensor_sub(
                    TN1[:, 0:TT], YACC[:, c, t0:t0 + TT], PSC[:, 0:TT]))
                incs = [("p3_fn", 1)] if c == NCH - 1 else []
                pl.emit("vector", lambda e: e.tensor_tensor(
                    TN1[:, 0:TT], TN1[:, 0:TT], PSD[:, 0:TT], ALU.mult), incs)
                pl.emit("vector", lambda e, c=c: e.tensor_add(
                    OUTT[:, c, 0:TT], X2T[:, c, 0:TT], TN1[:, 0:TT]), [("p3_out", 1)])
            lns_done("p3_fn")
            pl.wait("sync", "p3_out")
            pl.emit("sync", lambda e, t0=t0: e.dma_start(
                dout[:, t0:t0 + TT].rearrange("(c p) t -> p c t", p=128),
                OUTT[:, :, :]), [("s_out", 16)])
        pl.wait("sync", "s_out")

    emit(nc, pl)
    return nc


def emit(nc, pl):
    with ExitStack() as st:
        sems = {n: st.enter_context(nc.semaphore(n)) for n in pl.cnt}
        nums = sorted(s.num for s in sems.values())
        assert nums[-1] - nums[0] + 1 == len(nums), "sem numbers not contiguous"
        clear_range = range(nums[0], nums[-1] + 1)
        with nc.Block() as block0:
            @block0.gpsimd
            def _(g):
                g.dma_reset(clear_range)
                g.sem_clear(clear_range)
        nc.all_engine_barrier()
        with nc.Block() as block:
            for eng_name, deco in (("tensor", block.tensor), ("vector", block.vector),
                                   ("scalar", block.scalar), ("gpsimd", block.gpsimd),
                                   ("sync", block.sync)):
                def runner(e, eng_name=eng_name):
                    for item in pl.ops[eng_name]:
                        if item[0] == "wait":
                            e.wait_ge(sems[item[1]], item[2])
                        else:
                            ins = item[1](e)
                            incs = item[2]
                            if eng_name != "tensor":
                                assert len(incs) <= 1, (eng_name, incs)
                            if len(incs) > 1:
                                ins.then_inc(sems[incs[0][0]], incs[0][1])
                                for s, a in incs[1:]:
                                    e.nop().then_inc(sems[s], a)
                            else:
                                for s, a in incs:
                                    ins.then_inc(sems[s], a)
                deco(runner)



def _stage(inputs):
    x = np.asarray(inputs["L0_attn_x"])
    W_qk = np.asarray(inputs["L0_attn_W_qk"])
    b_qk = np.asarray(inputs["L0_attn_b_qk"])
    lepe_w = np.asarray(inputs["L0_attn_lepe_w"])
    lepe_b = np.asarray(inputs["L0_attn_lepe_b"])
    ln1w = np.asarray(inputs["L0_ffn_ln1_w"]); ln1b = np.asarray(inputs["L0_ffn_ln1_b"])
    ln2w = np.asarray(inputs["L0_ffn_ln2_w"]); ln2b = np.asarray(inputs["L0_ffn_ln2_b"])
    wqkv = np.asarray(inputs["L0_ffn_w_qkv"]); wproj = np.asarray(inputs["L0_ffn_w_proj"])
    gatew = np.asarray(inputs["L0_ffn_gate_w"]); gateb = np.asarray(inputs["L0_ffn_gate_b"])
    w1 = np.asarray(inputs["L0_ffn_w1"]); w2 = np.asarray(inputs["L0_ffn_w2"])

    k_max = C // 2
    theta = 1.0 / (10000.0 ** (np.arange(k_max, dtype=np.float64) / k_max))
    cosv = np.repeat(np.cos(theta), 2).astype(np.float32)
    sinv = np.zeros(C, dtype=np.float32)
    sinv[0::2] = -np.sin(theta)
    sinv[1::2] = np.sin(theta)

    def colview(v):
        return np.ascontiguousarray(np.asarray(v, dtype=np.float32).reshape(NCH, 128).T)

    hmask = np.zeros((NCH, NH, 128), dtype=np.float32)
    for c in range(NCH):
        hmask[c, 2 * c, 0:64] = 1.0
        hmask[c, 2 * c + 1, 64:128] = 1.0

    shared = {
        "wqkT": np.ascontiguousarray(W_qk.T),
        "bqk": np.ascontiguousarray(b_qk.reshape(16, 128).T),
        "ropecos": colview(cosv), "ropesin": colview(sinv),
        "lepew": np.ascontiguousarray(
            lepe_w[:, 0, :].reshape(NCH, 128, 3).transpose(1, 0, 2).reshape(128, NCH * 3)),
        "lepeb": colview(lepe_b),
        "ln1w": colview(ln1w), "ln1b": colview(ln1b),
        "ln2w": colview(ln2w), "ln2b": colview(ln2b),
        "wqkv": np.ascontiguousarray(wqkv), "wproj": np.ascontiguousarray(wproj),
        "gatew": np.ascontiguousarray(gatew),
        "gateb": np.ascontiguousarray(np.asarray(gateb).reshape(NE, 1)),
        "w1": np.ascontiguousarray(w1), "w2": np.ascontiguousarray(w2),
        "ident": np.eye(128, dtype=np.float32),
        "csts": np.broadcast_to(np.array(
            [1.0 / SEQ, -1.0, 0.125, 0.5, EPS, 0.0, 0.0, 0.0], dtype=np.float32),
            (128, 8)).copy(),
        "hmask": hmask,
    }
    in_maps = []
    for core in range(NCORES):
        b = core // 4
        t0 = (core % 4) * T
        m = dict(shared)
        m["x_fm"] = np.ascontiguousarray(x[b, t0:t0 + T].T)
        halo = np.zeros((C, 2), dtype=np.float32)
        if t0 > 0:
            halo[:, 0] = x[b, t0 - 1]
        if t0 + T < SEQ:
            halo[:, 1] = x[b, t0 + T]
        m["x_halo"] = halo
        in_maps.append(m)
    return in_maps


def kernel(**inputs):
    in_maps = _stage(inputs)
    nc = build_program()
    res = run_bass_kernel_spmd(nc, in_maps, core_ids=list(range(NCORES)))
    out = np.empty((NB, SEQ, C), dtype=np.float32)
    for core in range(NCORES):
        b = core // 4
        t0 = (core % 4) * T
        out[b, t0:t0 + T] = res.results[core]["out_fm"].T
    return out
